# version 04
# baseline (speedup 1.0000x reference)
import numpy as np
import concourse.bass as bass
import concourse.mybir as mybir
from concourse.bass_utils import run_bass_kernel_spmd



F32 = mybir.dt.float32
BF16 = mybir.dt.bfloat16
AF = mybir.ActivationFunctionType
OP = mybir.AluOpType
AX = mybir.AxisListType

QUEUES = ("pe", "dve", "act", "pool", "sp")
DMA_BASE = {"sp": "dsp", "pool": "dpool", "act": "dact"}
R_SLOTS = 12


class _Op:
    __slots__ = ("fn", "queue", "domain", "waits", "ms", "seq", "snapshot",
                 "milestoned", "label")

    def __init__(self, fn, queue, domain, seq, label=""):
        self.fn = fn
        self.queue = queue
        self.domain = domain
        self.waits = []
        self.ms = None
        self.seq = seq
        self.snapshot = None
        self.milestoned = False
        self.label = label


class Buf:
    __slots__ = ("key", "writer", "readers")
    _n = 0

    def __init__(self, key=None):
        if key is None:
            Buf._n += 1
            key = f"_b{Buf._n}"
        self.key = key
        self.writer = None
        self.readers = {}


class Sched:
    def __init__(self, nc):
        self.nc = nc
        self._ctx = []
        self.sems = {}
        self.dma_domains = []
        names = ["pe", "dve", "act", "pool"]
        for q, base in DMA_BASE.items():
            for s in range(R_SLOTS):
                names.append(f"{base}{s}")
                self.dma_domains.append(f"{base}{s}")
        for d in names:
            cm = nc.semaphore("sem_" + d)
            self._ctx.append(cm)
            self.sems[d] = cm.__enter__()
        self.dma_seq = {q: 0 for q in DMA_BASE}
        self.last_dma = {}
        self.ops = {q: [] for q in QUEUES}
        self.count = {d: 0 for d in names}
        self.drained_until = {q: 0 for q in QUEUES}
        self._pending_drain = {q: False for q in QUEUES}
        self.n_drains = 0
        self.drain_sites = {}
        self._drain_blame = (None, None)
        self.know = {q: {} for q in QUEUES}
        self.snap = {}
        self.n_waits = 0
        self.is_dma_domain = {d: (d not in QUEUES) for d in names}
        for q in QUEUES:
            self.is_dma_domain.setdefault(q, False)

    def _need(self, target_q, dep_op, pending, raw=True):
        if dep_op.domain == target_q:
            if raw and target_q != "pe" and                     dep_op.seq >= self.drained_until[target_q]:
                self._pending_drain[target_q] = True
                self._drain_blame = (dep_op.label, target_q)
            return
        if self.know[target_q].get(dep_op.domain, -1) >= dep_op.seq:
            return
        dep_op.milestoned = True
        k = dict(self.know[target_q])
        k[dep_op.domain] = dep_op.seq
        sn = dep_op.snapshot
        if sn:
            for d, v in sn.items():
                if v > k.get(d, -1):
                    k[d] = v
        self.know[target_q] = k
        pending.append(dep_op)

    def op(self, queue, fn, reads=(), writes=(), label=""):
        return self._op(queue, queue, fn, reads, writes, label)

    def dma(self, queue, fn, reads=(), writes=(), label=""):
        i = self.dma_seq[queue]
        self.dma_seq[queue] = i + 1
        domain = f"{DMA_BASE[queue]}{i % R_SLOTS}"
        return self._op(queue, domain, fn, reads, writes, label)

    def _op(self, queue, domain, fn, reads, writes, label):
        self._pending_drain[queue] = False
        op = _Op(fn, queue, domain, len(self.ops[queue]), label)
        pending = []
        for b in reads:
            if b.writer is not None:
                self._need(queue, b.writer, pending, raw=True)
        for b in writes:
            if b.writer is not None:
                self._need(queue, b.writer, pending, raw=False)
            for r in b.readers.values():
                self._need(queue, r, pending, raw=False)
        if self._pending_drain[queue]:
            key = (queue, label, self._drain_blame[0])
            self.drain_sites[key] = self.drain_sites.get(key, 0) + 1
            dr = _Op(lambda e: e.drain(), queue, queue, len(self.ops[queue]),
                     "drain")
            dr.snapshot = self.know[queue]
            self.ops[queue].append(dr)
            self.drained_until[queue] = dr.seq
            self.n_drains += 1
            op.seq = len(self.ops[queue])
        if domain != queue:
            prev = self.last_dma.get(domain)
            if prev is not None:
                self._need(queue, prev, pending)
            self.last_dma[domain] = op
            op.milestoned = True
        wmax = {}
        for dep in pending:
            cur = wmax.get(dep.domain)
            if cur is None or dep.seq > cur.seq:
                wmax[dep.domain] = dep
        op.waits = sorted(wmax.values(), key=lambda o: o.domain)
        self.n_waits += len(op.waits)
        op.snapshot = self.know[queue]
        self.ops[queue].append(op)
        for b in writes:
            b.writer = op
            b.readers = {}
        for b in reads:
            b.readers[op.domain] = op
        return op

    def finalize(self):
        nc = self.nc
        for q in QUEUES:
            for op in self.ops[q]:
                if op.milestoned:
                    self.count[op.domain] += 1
                    op.ms = self.count[op.domain]

        def run_queue(qname):
            def body(eng):
                for op in self.ops[qname]:
                    for dep in op.waits:
                        amt = 16 if self.is_dma_domain[dep.domain] else 1
                        assert dep.ms is not None, dep.label
                        eng.wait_ge(self.sems[dep.domain], dep.ms * amt)
                    if op.fn is None:
                        continue
                    inst = op.fn(eng)
                    if op.ms is not None:
                        amt = 16 if self.is_dma_domain[op.domain] else 1
                        inst.then_inc(self.sems[op.domain], amt)
                if qname == "sp":
                    for d in self.dma_domains:
                        if self.count[d]:
                            eng.wait_ge(self.sems[d], 16 * self.count[d])
            return body

        with nc.Block() as block:
            block.tensor(run_queue("pe"))
            block.vector(run_queue("dve"))
            block.scalar(run_queue("act"))
            block.gpsimd(run_queue("pool"))
            block.sync(run_queue("sp"))

    def close(self):
        for cm in reversed(self._ctx):
            cm.__exit__(None, None, None)

    def stats(self):
        d = {q: len(self.ops[q]) for q in QUEUES}
        d["milestones"] = sum(self.count.values())
        d["waits"] = self.n_waits
        return d


class Tile:
    __slots__ = ("ap", "buf", "shape", "dtype")

    def __init__(self, ap, buf, shape, dtype):
        self.ap = ap
        self.buf = buf
        self.shape = shape
        self.dtype = dtype

    def __getitem__(self, idx):
        return self.ap[idx]

    @property
    def full(self):
        return self.ap[:]


class Arena:
    def __init__(self, nc, name, bytes_per_partition):
        self.nbytes = bytes_per_partition
        assert bytes_per_partition % 4 == 0
        self._cm = nc.sbuf_tensor(name, [128, bytes_per_partition // 4], F32)
        self.t = self._cm.__enter__()
        self.off = 0
        self.name = name
        self.carves = []
        self.reset_deps = []
        self.hiwater = 0

    def carve(self, shape, dtype=F32, name=""):
        assert shape[0] == 128
        esz = 2 if dtype == BF16 else 4
        n = 1
        for s in shape[1:]:
            n *= s
        nbytes = ((n * esz + 63) // 64) * 64
        assert self.off + nbytes <= self.nbytes, \
            f"arena {self.name} overflow: {self.off}+{nbytes} > {self.nbytes} ({name})"
        off_f32 = self.off // 4
        ap = self.t[:, off_f32: off_f32 + (n * esz + 3) // 4]
        if dtype != F32:
            ap = ap.bitcast(dtype)
            ap = ap[:, 0:n]
        if len(shape) > 2:
            dims = {f"d{i}": shape[i + 1] for i in range(len(shape) - 1)}
            pat = ("p (" + " ".join(dims) + ") -> p " + " ".join(dims))
            ap = ap.rearrange(pat, **dims)
        self.off += nbytes
        self.hiwater = max(self.hiwater, self.off)
        b = Buf(f"{self.name}:{self.off}:{name}")
        for i, dep in enumerate(self.reset_deps):
            b.readers[f"__reset{i}_{dep.domain}"] = dep
        self.carves.append(b)
        return Tile(ap, b, shape, dtype)

    def reset(self):
        last = {}
        for b in self.carves:
            cands = list(b.readers.values())
            if b.writer is not None:
                cands.append(b.writer)
            for o in cands:
                cur = last.get((o.queue, o.domain))
                if cur is None or o.seq > cur.seq:
                    last[(o.queue, o.domain)] = o
        self.reset_deps = list(last.values())
        self.carves = []
        self.off = 0

    def free(self):
        self._cm.__exit__(None, None, None)



D = 1024
DT = 8
T = 4096
SEQ = 2048


def set_geometry(t=4096, seq=2048):
    global T, SEQ
    T = t
    SEQ = seq


class Ctx:
    def __init__(self, nc, perm_kb=34, big_kb=172):
        self.nc = nc
        self.S = Sched(nc)
        self.perm = Arena(nc, "perm", perm_kb * 1024)
        self.big = Arena(nc, "big", big_kb * 1024)
        self.wa = self.big
        self.act = self.big
        self._psum_cm = []
        self.banks = []
        for i in range(8):
            cm = nc.psum_tensor(f"bank{i}", [128, 512], F32)
            self._psum_cm.append(cm)
            t = cm.__enter__()
            self.banks.append(Tile(t[:], Buf(f"psum{i}"), [128, 512], F32))
        self.dram = {}
        self.ones_bf = self.perm.carve([128, 1], BF16, "ones_bf")
        self.ones_f = self.perm.carve([128, 1], F32, "ones_f")
        self.ones_row_f = self.perm.carve([128, 128], F32, "ones_row_f")
        self.S.op("dve", lambda v: v.memset(self.ones_bf.full, 1.0),
                  writes=[self.ones_bf.buf], label="ones_bf")
        self.S.op("dve", lambda v: v.memset(self.ones_f.full, 1.0),
                  writes=[self.ones_f.buf], label="ones_f")
        self.S.op("dve", lambda v: v.memset(self.ones_row_f[0:1, :], 1.0),
                  writes=[self.ones_row_f.buf], label="ones_row")
        self._consts = {}
        self.rows = self.perm.carve([128, 6 * 512], F32, "rows")

    def constant(self, value):
        if value not in self._consts:
            t = self.perm.carve([128, 1], F32, f"c{value}")
            self.S.op("dve", lambda v, t=t, value=value: v.memset(t.full, value),
                      writes=[t.buf], label="const")
            self._consts[value] = t
        return self._consts[value]

    def dram_in(self, name, shape, dtype=F32):
        ap = self.nc.dram_tensor(name, list(shape), dtype, kind="ExternalInput").ap()
        self.dram[name] = (ap, Buf("dram_" + name))
        return self.dram[name]

    def dram_out(self, name, shape, dtype=F32):
        ap = self.nc.dram_tensor(name, list(shape), dtype, kind="ExternalOutput").ap()
        self.dram[name] = (ap, Buf("dram_" + name))
        return self.dram[name]

    def dram_scratch(self, name, shape, dtype=F32, external=False):
        kind = "ExternalOutput" if external else "Internal"
        ap = self.nc.dram_tensor(name, list(shape), dtype, kind=kind).ap()
        self.dram[name] = (ap, Buf("dram_" + name))
        return self.dram[name]

    def close(self):
        self.S.close()
        self.big.free()
        self.perm.free()
        for cm in reversed(self._psum_cm):
            cm.__exit__(None, None, None)



def load_pvec(ctx, name, n=D):
    if not hasattr(ctx, "_pvecs"):
        ctx._pvecs = {}
    if name in ctx._pvecs:
        return ctx._pvecs[name]
    nt = n // 128
    ap, buf = ctx.dram_in(name, [nt, 128])
    tile = ctx.perm.carve([128, nt], F32, name)
    def f(eng):
        return eng.dma_start(out=tile.full, in_=ap.rearrange("t p -> p t"),
                             allow_slow_non_contiguous=True)
    ctx.S.dma("sp", f, reads=[buf], writes=[tile.buf], label="ld_" + name)
    ctx._pvecs[name] = tile
    return tile


def load_weight(ctx, name, kdim, mdim, arena=None, dram_name=None):
    arena = arena or ctx.wa
    kt = kdim // 128
    dram_name = dram_name or name
    if dram_name in ctx.dram:
        ap, buf = ctx.dram[dram_name]
    else:
        ap, buf = ctx.dram_in(dram_name, [128, kt, mdim])
    tile = arena.carve([128, kt, mdim], BF16, name)
    def f(eng):
        return eng.dma_start(out=tile.full, in_=ap[:])
    ctx.S.dma("pool", f, reads=[buf], writes=[tile.buf], label="ldw_" + name)
    return tile


def load_weight_f32(ctx, name, kdim, mdim, arena=None):
    kt = kdim // 128
    arena = arena or ctx.wa
    if name in ctx.dram:
        ap, buf = ctx.dram[name]
    else:
        ap, buf = ctx.dram_in(name, [128, kt, mdim])
    tile = arena.carve([128, kt, mdim], F32, name)
    ctx.S.dma("sp", lambda e: e.dma_start(out=tile.full, in_=ap[:]),
              reads=[buf], writes=[tile.buf], label="ldwf_" + name)
    return tile


def bcast_row(ctx, row_ap, row_buf, bank, width=512, label="bcast"):
    S = ctx.S
    onr = ctx.ones_row_f
    def f(eng):
        return eng.matmul(bank.ap[:, 0:width], onr[0:1, :], row_ap,
                          start=True, stop=True)
    S.op("pe", f, reads=[row_buf, onr.buf], writes=[bank.buf], label=label)


def mm_group(ctx, bank, w_tile, m_slice, rhs_tiles, n=512, label="mm",
             accumulate=False, final=True):
    S = ctx.S
    kt = w_tile.shape[1]
    m0, m1 = m_slice
    def f(eng):
        mm = None
        for k in range(kt):
            mm = eng.matmul(bank.ap[0:m1 - m0, 0:n], w_tile[:, k, m0:m1],
                            rhs_tiles[k][1], start=(k == 0 and not accumulate),
                            stop=(k == kt - 1 and final))
        return mm
    S.op("pe", f, reads=[w_tile.buf] + [rb for rb, _ in rhs_tiles],
         writes=[bank.buf], label=label)



def emit_stats(ctx, xt_par, cs, xbf_rot, sq_rot, ntiles=DT, with_s1=True):
    S = ctx.S
    b6, b7 = ctx.banks[6], ctx.banks[7]
    for m in range(ntiles):
        xb = xbf_rot[m % len(xbf_rot)]
        sq = sq_rot[m % len(sq_rot)]
        xbuf, xap = xt_par[m]
        S.op("act", lambda e, xap=xap, xb=xb, cs=cs:
             e.activation(out=xb.ap[:, 0:cs], in_=xap, func=AF.Copy),
             reads=[xbuf], writes=[xb.buf], label="cast_x")
        S.op("dve", lambda e, xb=xb, sq=sq, cs=cs:
             e.tensor_tensor(out=sq.ap[:, 0:cs], in0=xb.ap[:, 0:cs],
                             in1=xb.ap[:, 0:cs], op=OP.mult),
             reads=[xb.buf], writes=[sq.buf], label="sq_x")
        if with_s1:
            S.op("pe", lambda e, xb=xb, m=m, cs=cs:
                 e.matmul(b6.ap[0:1, 0:cs], ctx.ones_bf.full, xb.ap[:, 0:cs],
                          start=(m == 0), stop=(m == ntiles - 1)),
                 reads=[xb.buf, ctx.ones_bf.buf], writes=[b6.buf], label="S1mm")
        S.op("pe", lambda e, sq=sq, m=m, cs=cs:
             e.matmul(b7.ap[0:1, 0:cs], ctx.ones_bf.full, sq.ap[:, 0:cs],
                      start=(m == 0), stop=(m == ntiles - 1)),
             reads=[sq.buf, ctx.ones_bf.buf], writes=[b7.buf], label="S2mm")
    return b6, b7


def row(rows_tile, i, cs):
    return rows_tile.ap[0:1, i * 512: i * 512 + cs]


def emit_norm(ctx, xt_par, cs, xbf_rot, sq_rot, rows, kind, par, btiles,
              eps1=1e-5, eps2=1e-6):
    S = ctx.S
    with_s1 = kind != "rms"
    b6, b7 = emit_stats(ctx, xt_par, cs, xbf_rot, sq_rot, ntiles=len(xt_par),
                        with_s1=with_s1)
    r_mu, r_var, r_rstd, r_q, r_x2, r_pr = (row(rows, i, cs) for i in range(6))
    if kind == "rms":
        epsc = ctx.constant(eps2)
        S.op("act", lambda e: e.activation(out=r_pr, in_=b7.ap[0:1, 0:cs],
                                           func=AF.Ln, scale=1.0 / D,
                                           bias=epsc[0:1, 0:1]),
             reads=[b7.buf, epsc.buf], writes=[rows.buf], label="ln_s2")
        S.op("act", lambda e: e.activation(out=r_pr, in_=r_pr, func=AF.Exp,
                                           scale=-0.5),
             reads=[rows.buf], writes=[rows.buf], label="rms")
        blist = [(r_pr, btiles["rms_b"][par], b7)]
    else:
        eps1c = ctx.constant(eps1)
        S.op("act", lambda e: e.activation(out=r_mu, in_=b6.ap[0:1, 0:cs],
                                           func=AF.Copy, scale=1.0 / D),
             reads=[b6.buf], writes=[rows.buf], label="mu")
        S.op("act", lambda e: e.activation(out=r_q, in_=b6.ap[0:1, 0:cs],
                                           func=AF.Square, scale=1.0 / D),
             reads=[b6.buf], writes=[rows.buf], label="musq")
        S.op("dve", lambda e: e.scalar_tensor_tensor(
                 out=r_var, in0=b7.ap[0:1, 0:cs], scalar=1.0 / D, in1=r_q,
                 op0=OP.mult, op1=OP.subtract),
             reads=[b7.buf, rows.buf], writes=[rows.buf], label="var")
        S.op("act", lambda e: e.activation(out=r_rstd, in_=r_var, func=AF.Ln,
                                           bias=eps1c[0:1, 0:1]),
             reads=[rows.buf, eps1c.buf], writes=[rows.buf], label="lnvar")
        S.op("act", lambda e: e.activation(out=r_rstd, in_=r_rstd, func=AF.Exp,
                                           scale=-0.5),
             reads=[rows.buf], writes=[rows.buf], label="rstd")
        blist = [(r_mu, btiles["mu_b"][par], b6)]
        if "rstd_b" in btiles:
            blist.append((r_rstd, btiles["rstd_b"][par], b7))
        if kind in ("ln_rms", "ln_affine"):
            e2 = eps2 if kind == "ln_rms" else eps1
            e2c = ctx.constant(e2) if e2 != 0.0 else None
            S.op("dve", lambda e: e.tensor_tensor(out=r_q, in0=r_rstd,
                                                  in1=r_rstd, op=OP.mult),
                 reads=[rows.buf], writes=[rows.buf], label="rstd2")
            S.op("dve", lambda e: e.tensor_tensor(out=r_q, in0=r_var, in1=r_q,
                                                   op=OP.mult),
                 reads=[rows.buf], writes=[rows.buf], label="q")
            if e2c is None:
                S.op("act", lambda e: e.activation(out=r_x2, in_=r_q, func=AF.Ln),
                     reads=[rows.buf], writes=[rows.buf], label="lnq")
            else:
                S.op("act", lambda e: e.activation(out=r_x2, in_=r_q, func=AF.Ln,
                                                   bias=e2c[0:1, 0:1]),
                     reads=[rows.buf, e2c.buf], writes=[rows.buf], label="lnq")
            S.op("act", lambda e: e.activation(out=r_x2, in_=r_x2, func=AF.Exp,
                                               scale=-0.5),
                 reads=[rows.buf], writes=[rows.buf], label="rmsu")
            S.op("dve", lambda e: e.tensor_tensor(out=r_pr, in0=r_rstd,
                                                  in1=r_x2, op=OP.mult),
                 reads=[rows.buf], writes=[rows.buf], label="PR")
            key = "pr_b" if kind == "ln_rms" else "rr_b"
            blist.append((r_pr, btiles[key][par], b6))
    for rap, dst, bank in blist:
        bcast_row(ctx, rap, rows.buf, bank, cs)
        S.op("act", lambda e, dst=dst, bank=bank: e.activation(
                 out=dst.ap[:, 0:cs], in_=bank.ap[:, 0:cs], func=AF.Copy),
             reads=[bank.buf], writes=[dst.buf], label="evac_bcast")



def emit_phase_a1(ctx, cs=512):
    S = ctx.S
    nch = T // cs
    xin_ap, xin_buf = ctx.dram["XIN"]
    u2_ap, u2_buf = ctx.dram["U2"]

    wg = load_weight(ctx, "WGT", D, D)
    wv = load_weight(ctx, "WVT", D, D)
    wd = load_weight(ctx, "WDT", D, D)
    bg = load_pvec(ctx, "BG")
    bv = load_pvec(ctx, "BV")
    bd = load_pvec(ctx, "BD")
    rmsw = load_pvec(ctx, "RMSMIXW")

    A = ctx.act
    xt = [A.carve([128, DT, cs], F32, "xt") for _ in range(2)]
    t1 = [A.carve([128, DT, cs], BF16, "t1") for _ in range(2)]
    xbf_rot = [A.carve([128, cs], BF16, "xbf") for _ in range(2)]
    sq_rot = [A.carve([128, cs], BF16, "sqx") for _ in range(2)]
    rows = ctx.rows
    pr_b = [A.carve([128, cs], F32, "pr_b") for _ in range(2)]
    rstd_b = [A.carve([128, cs], F32, "rstd_b") for _ in range(2)]
    mu_b = [A.carve([128, cs], F32, "mu_b") for _ in range(2)]
    sg_rot = [A.carve([128, cs], F32, "sg") for _ in range(2)]
    tv_rot = [A.carve([128, cs], F32, "tv") for _ in range(2)]
    sa_rot = [A.carve([128, cs], F32, "sa") for _ in range(2)]
    xsc_rot = [A.carve([128, cs], F32, "xsc") for _ in range(2)]
    a_rot = [A.carve([128, cs], F32, "a") for _ in range(2)]
    h_rot = [A.carve([128, cs], F32, "h") for _ in range(2)]
    tmp_rot = [A.carve([128, cs], F32, "tmp") for _ in range(2)]
    u_rot = [A.carve([128, cs], F32, "u") for _ in range(2)]
    u2_rot = [A.carve([128, cs], F32, "u2") for _ in range(4)]
    carry = ctx.perm.carve([128, DT], F32, "carryA")
    eps5 = ctx.constant(1e-5)
    eps6 = ctx.constant(1e-6)

    b6, b7 = ctx.banks[6], ctx.banks[7]

    def emit_chunk(c):
        par = c % 2
        x_t = xt[par]
        S.dma("sp", lambda e: e.dma_start(
                  out=x_t.full,
                  in_=xin_ap[:, :, c * cs:(c + 1) * cs].rearrange("t p n -> p t n")),
              reads=[xin_buf], writes=[x_t.buf], label=f"ldx{c}")
        xt_par = [(x_t.buf, x_t[:, m, :]) for m in range(DT)]
        emit_stats(ctx, xt_par, cs, xbf_rot, sq_rot)
        r_mu, r_var, r_rstd, r_q, r_rmsu, r_pr = (
            row(rows, i, cs) for i in range(6))
        S.op("act", lambda e: e.activation(out=r_mu, in_=b6.ap[0:1, 0:cs],
                                           func=AF.Copy, scale=1.0 / D),
             reads=[b6.buf], writes=[rows.buf], label="mu")
        S.op("act", lambda e: e.activation(out=r_q, in_=b6.ap[0:1, 0:cs],
                                           func=AF.Square, scale=1.0 / D),
             reads=[b6.buf], writes=[rows.buf], label="musq")
        S.op("dve", lambda e: e.scalar_tensor_tensor(
                 out=r_var, in0=b7.ap[0:1, 0:cs], scalar=1.0 / D, in1=r_q,
                 op0=OP.mult, op1=OP.subtract),
             reads=[b7.buf, rows.buf], writes=[rows.buf], label="var")
        S.op("act", lambda e: e.activation(out=r_rstd, in_=r_var, func=AF.Ln,
                                           bias=eps5[0:1, 0:1]),
             reads=[rows.buf, eps5.buf], writes=[rows.buf], label="lnvar")
        S.op("act", lambda e: e.activation(out=r_rstd, in_=r_rstd, func=AF.Exp,
                                           scale=-0.5),
             reads=[rows.buf], writes=[rows.buf], label="rstd")
        S.op("dve", lambda e: e.tensor_tensor(out=r_q, in0=r_rstd, in1=r_rstd,
                                              op=OP.mult),
             reads=[rows.buf], writes=[rows.buf], label="rstd2")
        S.op("dve", lambda e: e.tensor_tensor(out=r_q, in0=r_var, in1=r_q,
                                               op=OP.mult),
             reads=[rows.buf], writes=[rows.buf], label="q")
        S.op("act", lambda e: e.activation(out=r_rmsu, in_=r_q, func=AF.Ln,
                                           bias=eps6[0:1, 0:1]),
             reads=[rows.buf, eps6.buf], writes=[rows.buf], label="lnq")
        S.op("act", lambda e: e.activation(out=r_rmsu, in_=r_rmsu, func=AF.Exp,
                                           scale=-0.5),
             reads=[rows.buf], writes=[rows.buf], label="rmsu")
        S.op("dve", lambda e: e.tensor_tensor(out=r_pr, in0=r_rstd, in1=r_rmsu,
                                              op=OP.mult),
             reads=[rows.buf], writes=[rows.buf], label="PR")
        for rap, dst, scl, bank in ((r_pr, pr_b[par], 1.0, b6),
                                    (r_rstd, rstd_b[par], 1.0, b7),
                                    (r_mu, mu_b[par], 1.0, b6)):
            bcast_row(ctx, rap, rows.buf, bank, cs)
            S.op("act", lambda e, dst=dst, scl=scl, bank=bank: e.activation(
                     out=dst.ap[:, 0:cs], in_=bank.ap[:, 0:cs], func=AF.Copy,
                     scale=scl),
                 reads=[bank.buf], writes=[dst.buf], label="evac_bcast")

        def emit_t1(m):
            tmp = tmp_rot[m % 2]
            S.op("dve", lambda e: e.tensor_tensor(
                     out=tmp.ap[:, 0:cs], in0=x_t[:, m, :],
                     in1=mu_b[par].ap[:, 0:cs], op=OP.subtract),
                 reads=[x_t.buf, mu_b[par].buf], writes=[tmp.buf], label="tmp")
            S.op("dve", lambda e: e.scalar_tensor_tensor(
                     out=t1[par][:, m, :], in0=tmp.ap[:, 0:cs],
                     scalar=rmsw[:, m:m + 1], in1=pr_b[par].ap[:, 0:cs],
                     op0=OP.mult, op1=OP.mult),
                 reads=[tmp.buf, rmsw.buf, pr_b[par].buf], writes=[t1[par].buf],
                 label="t1")
        for m in range(DT):
            emit_t1(m)
        t1_tiles = [(t1[par].buf, t1[par][:, k, :]) for k in range(DT)]

        def emit_m(m):
            bG = ctx.banks[(m % 2) * 3 + 0]
            bV = ctx.banks[(m % 2) * 3 + 1]
            bD = ctx.banks[(m % 2) * 3 + 2]
            mm_group(ctx, bG, wg, (m * 128, (m + 1) * 128), t1_tiles, cs, f"G{m}")
            mm_group(ctx, bV, wv, (m * 128, (m + 1) * 128), t1_tiles, cs, f"V{m}")
            mm_group(ctx, bD, wd, (m * 128, (m + 1) * 128), t1_tiles, cs, f"D{m}")
            sg, tv, sa = sg_rot[m % 2], tv_rot[m % 2], sa_rot[m % 2]
            xsc, a_t, h = xsc_rot[m % 2], a_rot[m % 2], h_rot[m % 2]
            S.op("act", lambda e: e.activation(
                     out=sg.ap[:, 0:cs], in_=bG.ap[:, 0:cs], func=AF.Sigmoid,
                     bias=bg[:, m:m + 1]),
                 reads=[bG.buf, bg.buf], writes=[sg.buf], label="sig_g")
            S.op("act", lambda e: e.activation(
                     out=tv.ap[:, 0:cs], in_=bV.ap[:, 0:cs], func=AF.Tanh,
                     bias=bv[:, m:m + 1]),
                 reads=[bV.buf, bv.buf], writes=[tv.buf], label="tanh_v")
            S.op("dve", lambda e: e.tensor_tensor(
                     out=xsc.ap[:, 0:cs], in0=sg.ap[:, 0:cs], in1=tv.ap[:, 0:cs],
                     op=OP.mult),
                 reads=[sg.buf, tv.buf], writes=[xsc.buf], label="xsc")
            S.op("act", lambda e: e.activation(
                     out=sa.ap[:, 0:cs], in_=bD.ap[:, 0:cs], func=AF.Sigmoid,
                     bias=bd[:, m:m + 1]),
                 reads=[bD.buf, bd.buf], writes=[sa.buf], label="sig_d")
            S.op("dve", lambda e: e.tensor_scalar(
                     out=a_t.ap[:, 0:cs], in0=sa.ap[:, 0:cs], scalar1=0.998,
                     scalar2=0.001, op0=OP.mult, op1=OP.add),
                 reads=[sa.buf], writes=[a_t.buf], label="a_affine")
            if c % (SEQ // cs) == 0:
                init = 0.0
                init_reads = []
            else:
                init = carry[:, m:m + 1]
                init_reads = [carry.buf]
            S.op("dve", lambda e: e.tensor_tensor_scan(
                     out=h.ap[:, 0:cs], data0=a_t.ap[:, 0:cs],
                     data1=xsc.ap[:, 0:cs], initial=init,
                     op0=OP.mult, op1=OP.add),
                 reads=[a_t.buf, xsc.buf] + init_reads, writes=[h.buf],
                 label="scan")
            S.op("act", lambda e: e.activation(
                     out=carry[:, m:m + 1], in_=h.ap[:, cs - 1:cs], func=AF.Copy),
                 reads=[h.buf], writes=[carry.buf], label="carry")
            tmp2, u, u2t = tmp_rot[m % 2], u_rot[m % 2], u2_rot[m % 4]
            S.op("dve", lambda e: e.tensor_tensor(
                     out=tmp2.ap[:, 0:cs], in0=x_t[:, m, :],
                     in1=mu_b[par].ap[:, 0:cs], op=OP.subtract),
                 reads=[x_t.buf, mu_b[par].buf], writes=[tmp2.buf], label="tmp2")
            S.op("dve", lambda e: e.tensor_tensor(
                     out=u.ap[:, 0:cs], in0=tmp2.ap[:, 0:cs],
                     in1=rstd_b[par].ap[:, 0:cs], op=OP.mult),
                 reads=[tmp2.buf, rstd_b[par].buf], writes=[u.buf], label="u")
            S.op("dve", lambda e: e.tensor_tensor(
                     out=u2t.ap[:, 0:cs], in0=u.ap[:, 0:cs], in1=h.ap[:, 0:cs],
                     op=OP.add),
                 reads=[u.buf, h.buf], writes=[u2t.buf], label="u2")
            S.dma("sp", lambda e: e.dma_start(
                      out=u2_ap[m, :, c * cs:(c + 1) * cs], in_=u2t.ap[:, 0:cs]),
                  reads=[u2t.buf], writes=[u2_buf], label=f"st_u2_{c}_{m}")

        for m in range(DT):
            emit_m(m)

    for c in range(nch):
        emit_chunk(c)



def emit_phase_a2a(ctx, cs=512):
    S = ctx.S
    nch = T // cs
    u2_ap, u2_buf = ctx.dram["U2"]
    gu_ap, gu_buf = ctx.dram["GU"]
    FT = 24

    wgate = load_weight(ctx, "WGATET", D, 3072)
    wup = load_weight(ctx, "WUPT", D, 3072)
    rmsfw = load_pvec(ctx, "RMSFFNW")

    A = ctx.act
    ut = [A.carve([128, DT, cs], F32, "ut") for _ in range(2)]
    t2 = [A.carve([128, DT, cs], BF16, "t2") for _ in range(2)]
    xbf_rot = [A.carve([128, cs], BF16, "xbf") for _ in range(2)]
    sq_rot = [A.carve([128, cs], BF16, "sqx") for _ in range(2)]
    rows = ctx.rows
    rms_b = [A.carve([128, cs], F32, "rms_b") for _ in range(2)]
    sil_rot = [A.carve([128, cs], BF16, "sil") for _ in range(2)]
    gu_rot = [A.carve([128, cs], BF16, "gu") for _ in range(4)]
    btiles = {"rms_b": rms_b}

    def emit_chunk(c):
        par = c % 2
        u_t = ut[par]
        S.dma("sp", lambda e: e.dma_start(
                  out=u_t.full,
                  in_=u2_ap[:, :, c * cs:(c + 1) * cs].rearrange("t p n -> p t n")),
              reads=[u2_buf], writes=[u_t.buf], label=f"ldu{c}")
        ut_par = [(u_t.buf, u_t[:, m, :]) for m in range(DT)]
        emit_norm(ctx, ut_par, cs, xbf_rot, sq_rot, rows, "rms", par, btiles,
                  eps2=1e-6)
        def emit_t2(m):
            S.op("dve", lambda e: e.scalar_tensor_tensor(
                     out=t2[par][:, m, :], in0=u_t[:, m, :],
                     scalar=rmsfw[:, m:m + 1], in1=rms_b[par].ap[:, 0:cs],
                     op0=OP.mult, op1=OP.mult),
                 reads=[u_t.buf, rmsfw.buf, rms_b[par].buf], writes=[t2[par].buf],
                 label="t2")
        for m in range(DT):
            emit_t2(m)
        t2_tiles = [(t2[par].buf, t2[par][:, k, :]) for k in range(DT)]

        def emit_f(f):
            bG = ctx.banks[(f % 3)]
            bU = ctx.banks[3 + (f % 3)]
            mm_group(ctx, bG, wgate, (f * 128, (f + 1) * 128), t2_tiles, cs, f"Ga{f}")
            mm_group(ctx, bU, wup, (f * 128, (f + 1) * 128), t2_tiles, cs, f"Ua{f}")
            sil = sil_rot[f % 2]
            gu_t = gu_rot[f % 4]
            S.op("act", lambda e: e.activation(
                     out=sil.ap[:, 0:cs], in_=bG.ap[:, 0:cs], func=AF.Silu),
                 reads=[bG.buf], writes=[sil.buf], label="silu")
            S.op("dve", lambda e: e.tensor_tensor(
                     out=gu_t.ap[:, 0:cs], in0=sil.ap[:, 0:cs],
                     in1=bU.ap[:, 0:cs], op=OP.mult),
                 reads=[sil.buf, bU.buf], writes=[gu_t.buf], label="gu")
            S.dma("sp", lambda e: e.dma_start(
                      out=gu_ap[f, :, c * cs:(c + 1) * cs], in_=gu_t.ap[:, 0:cs]),
                  reads=[gu_t.buf], writes=[gu_buf], label=f"st_gu{c}_{f}")
        for f in range(FT):
            emit_f(f)

    for c in range(nch):
        emit_chunk(c)



def emit_phase_a2b(ctx, cs=512):
    S = ctx.S
    nch = T // cs
    u2_ap, u2_buf = ctx.dram["U2"]
    gu_ap, gu_buf = ctx.dram["GU"]
    xin_ap, xin_buf = ctx.dram["XIN"]
    xr_ap, xr_buf = ctx.dram["XR"]
    FT = 24

    wout = load_weight(ctx, "WOUTT", 3072, D)

    A = ctx.act
    ut = [A.carve([128, DT, cs], F32, "ut")]
    xt = [A.carve([128, DT, cs], F32, "xt") for _ in range(2)]
    gut = [A.carve([128, FT, cs], BF16, "gut")]
    yt = [A.carve([128, DT, cs], F32, "yt")]
    xbf_rot = [A.carve([128, cs], BF16, "xbf") for _ in range(2)]
    sq_rot = [A.carve([128, cs], BF16, "sqx") for _ in range(2)]
    rows = ctx.rows
    mu_b = [A.carve([128, cs], F32, "mu_b") for _ in range(2)]
    rstd_b = [A.carve([128, cs], F32, "rstd_b") for _ in range(2)]
    tmp_rot = [A.carve([128, cs], F32, "tmp") for _ in range(2)]
    t_rot = [A.carve([128, cs], F32, "t") for _ in range(2)]
    xo_rot = [A.carve([128, cs], F32, "xo") for _ in range(3)]
    btiles = {"mu_b": mu_b, "rstd_b": rstd_b}

    def emit_chunk(c):
        par = c % 2
        u_t, x_t, gu_t, y_t = ut[0], xt[par], gut[0], yt[0]
        S.dma("sp", lambda e: e.dma_start(
                  out=u_t.full,
                  in_=u2_ap[:, :, c * cs:(c + 1) * cs].rearrange("t p n -> p t n")),
              reads=[u2_buf], writes=[u_t.buf], label=f"ldu{c}")
        S.dma("sp", lambda e: e.dma_start(
                  out=x_t.full,
                  in_=xin_ap[:, :, c * cs:(c + 1) * cs].rearrange("t p n -> p t n")),
              reads=[xin_buf], writes=[x_t.buf], label=f"ldx{c}")
        S.dma("sp", lambda e: e.dma_start(
                  out=gu_t.full,
                  in_=gu_ap[:, :, c * cs:(c + 1) * cs].rearrange("t p n -> p t n")),
              reads=[gu_buf], writes=[gu_t.buf], label=f"ldgu{c}")
        gu_tiles = [(gu_t.buf, gu_t[:, k, :]) for k in range(FT)]

        def emit_y(m):
            bank = ctx.banks[m % 4]
            mm_group(ctx, bank, wout, (m * 128, (m + 1) * 128), gu_tiles, cs,
                     f"Y{m}")
            S.op("dve", lambda e: e.tensor_tensor(
                     out=y_t[:, m, :], in0=u_t[:, m, :], in1=bank.ap[:, 0:cs],
                     op=OP.add),
                 reads=[u_t.buf, bank.buf], writes=[y_t.buf], label="yadd")
        for m in range(DT):
            emit_y(m)
        yt_par = [(y_t.buf, y_t[:, m, :]) for m in range(DT)]
        emit_norm(ctx, yt_par, cs, xbf_rot, sq_rot, rows, "ln", par, btiles)

        def emit_xo(m):
            tmp, t_, xo = tmp_rot[m % 2], t_rot[m % 2], xo_rot[m % 3]
            S.op("dve", lambda e: e.tensor_tensor(
                     out=tmp.ap[:, 0:cs], in0=y_t[:, m, :],
                     in1=mu_b[par].ap[:, 0:cs], op=OP.subtract),
                 reads=[y_t.buf, mu_b[par].buf], writes=[tmp.buf], label="ytmp")
            S.op("dve", lambda e: e.tensor_tensor(
                     out=t_.ap[:, 0:cs], in0=tmp.ap[:, 0:cs],
                     in1=rstd_b[par].ap[:, 0:cs], op=OP.mult),
                 reads=[tmp.buf, rstd_b[par].buf], writes=[t_.buf], label="yln")
            S.op("dve", lambda e: e.tensor_tensor(
                     out=xo.ap[:, 0:cs], in0=t_.ap[:, 0:cs], in1=x_t[:, m, :],
                     op=OP.add),
                 reads=[t_.buf, x_t.buf], writes=[xo.buf], label="xo")
            S.dma("sp", lambda e: e.dma_start(
                      out=xr_ap[m, :, c * cs:(c + 1) * cs], in_=xo.ap[:, 0:cs]),
                  reads=[xo.buf], writes=[xr_buf], label=f"st_xr{c}_{m}")
        for m in range(DT):
            emit_xo(m)

    for c in range(nch):
        emit_chunk(c)


def get_iden(ctx):
    if not hasattr(ctx, "iden"):
        ap, buf = ctx.dram_in("IDEN", [128, 128])
        tile = ctx.perm.carve([128, 128], F32, "iden")
        ctx.S.dma("sp", lambda e: e.dma_start(out=tile.full, in_=ap[:]),
                  reads=[buf], writes=[tile.buf], label="ld_iden")
        ctx.iden = tile
    return ctx.iden


def load_pcol(ctx, name, p):
    ap, buf = ctx.dram_in(name, [1, p])
    tile = ctx.perm.carve([128, 1], F32, name)
    ctx.S.dma("sp", lambda e: e.dma_start(out=tile.ap[0:p, 0:1],
                                          in_=ap.rearrange("o p -> p o"),
                                          allow_slow_non_contiguous=True),
              reads=[buf], writes=[tile.buf], label="ld_" + name)
    return tile



def emit_phase_b0(ctx, cs=512):
    S = ctx.S
    nch = T // cs
    xr_ap, xr_buf = ctx.dram["XR"]
    ub_ap, ub_buf = ctx.dram["UB"]
    rw = load_weight_f32(ctx, "RWTB", D, 16)
    rb = load_pcol(ctx, "RBB", 16)
    iden = get_iden(ctx)
    comb_ap, comb_buf = ctx.dram["COMB"]

    A = ctx.act
    xt = [A.carve([128, DT, cs], F32, "xt") for _ in range(2)]
    ub = [A.carve([128, DT, cs], BF16, "ub") for _ in range(2)]
    uf = [A.carve([128, DT, cs], F32, "uf")]
    xbf_rot = [A.carve([128, cs], BF16, "xbf") for _ in range(2)]
    sq_rot = [A.carve([128, cs], BF16, "sqx") for _ in range(2)]
    rows = ctx.rows
    mu_b = [A.carve([128, cs], F32, "mu_b") for _ in range(2)]
    rstd_b = [A.carve([128, cs], F32, "rstd_b") for _ in range(2)]
    tmp_rot = [A.carve([128, cs], F32, "tmp") for _ in range(2)]
    lbt = [A.carve([128, cs], F32, "lbt") for _ in range(2)]
    ltk_rot = [A.carve([128, 16], F32, "ltk") for _ in range(2)]
    ctile_rot = [A.carve([128, 128], F32, "ctile") for _ in range(2)]
    col_rot = [A.carve([128, 1], F32, f"col{i}") for i in range(8)]
    s16_rot = [A.carve([128, 16], F32, f"s16{i}") for i in range(6)]
    btiles = {"mu_b": mu_b, "rstd_b": rstd_b}

    def emit_chunk(c):
        par = c % 2
        x_t, ub_t, lb = xt[par], ub[par], lbt[par]
        S.dma("sp", lambda e: e.dma_start(
                  out=x_t.full,
                  in_=xr_ap[:, :, c * cs:(c + 1) * cs].rearrange("t p n -> p t n")),
              reads=[xr_buf], writes=[x_t.buf], label=f"ldx{c}")
        xt_par = [(x_t.buf, x_t[:, m, :]) for m in range(DT)]
        emit_norm(ctx, xt_par, cs, xbf_rot, sq_rot, rows, "ln", par, btiles)

        u_f = uf[0]
        def emit_u(m):
            tmp = tmp_rot[m % 2]
            S.op("dve", lambda e: e.tensor_tensor(
                     out=tmp.ap[:, 0:cs], in0=x_t[:, m, :],
                     in1=mu_b[par].ap[:, 0:cs], op=OP.subtract),
                 reads=[x_t.buf, mu_b[par].buf], writes=[tmp.buf], label="tmp")
            S.op("dve", lambda e: e.tensor_tensor(
                     out=u_f[:, m, :], in0=tmp.ap[:, 0:cs],
                     in1=rstd_b[par].ap[:, 0:cs], op=OP.mult),
                 reads=[tmp.buf, rstd_b[par].buf], writes=[u_f.buf], label="uf")
            S.op("act", lambda e: e.activation(
                     out=ub_t[:, m, :], in_=u_f[:, m, :], func=AF.Copy),
                 reads=[u_f.buf], writes=[ub_t.buf], label="ub")
        for m in range(DT):
            emit_u(m)
        S.dma("sp", lambda e: e.dma_start(
                  out=ub_ap[:, :, c * cs:(c + 1) * cs].rearrange("t p n -> p t n"),
                  in_=ub_t.full),
              reads=[ub_t.buf], writes=[ub_buf], label=f"st_ub{c}")
        uf_tiles = [(u_f.buf, u_f[:, k, :]) for k in range(DT)]
        b0 = ctx.banks[0]
        mm_group(ctx, b0, rw, (0, 16), uf_tiles, cs, "router")
        S.op("dve", lambda e: e.tensor_scalar(
                 out=lb[0:16, 0:cs], in0=b0.ap[0:16, 0:cs],
                 scalar1=rb[0:16, 0:1], scalar2=None, op0=OP.add),
             reads=[b0.buf, rb.buf], writes=[lb.buf], label="lb")

        def emit_j(j):
            bT = ctx.banks[1 + (j % 2)]
            S.op("pe", lambda e: e.matmul(bT.ap[:, 0:16],
                                          lb[0:16, j * 128:(j + 1) * 128],
                                          iden[0:16, 0:16], is_transpose=True,
                                          start=True, stop=True),
                 reads=[lb.buf, iden.buf], writes=[bT.buf], label="trin")
            ltk = ltk_rot[j % 2]
            S.op("act", lambda e: e.activation(out=ltk.full, in_=bT.ap[:, 0:16],
                                               func=AF.Copy),
                 reads=[bT.buf], writes=[ltk.buf], label="ltk")
            m1, m2, e2c, p1, p2 = (col_rot[(j % 2) * 4 + i] for i in range(4)) \
                if False else (col_rot[0], col_rot[1], col_rot[2], col_rot[3],
                               col_rot[4])
            d1, msk, ge1, ge2, ctk = (s16_rot[i] for i in range(5))
            S.op("dve", lambda e: e.reduce_max(out=m1.full, in_=ltk.full,
                                               axis=AX.X),
                 reads=[ltk.buf], writes=[m1.buf], label="m1")
            S.op("dve", lambda e: e.tensor_scalar(
                     out=d1.full, in0=ltk.full, scalar1=m1.full, scalar2=None,
                     op0=OP.subtract),
                 reads=[ltk.buf, m1.buf], writes=[d1.buf], label="d1")
            S.op("dve", lambda e: e.tensor_scalar(
                     out=ge1.full, in0=d1.full, scalar1=0.0, scalar2=None,
                     op0=OP.is_ge),
                 reads=[d1.buf], writes=[ge1.buf], label="ge1")
            S.op("dve", lambda e: e.scalar_tensor_tensor(
                     out=msk.full, in0=ge1.full, scalar=-1e30, in1=ltk.full,
                     op0=OP.mult, op1=OP.add),
                 reads=[ge1.buf, ltk.buf], writes=[msk.buf], label="msk")
            S.op("dve", lambda e: e.reduce_max(out=m2.full, in_=msk.full,
                                               axis=AX.X),
                 reads=[msk.buf], writes=[m2.buf], label="m2")
            S.op("dve", lambda e: e.tensor_tensor(
                     out=e2c.full, in0=m2.full, in1=m1.full, op=OP.subtract),
                 reads=[m2.buf, m1.buf], writes=[e2c.buf], label="d21")
            S.op("act", lambda e: e.activation(out=e2c.full, in_=e2c.full,
                                               func=AF.Exp),
                 reads=[e2c.buf], writes=[e2c.buf], label="e2")
            S.op("dve", lambda e: e.tensor_scalar(
                     out=p1.full, in0=e2c.full, scalar1=1.0, scalar2=1.0,
                     op0=OP.mult, op1=OP.add),
                 reads=[e2c.buf], writes=[p1.buf], label="den")
            S.op("dve", lambda e: e.reciprocal(out=p1.full, in_=p1.full),
                 reads=[p1.buf], writes=[p1.buf], label="p1")
            S.op("dve", lambda e: e.tensor_tensor(
                     out=p2.full, in0=e2c.full, in1=p1.full, op=OP.mult),
                 reads=[e2c.buf, p1.buf], writes=[p2.buf], label="p2")
            S.op("dve", lambda e: e.tensor_scalar(
                     out=ge2.full, in0=msk.full, scalar1=m2.full, scalar2=None,
                     op0=OP.subtract),
                 reads=[msk.buf, m2.buf], writes=[ge2.buf], label="d2")
            S.op("dve", lambda e: e.tensor_scalar(
                     out=ge2.full, in0=ge2.full, scalar1=0.0, scalar2=None,
                     op0=OP.is_ge),
                 reads=[ge2.buf], writes=[ge2.buf], label="ge2")
            S.op("dve", lambda e: e.tensor_scalar(
                     out=d1.full, in0=ge1.full, scalar1=p1.full, scalar2=None,
                     op0=OP.mult),
                 reads=[ge1.buf, p1.buf], writes=[d1.buf], label="c1")
            S.op("dve", lambda e: e.scalar_tensor_tensor(
                     out=ctk.full, in0=ge2.full, scalar=p2.full, in1=d1.full,
                     op0=OP.mult, op1=OP.add),
                 reads=[ge2.buf, p2.buf, d1.buf], writes=[ctk.buf], label="ctk")
            bT2 = ctx.banks[3 + (j % 2)]
            S.op("pe", lambda e: e.matmul(bT2.ap[0:16, 0:128], ctk.full,
                                          iden.full, is_transpose=True,
                                          start=True, stop=True),
                 reads=[ctk.buf, iden.buf], writes=[bT2.buf], label="trout")
            ctile = ctile_rot[j % 2]
            S.op("act", lambda e: e.activation(
                     out=ctile.ap[0:16, 0:128],
                     in_=bT2.ap[0:16, 0:128], func=AF.Copy),
                 reads=[bT2.buf], writes=[ctile.buf], label="comb")
            S.dma("sp", lambda e: e.dma_start(
                      out=comb_ap[0:16, c * cs + j * 128: c * cs + (j + 1) * 128],
                      in_=ctile.ap[0:16, 0:128]),
                  reads=[ctile.buf], writes=[comb_buf], label=f"stcomb{c}_{j}")
        for j in range(cs // 128):
            emit_j(j)

    for c in range(nch):
        emit_chunk(c)



def emit_phase_bexp(ctx, experts, first, cs=512):
    S = ctx.S
    nch = T // cs
    ub_ap, ub_buf = ctx.dram["UB"]
    acc_ap, acc_buf = ctx.dram["ACC"]
    comb_ap, comb_buf = ctx.dram["COMB"]
    NE = len(experts)

    wg = [load_weight(ctx, f"BGATE{e}", D, 512) for e in experts]
    wu = [load_weight(ctx, f"BUP{e}", D, 512) for e in experts]
    wd = [load_weight(ctx, f"BDOWN{e}", 512, D) for e in experts]

    A = ctx.act
    ub = [A.carve([128, DT, cs], BF16, "ub") for _ in range(2)]
    hb = A.carve([128, NE * 4, cs], BF16, "hb")
    cb = [A.carve([128, cs], BF16, f"cb{i}") for i in range(NE)]
    sil_rot = [A.carve([128, cs], BF16, "sil") for _ in range(2)]
    upe_rot = [A.carve([128, cs], BF16, "upe") for _ in range(2)]
    h1_rot = [A.carve([128, cs], BF16, "h1") for _ in range(2)]
    res_rot = [A.carve([128, cs], F32, "res") for _ in range(4)]

    def emit_chunk(c):
        par = c % 2
        ub_t = ub[par]
        S.dma("sp", lambda e: e.dma_start(
                  out=ub_t.full,
                  in_=ub_ap[:, :, c * cs:(c + 1) * cs].rearrange("t p n -> p t n")),
              reads=[ub_buf], writes=[ub_t.buf], label=f"ldub{c}")
        ub_tiles = [(ub_t.buf, ub_t[:, k, :]) for k in range(DT)]
        for i, e_ in enumerate(experts):
            S.dma("pool", lambda e, i=i, e_=e_: e.dma_start(
                      out=cb[i].ap[:, 0:cs],
                      in_=comb_ap[e_, c * cs:(c + 1) * cs].partition_broadcast(128)),
                  reads=[comb_buf], writes=[cb[i].buf], label=f"ldcb{i}")

        def emit_eh(i, fm):
            bG = ctx.banks[fm % 2]
            bU = ctx.banks[2 + fm % 2]
            mm_group(ctx, bG, wg[i], (fm * 128, (fm + 1) * 128), ub_tiles, cs,
                     f"eG{i}_{fm}")
            mm_group(ctx, bU, wu[i], (fm * 128, (fm + 1) * 128), ub_tiles, cs,
                     f"eU{i}_{fm}")
            sil, upe, h1 = sil_rot[fm % 2], upe_rot[fm % 2], h1_rot[fm % 2]
            S.op("act", lambda e: e.activation(
                     out=sil.ap[:, 0:cs], in_=bG.ap[:, 0:cs], func=AF.Silu),
                 reads=[bG.buf], writes=[sil.buf], label="silu")
            S.op("dve", lambda e: e.activation(out=upe.ap[:, 0:cs],
                                               in_=bU.ap[:, 0:cs], func=AF.Copy)
                 if False else e.tensor_copy(out=upe.ap[:, 0:cs],
                                             in_=bU.ap[:, 0:cs]),
                 reads=[bU.buf], writes=[upe.buf], label="upe")
            S.op("dve", lambda e: e.tensor_tensor(
                     out=h1.ap[:, 0:cs], in0=sil.ap[:, 0:cs],
                     in1=upe.ap[:, 0:cs], op=OP.mult),
                 reads=[sil.buf, upe.buf], writes=[h1.buf], label="h1")
            S.op("dve", lambda e: e.tensor_tensor(
                     out=hb[:, i * 4 + fm, :], in0=h1.ap[:, 0:cs],
                     in1=cb[i].ap[:, 0:cs], op=OP.mult),
                 reads=[h1.buf, cb[i].buf], writes=[hb.buf], label="hb")
        for i in range(NE):
            for fm in range(4):
                emit_eh(i, fm)

        def emit_down(m):
            bank = ctx.banks[4 + (m % 2)]
            for i in range(NE):
                tiles = [(hb.buf, hb[:, i * 4 + k, :]) for k in range(4)]
                mm_group(ctx, bank, wd[i], (m * 128, (m + 1) * 128), tiles, cs,
                         f"dn{i}_{m}", accumulate=(i > 0), final=(i == NE - 1))
            res = res_rot[m % 4]
            S.op("act", lambda e: e.activation(out=res.ap[:, 0:cs],
                                               in_=bank.ap[:, 0:cs], func=AF.Copy),
                 reads=[bank.buf], writes=[res.buf], label="res")
            if first:
                S.dma("pool", lambda e: e.dma_start(
                          out=acc_ap[m, :, c * cs:(c + 1) * cs],
                          in_=res.ap[:, 0:cs]),
                      reads=[res.buf], writes=[acc_buf], label=f"stacc{c}_{m}")
            else:
                S.dma("pool", lambda e: e.dma_start(
                          out=acc_ap[m, :, c * cs:(c + 1) * cs],
                          in_=res.ap[:, 0:cs], accum_op=OP.add),
                      reads=[res.buf], writes=[acc_buf], label=f"acacc{c}_{m}")
        for m in range(DT):
            emit_down(m)

    for c in range(nch):
        emit_chunk(c)



def emit_phase_b5(ctx, cs=512):
    S = ctx.S
    nch = T // cs
    ub_ap, ub_buf = ctx.dram["UB"]
    acc_ap, acc_buf = ctx.dram["ACC"]
    xr_ap, xr_buf = ctx.dram["XR"]

    shg = load_weight(ctx, "SHGT", D, D)
    shu = load_weight(ctx, "SHUT", D, D)
    shd = load_weight(ctx, "SHDT", D, D)

    A = ctx.act
    ub = [A.carve([128, DT, cs], BF16, "ub") for _ in range(2)]
    acct = [A.carve([128, DT, cs], F32, "acct")]
    xt = [A.carve([128, DT, cs], F32, "xt") for _ in range(2)]
    hh = A.carve([128, DT, cs], BF16, "hh")
    yt = [A.carve([128, DT, cs], F32, "yt")]
    sil_rot = [A.carve([128, cs], BF16, "sil") for _ in range(2)]
    upe_rot = [A.carve([128, cs], BF16, "upe") for _ in range(2)]
    xbf_rot = [A.carve([128, cs], BF16, "xbf") for _ in range(2)]
    sq_rot = [A.carve([128, cs], BF16, "sqx") for _ in range(2)]
    rows = ctx.rows
    mu_b = [A.carve([128, cs], F32, "mu_b") for _ in range(2)]
    rstd_b = [A.carve([128, cs], F32, "rstd_b") for _ in range(2)]
    tmp_rot = [A.carve([128, cs], F32, "tmp") for _ in range(2)]
    t_rot = [A.carve([128, cs], F32, "t") for _ in range(2)]
    xo_rot = [A.carve([128, cs], F32, "xo") for _ in range(3)]
    btiles = {"mu_b": mu_b, "rstd_b": rstd_b}

    def emit_chunk(c):
        par = c % 2
        ub_t, acc_t, x_t, y_t = ub[par], acct[0], xt[par], yt[0]
        S.dma("sp", lambda e: e.dma_start(
                  out=ub_t.full,
                  in_=ub_ap[:, :, c * cs:(c + 1) * cs].rearrange("t p n -> p t n")),
              reads=[ub_buf], writes=[ub_t.buf], label=f"ldub{c}")
        S.dma("sp", lambda e: e.dma_start(
                  out=acc_t.full,
                  in_=acc_ap[:, :, c * cs:(c + 1) * cs].rearrange("t p n -> p t n")),
              reads=[acc_buf], writes=[acc_t.buf], label=f"ldacc{c}")
        S.dma("sp", lambda e: e.dma_start(
                  out=x_t.full,
                  in_=xr_ap[:, :, c * cs:(c + 1) * cs].rearrange("t p n -> p t n")),
              reads=[xr_buf], writes=[x_t.buf], label=f"ldx{c}")
        ub_tiles = [(ub_t.buf, ub_t[:, k, :]) for k in range(DT)]

        def emit_sh(fm):
            bG = ctx.banks[fm % 2]
            bU = ctx.banks[2 + fm % 2]
            mm_group(ctx, bG, shg, (fm * 128, (fm + 1) * 128), ub_tiles, cs,
                     f"shG{fm}")
            mm_group(ctx, bU, shu, (fm * 128, (fm + 1) * 128), ub_tiles, cs,
                     f"shU{fm}")
            sil, upe = sil_rot[fm % 2], upe_rot[fm % 2]
            S.op("act", lambda e: e.activation(
                     out=sil.ap[:, 0:cs], in_=bG.ap[:, 0:cs], func=AF.Silu),
                 reads=[bG.buf], writes=[sil.buf], label="silu")
            S.op("dve", lambda e: e.tensor_copy(out=upe.ap[:, 0:cs],
                                                in_=bU.ap[:, 0:cs]),
                 reads=[bU.buf], writes=[upe.buf], label="upe")
            S.op("dve", lambda e: e.tensor_tensor(
                     out=hh[:, fm, :], in0=sil.ap[:, 0:cs], in1=upe.ap[:, 0:cs],
                     op=OP.mult),
                 reads=[sil.buf, upe.buf], writes=[hh.buf], label="hh")
        for fm in range(DT):
            emit_sh(fm)
        hh_tiles = [(hh.buf, hh[:, k, :]) for k in range(DT)]

        def emit_y(m):
            bank = ctx.banks[4 + (m % 2)]
            mm_group(ctx, bank, shd, (m * 128, (m + 1) * 128), hh_tiles, cs,
                     f"shD{m}")
            S.op("dve", lambda e: e.tensor_tensor(
                     out=y_t[:, m, :], in0=acc_t[:, m, :], in1=bank.ap[:, 0:cs],
                     op=OP.add),
                 reads=[acc_t.buf, bank.buf], writes=[y_t.buf], label="yadd")
        for m in range(DT):
            emit_y(m)
        yt_par = [(y_t.buf, y_t[:, m, :]) for m in range(DT)]
        emit_norm(ctx, yt_par, cs, xbf_rot, sq_rot, rows, "ln", par, btiles)

        def emit_xo(m):
            tmp, t_, xo = tmp_rot[m % 2], t_rot[m % 2], xo_rot[m % 3]
            S.op("dve", lambda e: e.tensor_tensor(
                     out=tmp.ap[:, 0:cs], in0=y_t[:, m, :],
                     in1=mu_b[par].ap[:, 0:cs], op=OP.subtract),
                 reads=[y_t.buf, mu_b[par].buf], writes=[tmp.buf], label="ytmp")
            S.op("dve", lambda e: e.tensor_tensor(
                     out=t_.ap[:, 0:cs], in0=tmp.ap[:, 0:cs],
                     in1=rstd_b[par].ap[:, 0:cs], op=OP.mult),
                 reads=[tmp.buf, rstd_b[par].buf], writes=[t_.buf], label="yln")
            S.op("dve", lambda e: e.tensor_tensor(
                     out=xo.ap[:, 0:cs], in0=t_.ap[:, 0:cs], in1=x_t[:, m, :],
                     op=OP.add),
                 reads=[t_.buf, x_t.buf], writes=[xo.buf], label="xo")
            S.dma("sp", lambda e: e.dma_start(
                      out=xr_ap[m, :, c * cs:(c + 1) * cs], in_=xo.ap[:, 0:cs]),
                  reads=[xo.buf], writes=[xr_buf], label=f"st_xr{c}_{m}")
        for m in range(DT):
            emit_xo(m)

    for c in range(nch):
        emit_chunk(c)



def emit_phase_c0(ctx, cs=512):
    S = ctx.S
    nch = T // cs
    HT = 16
    xr_ap, xr_buf = ctx.dram["XR"]
    xc_ap, xc_buf = ctx.dram["XC"]

    wx = load_weight(ctx, "WIXCT", D, 2048)
    gam1 = load_pvec(ctx, "GAM1")
    cw = [load_pvec(ctx, f"CONVW{k}", 2048) for k in range(4)]
    cb = load_pvec(ctx, "CONVB", 2048)

    A = ctx.act
    xt = [A.carve([128, DT, cs], F32, "xt")]
    xn = [A.carve([128, DT, cs], BF16, "xn")]
    xcbuf = [A.carve([128, HT, cs + 4], F32, "xcbuf") for _ in range(2)]
    accb = A.carve([128, HT, cs], F32, "accb")
    xbf_rot = [A.carve([128, cs], BF16, "xbf") for _ in range(2)]
    sq_rot = [A.carve([128, cs], BF16, "sqx") for _ in range(2)]
    rows = ctx.rows
    mu_b = [A.carve([128, cs], F32, "mu_b") for _ in range(2)]
    pr_b = [A.carve([128, cs], F32, "pr_b") for _ in range(2)]
    tmp_rot = [A.carve([128, cs], F32, "tmp") for _ in range(2)]
    btiles = {"mu_b": mu_b, "pr_b": pr_b}

    def emit_chunk(c):
        par = c % 2
        x_t, xn_t, xcb = xt[0], xn[0], xcbuf[par]
        S.dma("sp", lambda e: e.dma_start(
                  out=x_t.full,
                  in_=xr_ap[:, :, c * cs:(c + 1) * cs].rearrange("t p n -> p t n")),
              reads=[xr_buf], writes=[x_t.buf], label=f"ldx{c}")
        xt_par = [(x_t.buf, x_t[:, m, :]) for m in range(DT)]
        emit_norm(ctx, xt_par, cs, xbf_rot, sq_rot, rows, "ln_rms", par, btiles,
                  eps2=0.0)
        def emit_xn(m):
            tmp = tmp_rot[m % 2]
            S.op("dve", lambda e: e.tensor_tensor(
                     out=tmp.ap[:, 0:cs], in0=x_t[:, m, :],
                     in1=mu_b[par].ap[:, 0:cs], op=OP.subtract),
                 reads=[x_t.buf, mu_b[par].buf], writes=[tmp.buf], label="tmp")
            S.op("dve", lambda e: e.scalar_tensor_tensor(
                     out=xn_t[:, m, :], in0=tmp.ap[:, 0:cs],
                     scalar=gam1[:, m:m + 1], in1=pr_b[par].ap[:, 0:cs],
                     op0=OP.mult, op1=OP.mult),
                 reads=[tmp.buf, gam1.buf, pr_b[par].buf], writes=[xn_t.buf],
                 label="xn")
        for m in range(DT):
            emit_xn(m)
        xn_tiles = [(xn_t.buf, xn_t[:, k, :]) for k in range(DT)]
        if c % (SEQ // cs) == 0:
            S.op("dve", lambda e: e.memset(xcb[:, :, 0:3], 0.0),
                 writes=[xcb.buf], label="halo0")
        else:
            prev = xcbuf[1 - par]
            S.op("act", lambda e: e.activation(
                     out=xcb[:, :, 0:3], in_=prev[:, :, cs:cs + 3], func=AF.Copy),
                 reads=[prev.buf], writes=[xcb.buf], label="halo")
        for fm in range(HT):
            bank = ctx.banks[fm % 4]
            mm_group(ctx, bank, wx, (fm * 128, (fm + 1) * 128), xn_tiles, cs,
                     f"xc{fm}")
            S.op("act", lambda e, fm=fm, bank=bank: e.activation(
                     out=xcb[:, fm, 3:3 + cs], in_=bank.ap[:, 0:cs], func=AF.Copy),
                 reads=[bank.buf], writes=[xcb.buf], label="evxc")
        def st0(fm):
            S.op("dve", lambda e: e.tensor_scalar(
                     out=accb[:, fm, :], in0=xcb[:, fm, 0:cs],
                     scalar1=cw[0][:, fm:fm + 1], scalar2=cb[:, fm:fm + 1],
                     op0=OP.mult, op1=OP.add),
                 reads=[xcb.buf, cw[0].buf, cb.buf], writes=[accb.buf],
                 label="cv0")
        for fm in range(HT):
            st0(fm)
        for k in range(1, 4):
            def stk(fm, k=k):
                S.op("dve", lambda e: e.scalar_tensor_tensor(
                         out=accb[:, fm, :], in0=xcb[:, fm, k:k + cs],
                         scalar=cw[k][:, fm:fm + 1], in1=accb[:, fm, :],
                         op0=OP.mult, op1=OP.add),
                     reads=[xcb.buf, cw[k].buf, accb.buf], writes=[accb.buf],
                     label=f"cv{k}")
            for fm in range(HT):
                stk(fm)
        S.dma("sp", lambda e: e.dma_start(
                  out=xc_ap[:, :, c * cs:(c + 1) * cs].rearrange("t p n -> p t n"),
                  in_=accb.full),
              reads=[accb.buf], writes=[xc_buf], label=f"st_xc{c}")
    for c in range(nch):
        emit_chunk(c)



def emit_phase_c1f(ctx, cs=256):
    S = ctx.S
    nch = T // cs
    HT = 16
    xc_ap, xc_buf = ctx.dram["XC"]
    at_ap, at_buf = ctx.dram["AT"]

    wf = load_weight(ctx, "WGATESFT", 2048, 2048)
    bgf = load_pvec(ctx, "BGF", 2048)
    fb = load_pvec(ctx, "FB", 2048)
    one_c = ctx.constant(1.0)
    c2 = ctx.perm.carve([128, HT], F32, "c2col")
    t1 = ctx.perm.carve([128, HT], F32, "spt1")
    S.op("act", lambda e: e.activation(out=t1.full, in_=fb.full, func=AF.Abs),
         reads=[fb.buf], writes=[t1.buf], label="spabs")
    S.op("act", lambda e: e.activation(out=t1.full, in_=t1.full, func=AF.Exp,
                                       scale=-1.0),
         reads=[t1.buf], writes=[t1.buf], label="spexp")
    S.op("act", lambda e: e.activation(out=t1.full, in_=t1.full, func=AF.Ln,
                                       bias=one_c.full),
         reads=[t1.buf, one_c.buf], writes=[t1.buf], label="spln")
    S.op("dve", lambda e: e.tensor_scalar(out=c2.full, in0=fb.full, scalar1=0.0,
                                           scalar2=None, op0=OP.max),
         reads=[fb.buf], writes=[c2.buf], label="spmax")
    S.op("dve", lambda e: e.tensor_tensor(out=c2.full, in0=c2.full, in1=t1.full,
                                          op=OP.add),
         reads=[c2.buf, t1.buf], writes=[c2.buf], label="spadd")
    S.op("dve", lambda e: e.tensor_scalar(out=c2.full, in0=c2.full, scalar1=-4.0,
                                           scalar2=None, op0=OP.mult),
         reads=[c2.buf], writes=[c2.buf], label="spscale")
    ctx.c2col = c2

    A = ctx.act
    xcb = [A.carve([128, HT, cs], BF16, "xcb") for _ in range(2)]
    fbuf = A.carve([128, HT, cs], F32, "fbuf")
    atb = [A.carve([128, HT, cs], F32, "atb") for _ in range(2)]

    def emit_chunk(c):
        par = c % 2
        xc_t, at_t = xcb[par], atb[par]
        S.dma("pool", lambda e: e.dma_start(
                  out=xc_t.full,
                  in_=xc_ap[:, :, c * cs:(c + 1) * cs].rearrange("t p n -> p t n")),
              reads=[xc_buf], writes=[xc_t.buf], label=f"ldxc{c}")
        xc_tiles = [(xc_t.buf, xc_t[:, k, :]) for k in range(HT)]
        for fm in range(HT):
            bank = ctx.banks[fm % 4]
            mm_group(ctx, bank, wf, (fm * 128, (fm + 1) * 128), xc_tiles, cs,
                     f"fg{fm}")
            S.op("dve", lambda e, fm=fm, bank=bank: e.tensor_scalar(
                     out=fbuf[:, fm, :], in0=bank.ap[:, 0:cs],
                     scalar1=bgf[:, fm:fm + 1], scalar2=None, op0=OP.add),
                 reads=[bank.buf, bgf.buf], writes=[fbuf.buf], label="evf")
        S.op("act", lambda e: e.activation(out=fbuf.full, in_=fbuf.full,
                                           func=AF.Tanh, scale=0.5),
             reads=[fbuf.buf], writes=[fbuf.buf], label="tanhF")
        def emit_at(fm):
            S.op("dve", lambda e: e.tensor_scalar(
                     out=at_t[:, fm, :], in0=fbuf[:, fm, :],
                     scalar1=c2[:, fm:fm + 1], scalar2=c2[:, fm:fm + 1],
                     op0=OP.mult, op1=OP.add),
                 reads=[fbuf.buf, c2.buf], writes=[at_t.buf], label="at")
        for fm in range(HT):
            emit_at(fm)
        S.dma("sp", lambda e: e.dma_start(
                  out=at_ap[:, :, c * cs:(c + 1) * cs].rearrange("t p n -> p t n"),
                  in_=at_t.full),
              reads=[at_t.buf], writes=[at_buf], label=f"st_at{c}")
    for c in range(nch):
        emit_chunk(c)



def emit_phase_c1i(ctx, cs=256):
    S = ctx.S
    nch = T // cs
    HT = 16
    xc_ap, xc_buf = ctx.dram["XC"]
    at_ap, at_buf = ctx.dram["AT"]
    h_ap, h_buf = ctx.dram["H"]

    wi = load_weight(ctx, "WGATESIT", 2048, 2048)
    bgi = load_pvec(ctx, "BGI", 2048)

    A = ctx.act
    xcb = [A.carve([128, HT, cs], BF16, "xcb")]
    ibuf = A.carve([128, HT, cs], F32, "ibuf")
    atb = [A.carve([128, HT, cs], F32, "atb") for _ in range(2)]
    albuf = A.carve([128, HT, cs], F32, "albuf")
    bebuf = A.carve([128, HT, cs], F32, "bebuf")
    hb = [A.carve([128, HT, cs], BF16, "hb") for _ in range(2)]

    def emit_chunk(c):
        par = c % 2
        xc_t, at_t, h_t = xcb[0], atb[par], hb[par]
        S.dma("pool", lambda e: e.dma_start(
                  out=xc_t.full,
                  in_=xc_ap[:, :, c * cs:(c + 1) * cs].rearrange("t p n -> p t n")),
              reads=[xc_buf], writes=[xc_t.buf], label=f"ldxc{c}")
        S.dma("sp", lambda e: e.dma_start(
                  out=at_t.full,
                  in_=at_ap[:, :, c * cs:(c + 1) * cs].rearrange("t p n -> p t n")),
              reads=[at_buf], writes=[at_t.buf], label=f"ldat{c}")
        xc_tiles = [(xc_t.buf, xc_t[:, k, :]) for k in range(HT)]
        for fm in range(HT):
            bank = ctx.banks[fm % 4]
            mm_group(ctx, bank, wi, (fm * 128, (fm + 1) * 128), xc_tiles, cs,
                     f"ig{fm}")
            S.op("dve", lambda e, fm=fm, bank=bank: e.tensor_scalar(
                     out=ibuf[:, fm, :], in0=bank.ap[:, 0:cs],
                     scalar1=bgi[:, fm:fm + 1], scalar2=None, op0=OP.add),
                 reads=[bank.buf, bgi.buf], writes=[ibuf.buf], label="evi")
        S.op("act", lambda e: e.activation(out=ibuf.full, in_=ibuf.full,
                                           func=AF.Tanh, scale=0.5),
             reads=[ibuf.buf], writes=[ibuf.buf], label="tanhI")
        S.op("act", lambda e: e.activation(out=albuf.full, in_=at_t.full,
                                           func=AF.Exp),
             reads=[at_t.buf], writes=[albuf.buf], label="alpha")
        S.op("act", lambda e: e.activation(out=bebuf.full, in_=at_t.full,
                                           func=AF.Exp, scale=2.0),
             reads=[at_t.buf], writes=[bebuf.buf], label="alpha2")
        S.op("dve", lambda e: e.tensor_scalar(
                 out=bebuf.full, in0=bebuf.full, scalar1=-1.0,
                 scalar2=1.0 + 1e-6, op0=OP.mult, op1=OP.add),
             reads=[bebuf.buf], writes=[bebuf.buf], label="bsq")
        S.op("act", lambda e: e.activation(out=bebuf.full, in_=bebuf.full,
                                           func=AF.Sqrt),
             reads=[bebuf.buf], writes=[bebuf.buf], label="beta")
        S.op("dve", lambda e: e.tensor_tensor(out=ibuf.full, in0=bebuf.full,
                                               in1=ibuf.full, op=OP.mult),
             reads=[bebuf.buf, ibuf.buf], writes=[ibuf.buf], label="btI")
        S.op("dve", lambda e: e.tensor_tensor(out=at_t.full, in0=bebuf.full,
                                              in1=ibuf.full, op=OP.add),
             reads=[bebuf.buf, ibuf.buf], writes=[at_t.buf], label="xs2")
        def emit_xs(fm):
            S.op("dve", lambda e: e.scalar_tensor_tensor(
                     out=at_t[:, fm, :], in0=at_t[:, fm, :], scalar=0.5,
                     in1=xc_t[:, fm, :], op0=OP.mult, op1=OP.mult),
                 reads=[at_t.buf, xc_t.buf], writes=[at_t.buf], label="xs")
        for fm in range(HT):
            emit_xs(fm)
        def emit_scan(fm):
            if c % (SEQ // cs) == 0:
                init = 0.0
                init_reads = []
            else:
                init = hb[1 - par][:, fm, cs - 1:cs]
                init_reads = [hb[1 - par].buf]
            S.op("dve", lambda e, init=init: e.tensor_tensor_scan(
                     out=h_t[:, fm, :], data0=albuf[:, fm, :],
                     data1=at_t[:, fm, :], initial=init, op0=OP.mult,
                     op1=OP.add),
                 reads=[albuf.buf, at_t.buf] + init_reads, writes=[h_t.buf],
                 label="scan")
        for fm in range(HT):
            emit_scan(fm)
        S.dma("sp", lambda e: e.dma_start(
                  out=h_ap[:, :, c * cs:(c + 1) * cs].rearrange("t p n -> p t n"),
                  in_=h_t.full),
              reads=[h_t.buf], writes=[h_buf], label=f"st_h{c}")
    for c in range(nch):
        emit_chunk(c)



def emit_phase_c2a(ctx, cs=512):
    S = ctx.S
    nch = T // cs
    HT = 16
    xr_ap, xr_buf = ctx.dram["XR"]
    h_ap, h_buf = ctx.dram["H"]
    u2_ap, u2_buf = ctx.dram["U2"]

    wg = load_weight(ctx, "WINGT", D, 2048)
    wo = load_weight(ctx, "WOUTHT", 2048, D)
    gam1 = load_pvec(ctx, "GAM1")
    A = ctx.act
    xt = [A.carve([128, DT, cs], F32, "xt")]
    xn = [A.carve([128, DT, cs], BF16, "xn")]
    ub = A.carve([128, DT, cs], F32, "ub")
    ht = [A.carve([128, HT, cs], BF16, "ht")]
    pb = A.carve([128, HT, cs], BF16, "pb")
    xbf_rot = [A.carve([128, cs], BF16, "xbf") for _ in range(2)]
    sq_rot = [A.carve([128, cs], BF16, "sqx") for _ in range(2)]
    rows = ctx.rows
    mu_b = [A.carve([128, cs], F32, "mu_b") for _ in range(2)]
    rstd_b = [A.carve([128, cs], F32, "rstd_b") for _ in range(2)]
    pr_b = [A.carve([128, cs], F32, "pr_b") for _ in range(2)]
    tmp_rot = [A.carve([128, cs], F32, "tmp") for _ in range(2)]
    gl_rot = [A.carve([128, cs], BF16, "gl") for _ in range(2)]
    u2_rot = [A.carve([128, cs], F32, "u2o") for _ in range(4)]
    btiles = {"mu_b": mu_b, "rstd_b": rstd_b, "pr_b": pr_b}

    def emit_chunk(c):
        par = c % 2
        x_t, xn_t, h_t = xt[0], xn[0], ht[0]
        S.dma("sp", lambda e: e.dma_start(
                  out=x_t.full,
                  in_=xr_ap[:, :, c * cs:(c + 1) * cs].rearrange("t p n -> p t n")),
              reads=[xr_buf], writes=[x_t.buf], label=f"ldx{c}")
        S.dma("sp", lambda e: e.dma_start(
                  out=h_t.full,
                  in_=h_ap[:, :, c * cs:(c + 1) * cs].rearrange("t p n -> p t n")),
              reads=[h_buf], writes=[h_t.buf], label=f"ldh{c}")
        xt_par = [(x_t.buf, x_t[:, m, :]) for m in range(DT)]
        emit_norm(ctx, xt_par, cs, xbf_rot, sq_rot, rows, "ln_rms", par, btiles,
                  eps2=0.0)
        def emit_xn(m):
            tmp = tmp_rot[m % 2]
            S.op("dve", lambda e: e.tensor_tensor(
                     out=tmp.ap[:, 0:cs], in0=x_t[:, m, :],
                     in1=mu_b[par].ap[:, 0:cs], op=OP.subtract),
                 reads=[x_t.buf, mu_b[par].buf], writes=[tmp.buf], label="tmp")
            S.op("dve", lambda e: e.scalar_tensor_tensor(
                     out=xn_t[:, m, :], in0=tmp.ap[:, 0:cs],
                     scalar=gam1[:, m:m + 1], in1=pr_b[par].ap[:, 0:cs],
                     op0=OP.mult, op1=OP.mult),
                 reads=[tmp.buf, gam1.buf, pr_b[par].buf], writes=[xn_t.buf],
                 label="xn")
            S.op("dve", lambda e: e.tensor_tensor(
                     out=ub[:, m, :], in0=tmp.ap[:, 0:cs],
                     in1=rstd_b[par].ap[:, 0:cs], op=OP.mult),
                 reads=[tmp.buf, rstd_b[par].buf], writes=[ub.buf], label="u")
        for m in range(DT):
            emit_xn(m)
        xn_tiles = [(xn_t.buf, xn_t[:, k, :]) for k in range(DT)]
        for fm in range(HT):
            bank = ctx.banks[fm % 4]
            mm_group(ctx, bank, wg, (fm * 128, (fm + 1) * 128), xn_tiles, cs,
                     f"gt{fm}")
            gl = gl_rot[fm % 2]
            S.op("act", lambda e, bank=bank, gl=gl: e.activation(
                     out=gl.ap[:, 0:cs], in_=bank.ap[:, 0:cs], func=AF.Gelu),
                 reads=[bank.buf], writes=[gl.buf], label="gelu")
            S.op("dve", lambda e, fm=fm, gl=gl: e.tensor_tensor(
                     out=pb[:, fm, :], in0=gl.ap[:, 0:cs], in1=h_t[:, fm, :],
                     op=OP.mult),
                 reads=[gl.buf, h_t.buf], writes=[pb.buf], label="p")
        pb_tiles = [(pb.buf, pb[:, k, :]) for k in range(HT)]
        def emit_r(m):
            bank = ctx.banks[4 + (m % 2)]
            mm_group(ctx, bank, wo, (m * 128, (m + 1) * 128), pb_tiles, cs,
                     f"r{m}")
            u2o = u2_rot[m % 4]
            S.op("dve", lambda e: e.tensor_tensor(
                     out=u2o.ap[:, 0:cs], in0=ub[:, m, :], in1=bank.ap[:, 0:cs],
                     op=OP.add),
                 reads=[ub.buf, bank.buf], writes=[u2o.buf], label="u2")
            S.dma("sp", lambda e: e.dma_start(
                      out=u2_ap[m, :, c * cs:(c + 1) * cs], in_=u2o.ap[:, 0:cs]),
                  reads=[u2o.buf], writes=[u2_buf], label=f"st_u2{c}_{m}")
        for m in range(DT):
            emit_r(m)
    for c in range(nch):
        emit_chunk(c)



def emit_phase_c2b(ctx, cs=256):
    S = ctx.S
    nch = T // cs
    HT = 16
    xr_ap, xr_buf = ctx.dram["XR"]
    u2_ap, u2_buf = ctx.dram["U2"]

    wgr = load_weight(ctx, "WGROWT", D, 4096)
    wsh = load_weight(ctx, "WSHRT", 2048, D)
    gam2 = load_pvec(ctx, "GAM2")

    A = ctx.act
    ut = [A.carve([128, DT, cs], F32, "ut")]
    xt = [A.carve([128, DT, cs], F32, "xt")]
    xn = [A.carve([128, DT, cs], BF16, "xn")]
    tb = A.carve([128, HT, cs], BF16, "tb")
    yt = [A.carve([128, DT, cs], F32, "yt")]
    xbf_rot = [A.carve([128, cs], BF16, "xbf") for _ in range(2)]
    sq_rot = [A.carve([128, cs], BF16, "sqx") for _ in range(2)]
    rows = ctx.rows
    rms_b = [A.carve([128, cs], F32, "rms_b") for _ in range(2)]
    mu_b = [A.carve([128, cs], F32, "mu_b") for _ in range(2)]
    rstd_b = [A.carve([128, cs], F32, "rstd_b") for _ in range(2)]
    gl_rot = [A.carve([128, cs], BF16, "gl") for _ in range(2)]
    mx_rot = [A.carve([128, cs], BF16, "mx") for _ in range(2)]
    tmp_rot = [A.carve([128, cs], F32, "tmp") for _ in range(2)]
    t_rot = [A.carve([128, cs], F32, "t") for _ in range(2)]
    xo_rot = [A.carve([128, cs], F32, "xo") for _ in range(3)]

    def emit_chunk(c):
        par = c % 2
        u_t, x_t, xn_t, y_t = ut[0], xt[0], xn[0], yt[0]
        S.dma("sp", lambda e: e.dma_start(
                  out=u_t.full,
                  in_=u2_ap[:, :, c * cs:(c + 1) * cs].rearrange("t p n -> p t n")),
              reads=[u2_buf], writes=[u_t.buf], label=f"ldu{c}")
        S.dma("sp", lambda e: e.dma_start(
                  out=x_t.full,
                  in_=xr_ap[:, :, c * cs:(c + 1) * cs].rearrange("t p n -> p t n")),
              reads=[xr_buf], writes=[x_t.buf], label=f"ldx{c}")
        ut_par = [(u_t.buf, u_t[:, m, :]) for m in range(DT)]
        emit_norm(ctx, ut_par, cs, xbf_rot, sq_rot, rows, "rms", par,
                  {"rms_b": rms_b}, eps2=0.0)
        def emit_xn(m):
            S.op("dve", lambda e: e.scalar_tensor_tensor(
                     out=xn_t[:, m, :], in0=u_t[:, m, :],
                     scalar=gam2[:, m:m + 1], in1=rms_b[par].ap[:, 0:cs],
                     op0=OP.mult, op1=OP.mult),
                 reads=[u_t.buf, gam2.buf, rms_b[par].buf], writes=[xn_t.buf],
                 label="xn2")
        for m in range(DT):
            emit_xn(m)
        xn_tiles = [(xn_t.buf, xn_t[:, k, :]) for k in range(DT)]
        def emit_pair(fm):
            bM = ctx.banks[fm % 2]
            bX = ctx.banks[2 + fm % 2]
            mm_group(ctx, bM, wgr, (fm * 128, (fm + 1) * 128), xn_tiles, cs,
                     f"mg{fm}")
            mm_group(ctx, bX, wgr, (2048 + fm * 128, 2048 + (fm + 1) * 128),
                     xn_tiles, cs, f"mx{fm}")
            gl, mx = gl_rot[fm % 2], mx_rot[fm % 2]
            S.op("act", lambda e: e.activation(
                     out=gl.ap[:, 0:cs], in_=bM.ap[:, 0:cs], func=AF.Gelu),
                 reads=[bM.buf], writes=[gl.buf], label="gelu")
            S.op("dve", lambda e: e.tensor_copy(out=mx.ap[:, 0:cs],
                                                in_=bX.ap[:, 0:cs]),
                 reads=[bX.buf], writes=[mx.buf], label="mx")
            S.op("dve", lambda e: e.tensor_tensor(
                     out=tb[:, fm, :], in0=gl.ap[:, 0:cs], in1=mx.ap[:, 0:cs],
                     op=OP.mult),
                 reads=[gl.buf, mx.buf], writes=[tb.buf], label="t")
        for fm in range(HT):
            emit_pair(fm)
        tb_tiles = [(tb.buf, tb[:, k, :]) for k in range(HT)]
        def emit_y(m):
            bank = ctx.banks[4 + (m % 2)]
            mm_group(ctx, bank, wsh, (m * 128, (m + 1) * 128), tb_tiles, cs,
                     f"ml{m}")
            S.op("dve", lambda e: e.tensor_tensor(
                     out=y_t[:, m, :], in0=u_t[:, m, :], in1=bank.ap[:, 0:cs],
                     op=OP.add),
                 reads=[u_t.buf, bank.buf], writes=[y_t.buf], label="yadd")
        for m in range(DT):
            emit_y(m)
        yt_par = [(y_t.buf, y_t[:, m, :]) for m in range(DT)]
        emit_norm(ctx, yt_par, cs, xbf_rot, sq_rot, rows, "ln", par,
                  {"mu_b": mu_b, "rstd_b": rstd_b})
        def emit_xo(m):
            tmp, t_, xo = tmp_rot[m % 2], t_rot[m % 2], xo_rot[m % 3]
            S.op("dve", lambda e: e.tensor_tensor(
                     out=tmp.ap[:, 0:cs], in0=y_t[:, m, :],
                     in1=mu_b[par].ap[:, 0:cs], op=OP.subtract),
                 reads=[y_t.buf, mu_b[par].buf], writes=[tmp.buf], label="ytmp")
            S.op("dve", lambda e: e.tensor_tensor(
                     out=t_.ap[:, 0:cs], in0=tmp.ap[:, 0:cs],
                     in1=rstd_b[par].ap[:, 0:cs], op=OP.mult),
                 reads=[tmp.buf, rstd_b[par].buf], writes=[t_.buf], label="yln")
            S.op("dve", lambda e: e.tensor_tensor(
                     out=xo.ap[:, 0:cs], in0=t_.ap[:, 0:cs], in1=x_t[:, m, :],
                     op=OP.add),
                 reads=[t_.buf, x_t.buf], writes=[xo.buf], label="xo")
            S.dma("sp", lambda e: e.dma_start(
                      out=xr_ap[m, :, c * cs:(c + 1) * cs], in_=xo.ap[:, 0:cs]),
                  reads=[xo.buf], writes=[xr_buf], label=f"st_xr{c}_{m}")
        for m in range(DT):
            emit_xo(m)
    for c in range(nch):
        emit_chunk(c)



def emit_phase_d0(ctx, cs=512):
    S = ctx.S
    nch = T // cs
    xr_ap, xr_buf = ctx.dram["XR"]
    xn_ap, xn_buf = ctx.dram["XN"]
    dg_ap, dg_buf = ctx.dram["DGC"]

    rw = load_weight_f32(ctx, "RWTD", D, 8)
    lng = load_pvec(ctx, "LNG")
    lnb = load_pvec(ctx, "LNB")
    iden = get_iden(ctx)

    A = ctx.act
    xt = [A.carve([128, DT, cs], F32, "xt") for _ in range(2)]
    xnf = [A.carve([128, DT, cs], F32, "xnf")]
    xnb = [A.carve([128, DT, cs], BF16, "xnb")]
    xbf_rot = [A.carve([128, cs], BF16, "xbf") for _ in range(2)]
    sq_rot = [A.carve([128, cs], BF16, "sqx") for _ in range(2)]
    rows = ctx.rows
    mu_b = [A.carve([128, cs], F32, "mu_b") for _ in range(2)]
    rr_b = [A.carve([128, cs], F32, "rr_b") for _ in range(2)]
    tmp_rot = [A.carve([128, cs], F32, "tmp") for _ in range(2)]
    xp_rot = [A.carve([128, cs], F32, "xp") for _ in range(2)]
    lbt = [A.carve([128, cs], F32, "lbt") for _ in range(2)]
    ltk_rot = [A.carve([128, 8], F32, "ltk") for _ in range(2)]
    ebuf = A.carve([128, 8], F32, "ebuf")
    col_rot = [A.carve([128, 1], F32, f"dcol{i}") for i in range(6)]
    s8_rot = [A.carve([128, 8], F32, f"s8{i}") for i in range(5)]
    ctile_rot = [A.carve([128, 128], F32, "ctile") for _ in range(2)]
    btiles = {"mu_b": mu_b, "rr_b": rr_b}

    def emit_chunk(c):
        par = c % 2
        x_t, xn_f, xn_t, lb = xt[par], xnf[0], xnb[0], lbt[par]
        S.dma("sp", lambda e: e.dma_start(
                  out=x_t.full,
                  in_=xr_ap[:, :, c * cs:(c + 1) * cs].rearrange("t p n -> p t n")),
              reads=[xr_buf], writes=[x_t.buf], label=f"ldx{c}")
        xt_par = [(x_t.buf, x_t[:, m, :]) for m in range(DT)]
        emit_norm(ctx, xt_par, cs, xbf_rot, sq_rot, rows, "ln_affine", par,
                  btiles)
        def emit_xn(m):
            tmp, xp = tmp_rot[m % 2], xp_rot[m % 2]
            S.op("dve", lambda e: e.tensor_tensor(
                     out=tmp.ap[:, 0:cs], in0=x_t[:, m, :],
                     in1=mu_b[par].ap[:, 0:cs], op=OP.subtract),
                 reads=[x_t.buf, mu_b[par].buf], writes=[tmp.buf], label="tmp")
            S.op("dve", lambda e: e.scalar_tensor_tensor(
                     out=xp.ap[:, 0:cs], in0=tmp.ap[:, 0:cs],
                     scalar=lng[:, m:m + 1], in1=rr_b[par].ap[:, 0:cs],
                     op0=OP.mult, op1=OP.mult),
                 reads=[tmp.buf, lng.buf, rr_b[par].buf], writes=[xp.buf],
                 label="xp")
            S.op("dve", lambda e: e.tensor_scalar(
                     out=xn_f[:, m, :], in0=xp.ap[:, 0:cs],
                     scalar1=lnb[:, m:m + 1], scalar2=None, op0=OP.add),
                 reads=[xp.buf, lnb.buf], writes=[xn_f.buf], label="xnf")
            S.op("act", lambda e: e.activation(
                     out=xn_t[:, m, :], in_=xn_f[:, m, :], func=AF.Copy),
                 reads=[xn_f.buf], writes=[xn_t.buf], label="xnb")
        for m in range(DT):
            emit_xn(m)
        S.dma("sp", lambda e: e.dma_start(
                  out=xn_ap[:, :, c * cs:(c + 1) * cs].rearrange("t p n -> p t n"),
                  in_=xn_t.full),
              reads=[xn_t.buf], writes=[xn_buf], label=f"st_xn{c}")
        xnf_tiles = [(xn_f.buf, xn_f[:, k, :]) for k in range(DT)]
        b0 = ctx.banks[0]
        mm_group(ctx, b0, rw, (0, 8), xnf_tiles, cs, "routerD")
        S.op("dve", lambda e: e.tensor_scalar(
                 out=lb[0:8, 0:cs], in0=b0.ap[0:8, 0:cs], scalar1=-1e4,
                 scalar2=1e4, op0=OP.max, op1=OP.min),
             reads=[b0.buf], writes=[lb.buf], label="clip")

        def emit_j(j):
            bT = ctx.banks[1 + (j % 2)]
            S.op("pe", lambda e: e.matmul(bT.ap[:, 0:8],
                                          lb[0:8, j * 128:(j + 1) * 128],
                                          iden[0:8, 0:8], is_transpose=True,
                                          start=True, stop=True),
                 reads=[lb.buf, iden.buf], writes=[bT.buf], label="trin")
            ltk = ltk_rot[j % 2]
            S.op("act", lambda e: e.activation(out=ltk.full, in_=bT.ap[:, 0:8],
                                               func=AF.Copy),
                 reads=[bT.buf], writes=[ltk.buf], label="ltk")
            m1, s_, m1p, m2 = col_rot[0], col_rot[1], col_rot[2], col_rot[3]
            d_, prc, msk, ge2, dg = (s8_rot[i] for i in range(5))
            S.op("dve", lambda e: e.reduce_max(out=m1.full, in_=ltk.full,
                                               axis=AX.X),
                 reads=[ltk.buf], writes=[m1.buf], label="m1")
            S.op("dve", lambda e: e.tensor_scalar(
                     out=d_.full, in0=ltk.full, scalar1=m1.full, scalar2=None,
                     op0=OP.subtract),
                 reads=[ltk.buf, m1.buf], writes=[d_.buf], label="dsub")
            S.op("act", lambda e: e.activation(out=ebuf.full, in_=d_.full,
                                               func=AF.Exp),
                 reads=[d_.buf], writes=[ebuf.buf], label="eexp")
            S.op("dve", lambda e: e.reduce_sum(out=s_.full, in_=ebuf.full,
                                               axis=AX.X),
                 reads=[ebuf.buf], writes=[s_.buf], label="esum")
            S.op("dve", lambda e: e.tensor_scalar(
                     out=s_.full, in0=s_.full, scalar1=1.0, scalar2=1e-8,
                     op0=OP.mult, op1=OP.add),
                 reads=[s_.buf], writes=[s_.buf], label="den")
            S.op("dve", lambda e: e.reciprocal(out=s_.full, in_=s_.full),
                 reads=[s_.buf], writes=[s_.buf], label="rec")
            S.op("dve", lambda e: e.tensor_scalar(
                     out=prc.full, in0=ebuf.full, scalar1=s_.full, scalar2=None,
                     op0=OP.mult),
                 reads=[ebuf.buf, s_.buf], writes=[prc.buf], label="probs")
            S.op("dve", lambda e: e.tensor_scalar(
                     out=prc.full, in0=prc.full, scalar1=1e-8, scalar2=1.0,
                     op0=OP.max, op1=OP.min),
                 reads=[prc.buf], writes=[prc.buf], label="clipp")
            S.op("dve", lambda e: e.reduce_max(out=m1p.full, in_=prc.full,
                                               axis=AX.X),
                 reads=[prc.buf], writes=[m1p.buf], label="m1p")
            S.op("dve", lambda e: e.tensor_scalar(
                     out=msk.full, in0=prc.full, scalar1=m1p.full, scalar2=None,
                     op0=OP.subtract),
                 reads=[prc.buf, m1p.buf], writes=[msk.buf], label="d1")
            S.op("dve", lambda e: e.tensor_scalar(
                     out=msk.full, in0=msk.full, scalar1=0.0, scalar2=None,
                     op0=OP.is_ge),
                 reads=[msk.buf], writes=[msk.buf], label="ge1")
            S.op("dve", lambda e: e.scalar_tensor_tensor(
                     out=msk.full, in0=msk.full, scalar=-1e30, in1=prc.full,
                     op0=OP.mult, op1=OP.add),
                 reads=[msk.buf, prc.buf], writes=[msk.buf], label="msk")
            S.op("dve", lambda e: e.reduce_max(out=m2.full, in_=msk.full,
                                               axis=AX.X),
                 reads=[msk.buf], writes=[m2.buf], label="m2")
            S.op("dve", lambda e: e.tensor_scalar(
                     out=ge2.full, in0=prc.full, scalar1=m2.full, scalar2=None,
                     op0=OP.subtract),
                 reads=[prc.buf, m2.buf], writes=[ge2.buf], label="d2")
            S.op("dve", lambda e: e.tensor_scalar(
                     out=ge2.full, in0=ge2.full, scalar1=0.0, scalar2=None,
                     op0=OP.is_ge),
                 reads=[ge2.buf], writes=[ge2.buf], label="ge2")
            S.op("dve", lambda e: e.tensor_tensor(
                     out=dg.full, in0=prc.full, in1=ge2.full, op=OP.mult),
                 reads=[prc.buf, ge2.buf], writes=[dg.buf], label="dg")
            bT2 = ctx.banks[3 + (j % 2)]
            S.op("pe", lambda e: e.matmul(bT2.ap[0:8, 0:128], dg.full,
                                          iden.full, is_transpose=True,
                                          start=True, stop=True),
                 reads=[dg.buf, iden.buf], writes=[bT2.buf], label="trout")
            ctile = ctile_rot[j % 2]
            S.op("act", lambda e: e.activation(
                     out=ctile.ap[0:8, 0:128], in_=bT2.ap[0:8, 0:128],
                     func=AF.Copy),
                 reads=[bT2.buf], writes=[ctile.buf], label="dgc")
            S.dma("sp", lambda e: e.dma_start(
                      out=dg_ap[0:8, c * cs + j * 128: c * cs + (j + 1) * 128],
                      in_=ctile.ap[0:8, 0:128]),
                  reads=[ctile.buf], writes=[dg_buf], label=f"stdg{c}_{j}")
        for j in range(cs // 128):
            emit_j(j)
    for c in range(nch):
        emit_chunk(c)



def emit_phase_dexp(ctx, e_, first, cs=512):
    S = ctx.S
    nch = T // cs
    HT = 16
    xn_ap, xn_buf = ctx.dram["XN"]
    dg_ap, dg_buf = ctx.dram["DGC"]
    acc_ap, acc_buf = ctx.dram["ACC"]

    wgu = load_weight(ctx, f"DGUT{e_}", D, 4096)
    wdn = load_weight(ctx, f"DDT{e_}", 2048, D)

    A = ctx.act
    xnb = [A.carve([128, DT, cs], BF16, "xnb") for _ in range(2)]
    xeb = A.carve([128, DT, cs], BF16, "xeb")
    dgb = [A.carve([128, cs], BF16, "dgb") for _ in range(2)]
    hbuf = A.carve([128, HT, cs], BF16, "hbuf")
    sil_rot = [A.carve([128, cs], BF16, "sil") for _ in range(2)]
    upe_rot = [A.carve([128, cs], BF16, "upe") for _ in range(2)]
    res_rot = [A.carve([128, cs], F32, "res") for _ in range(4)]

    def emit_chunk(c):
        par = c % 2
        xn_t, dg_t = xnb[par], dgb[par]
        S.dma("sp", lambda e: e.dma_start(
                  out=xn_t.full,
                  in_=xn_ap[:, :, c * cs:(c + 1) * cs].rearrange("t p n -> p t n")),
              reads=[xn_buf], writes=[xn_t.buf], label=f"ldxn{c}")
        S.dma("pool", lambda e: e.dma_start(
                  out=dg_t.ap[:, 0:cs],
                  in_=dg_ap[e_, c * cs:(c + 1) * cs].partition_broadcast(128)),
              reads=[dg_buf], writes=[dg_t.buf], label=f"lddg{c}")
        def emit_xe(m):
            S.op("dve", lambda e: e.tensor_tensor(
                     out=xeb[:, m, :], in0=xn_t[:, m, :], in1=dg_t.ap[:, 0:cs],
                     op=OP.mult),
                 reads=[xn_t.buf, dg_t.buf], writes=[xeb.buf], label="xe")
        for m in range(DT):
            emit_xe(m)
        xe_tiles = [(xeb.buf, xeb[:, k, :]) for k in range(DT)]
        def emit_h(fm):
            bG = ctx.banks[fm % 2]
            bU = ctx.banks[2 + fm % 2]
            mm_group(ctx, bG, wgu, (fm * 128, (fm + 1) * 128), xe_tiles, cs,
                     f"dG{fm}")
            mm_group(ctx, bU, wgu, (2048 + fm * 128, 2048 + (fm + 1) * 128),
                     xe_tiles, cs, f"dU{fm}")
            sil, upe = sil_rot[fm % 2], upe_rot[fm % 2]
            S.op("act", lambda e: e.activation(
                     out=sil.ap[:, 0:cs], in_=bG.ap[:, 0:cs], func=AF.Silu),
                 reads=[bG.buf], writes=[sil.buf], label="silu")
            S.op("dve", lambda e: e.tensor_copy(out=upe.ap[:, 0:cs],
                                                in_=bU.ap[:, 0:cs]),
                 reads=[bU.buf], writes=[upe.buf], label="upe")
            S.op("dve", lambda e: e.tensor_tensor(
                     out=hbuf[:, fm, :], in0=sil.ap[:, 0:cs],
                     in1=upe.ap[:, 0:cs], op=OP.mult),
                 reads=[sil.buf, upe.buf], writes=[hbuf.buf], label="hb")
        for fm in range(HT):
            emit_h(fm)
        hb_tiles = [(hbuf.buf, hbuf[:, k, :]) for k in range(HT)]
        def emit_down(m):
            bank = ctx.banks[4 + (m % 2)]
            mm_group(ctx, bank, wdn, (m * 128, (m + 1) * 128), hb_tiles, cs,
                     f"dD{m}")
            res = res_rot[m % 4]
            S.op("act", lambda e: e.activation(out=res.ap[:, 0:cs],
                                               in_=bank.ap[:, 0:cs],
                                               func=AF.Copy),
                 reads=[bank.buf], writes=[res.buf], label="res")
            if first:
                S.dma("pool", lambda e: e.dma_start(
                          out=acc_ap[m, :, c * cs:(c + 1) * cs],
                          in_=res.ap[:, 0:cs]),
                      reads=[res.buf], writes=[acc_buf], label=f"stacc{c}_{m}")
            else:
                S.dma("pool", lambda e: e.dma_start(
                          out=acc_ap[m, :, c * cs:(c + 1) * cs],
                          in_=res.ap[:, 0:cs], accum_op=OP.add),
                      reads=[res.buf], writes=[acc_buf], label=f"acacc{c}_{m}")
        for m in range(DT):
            emit_down(m)
    for c in range(nch):
        emit_chunk(c)



def emit_phase_df(ctx, cs=512):
    S = ctx.S
    nch = T // cs
    xr_ap, xr_buf = ctx.dram["XR"]
    acc_ap, acc_buf = ctx.dram["ACC"]
    y_ap, y_buf = ctx.dram["Y"]

    A = ctx.act
    xt = [A.carve([128, DT, cs], F32, "xt") for _ in range(2)]
    acct = [A.carve([128, DT, cs], F32, "acct")]
    yt = [A.carve([128, DT, cs], F32, "yt")]
    xbf_rot = [A.carve([128, cs], BF16, "xbf") for _ in range(2)]
    sq_rot = [A.carve([128, cs], BF16, "sqx") for _ in range(2)]
    rows = ctx.rows
    mu_b = [A.carve([128, cs], F32, "mu_b") for _ in range(2)]
    rstd_b = [A.carve([128, cs], F32, "rstd_b") for _ in range(2)]
    mu2_b = [A.carve([128, cs], F32, "mu2_b") for _ in range(2)]
    rstd2_b = [A.carve([128, cs], F32, "rstd2_b") for _ in range(2)]
    tmp_rot = [A.carve([128, cs], F32, "tmp") for _ in range(2)]
    u_rot = [A.carve([128, cs], F32, "u") for _ in range(2)]
    t_rot = [A.carve([128, cs], F32, "t") for _ in range(2)]
    xo_rot = [A.carve([128, cs], F32, "xo") for _ in range(3)]

    def emit_chunk(c):
        par = c % 2
        x_t, acc_t, y_t = xt[par], acct[0], yt[0]
        S.dma("sp", lambda e: e.dma_start(
                  out=x_t.full,
                  in_=xr_ap[:, :, c * cs:(c + 1) * cs].rearrange("t p n -> p t n")),
              reads=[xr_buf], writes=[x_t.buf], label=f"ldx{c}")
        S.dma("sp", lambda e: e.dma_start(
                  out=acc_t.full,
                  in_=acc_ap[:, :, c * cs:(c + 1) * cs].rearrange("t p n -> p t n")),
              reads=[acc_buf], writes=[acc_t.buf], label=f"ldacc{c}")
        xt_par = [(x_t.buf, x_t[:, m, :]) for m in range(DT)]
        emit_norm(ctx, xt_par, cs, xbf_rot, sq_rot, rows, "ln", par,
                  {"mu_b": mu_b, "rstd_b": rstd_b})
        def emit_y(m):
            tmp, u = tmp_rot[m % 2], u_rot[m % 2]
            S.op("dve", lambda e: e.tensor_tensor(
                     out=tmp.ap[:, 0:cs], in0=x_t[:, m, :],
                     in1=mu_b[par].ap[:, 0:cs], op=OP.subtract),
                 reads=[x_t.buf, mu_b[par].buf], writes=[tmp.buf], label="tmp")
            S.op("dve", lambda e: e.tensor_tensor(
                     out=u.ap[:, 0:cs], in0=tmp.ap[:, 0:cs],
                     in1=rstd_b[par].ap[:, 0:cs], op=OP.mult),
                 reads=[tmp.buf, rstd_b[par].buf], writes=[u.buf], label="u")
            S.op("dve", lambda e: e.tensor_tensor(
                     out=y_t[:, m, :], in0=u.ap[:, 0:cs], in1=acc_t[:, m, :],
                     op=OP.add),
                 reads=[u.buf, acc_t.buf], writes=[y_t.buf], label="y")
        for m in range(DT):
            emit_y(m)
        yt_par = [(y_t.buf, y_t[:, m, :]) for m in range(DT)]
        emit_norm(ctx, yt_par, cs, xbf_rot, sq_rot, rows, "ln", par,
                  {"mu_b": mu2_b, "rstd_b": rstd2_b})
        def emit_xo(m):
            tmp, t_, xo = tmp_rot[m % 2], t_rot[m % 2], xo_rot[m % 3]
            S.op("dve", lambda e: e.tensor_tensor(
                     out=tmp.ap[:, 0:cs], in0=y_t[:, m, :],
                     in1=mu2_b[par].ap[:, 0:cs], op=OP.subtract),
                 reads=[y_t.buf, mu2_b[par].buf], writes=[tmp.buf], label="ytmp")
            S.op("dve", lambda e: e.tensor_tensor(
                     out=t_.ap[:, 0:cs], in0=tmp.ap[:, 0:cs],
                     in1=rstd2_b[par].ap[:, 0:cs], op=OP.mult),
                 reads=[tmp.buf, rstd2_b[par].buf], writes=[t_.buf], label="yln")
            S.op("dve", lambda e: e.tensor_tensor(
                     out=xo.ap[:, 0:cs], in0=t_.ap[:, 0:cs], in1=x_t[:, m, :],
                     op=OP.add),
                 reads=[t_.buf, x_t.buf], writes=[xo.buf], label="xo")
            S.dma("sp", lambda e: e.dma_start(
                      out=y_ap[m, :, c * cs:(c + 1) * cs], in_=xo.ap[:, 0:cs]),
                  reads=[xo.buf], writes=[y_buf], label=f"st_y{c}_{m}")
        for m in range(DT):
            emit_xo(m)
    for c in range(nch):
        emit_chunk(c)




def fm(x2, t):
    return np.ascontiguousarray(x2.T.reshape(8, 128, t))


def wpack(wt):
    kdim, mdim = wt.shape
    return np.ascontiguousarray(wt.reshape(kdim // 128, 128, mdim)
                                .transpose(1, 0, 2))


def build_inmap(p, xcore, t):
    A = lambda k: np.asarray(p[k])
    o = {
        "XIN": fm(xcore, t),
        "IDEN": np.eye(128, dtype=np.float32),
        "WGT": wpack(np.ascontiguousarray(A("L0_attn_Wg").T)),
        "WVT": wpack(np.ascontiguousarray(A("L0_attn_Wv").T)),
        "WDT": wpack(np.ascontiguousarray(A("L0_attn_Wd").T)),
        "BG": A("L0_attn_bg").reshape(8, 128),
        "BV": A("L0_attn_bv").reshape(8, 128),
        "BD": A("L0_attn_bd").reshape(8, 128),
        "RMSMIXW": A("L0_attn_rms_mix_w").reshape(8, 128),
        "RMSFFNW": A("L0_attn_rms_ffn_w").reshape(8, 128),
        "WGATET": wpack(np.ascontiguousarray(A("L0_attn_W_gate").T)),
        "WUPT": wpack(np.ascontiguousarray(A("L0_attn_W_up").T)),
        "WOUTT": wpack(np.ascontiguousarray(A("L0_attn_W_out").T)),
        "RWTB": wpack(np.ascontiguousarray(A("L0_ffn_router_w").T)),
        "RBB": A("L0_ffn_router_bias").reshape(1, 16),
        "SHGT": wpack(np.ascontiguousarray(A("L0_ffn_sh_gate").T)),
        "SHUT": wpack(np.ascontiguousarray(A("L0_ffn_sh_up").T)),
        "SHDT": wpack(np.ascontiguousarray(A("L0_ffn_sh_down").T)),
        "WIXCT": wpack(np.ascontiguousarray(A("L1_attn_W_in")[2048:].T)),
        "WINGT": wpack(np.ascontiguousarray(A("L1_attn_W_in")[:2048].T)),
        "CONVB": A("L1_attn_conv_b").reshape(16, 128),
        "WGATESFT": wpack(np.ascontiguousarray(A("L1_attn_W_gates")[:2048].T)),
        "WGATESIT": wpack(np.ascontiguousarray(A("L1_attn_W_gates")[2048:].T)),
        "BGF": A("L1_attn_b_gates")[:2048].reshape(16, 128),
        "BGI": A("L1_attn_b_gates")[2048:].reshape(16, 128),
        "FB": A("L1_attn_forget_base").reshape(16, 128),
        "WOUTHT": wpack(np.ascontiguousarray(A("L1_attn_W_out").T)),
        "GAM1": A("L1_attn_gamma1").reshape(8, 128),
        "GAM2": A("L1_attn_gamma2").reshape(8, 128),
        "WGROWT": wpack(np.ascontiguousarray(A("L1_attn_W_grow").T)),
        "WSHRT": wpack(np.ascontiguousarray(A("L1_attn_W_shrink").T)),
        "LNG": A("L1_ffn_ln_gamma").reshape(8, 128),
        "LNB": A("L1_ffn_ln_beta").reshape(8, 128),
        "RWTD": wpack(np.ascontiguousarray(A("L1_ffn_router_w").T)),
    }
    up = A("L0_ffn_up_proj")
    gp = A("L0_ffn_gate_proj")
    dp = A("L0_ffn_down_proj")
    for e in range(16):
        o[f"BGATE{e}"] = wpack(gp[e])
        o[f"BUP{e}"] = wpack(up[e])
        o[f"BDOWN{e}"] = wpack(dp[e])
    cw = A("L1_attn_conv_w")[:, 0, :]
    for k in range(4):
        o[f"CONVW{k}"] = np.ascontiguousarray(cw[:, k].reshape(16, 128))
    gu = A("L1_ffn_gate_up_w")
    dw = A("L1_ffn_down_w")
    for e in range(8):
        o[f"DGUT{e}"] = wpack(np.ascontiguousarray(gu[e].T))
        o[f"DDT{e}"] = wpack(np.ascontiguousarray(dw[e].T))
    return o


_PROG_CACHE = {}


def _build_program():
    if "nc" in _PROG_CACHE:
        return _PROG_CACHE["nc"]
    set_geometry(4096, 2048)
    nc = bass.Bass("TRN2", target_bir_lowering=False)
    ctx = Ctx(nc)
    ctx.dram_in("XIN", [8, 128, T])
    ctx.dram_out("Y", [8, 128, T])
    for name, shape, dt in (
            ("U2", [8, 128, T], F32), ("GU", [24, 128, T], BF16),
            ("XR", [8, 128, T], F32), ("UB", [8, 128, T], BF16),
            ("ACC", [8, 128, T], F32), ("COMB", [16, T], F32),
            ("XC", [16, 128, T], F32), ("AT", [16, 128, T], F32),
            ("H", [16, 128, T], BF16), ("XN", [8, 128, T], BF16),
            ("DGC", [8, T], F32)):
        ctx.dram_scratch(name, shape, dt)
    phases = [
        emit_phase_a1, emit_phase_a2a, emit_phase_a2b,
        emit_phase_b0,
        lambda c: emit_phase_bexp(c, [0, 1, 2, 3], True),
        lambda c: emit_phase_bexp(c, [4, 5, 6, 7], False),
        lambda c: emit_phase_bexp(c, [8, 9, 10, 11], False),
        lambda c: emit_phase_bexp(c, [12, 13, 14, 15], False),
        emit_phase_b5,
        emit_phase_c0, emit_phase_c1f, emit_phase_c1i,
        emit_phase_c2a, emit_phase_c2b,
        emit_phase_d0,
    ] + [(lambda e: lambda c: emit_phase_dexp(c, e, e == 0))(e)
         for e in range(8)] + [emit_phase_df]
    for fn in phases:
        fn(ctx)
        ctx.big.reset()
    ctx.S.finalize()
    _PROG_CACHE["nc"] = nc
    return nc


def kernel(**inputs):
    nc = _build_program()
    x = np.asarray(inputs["L0_attn_x"])
    in_maps = []
    for core in range(8):
        xc = np.ascontiguousarray(x[2 * core: 2 * core + 2]).reshape(T, D)
        in_maps.append(build_inmap(inputs, xc, T))
    res = run_bass_kernel_spmd(nc, in_maps, core_ids=list(range(8)))
    outs = []
    for core in range(8):
        yf = np.asarray(res.results[core]["Y"])
        outs.append(yf.reshape(D, T).T.reshape(2, 2048, D))
    return np.concatenate(outs, axis=0)


# version 05
# speedup vs baseline: 1.0740x; 1.0740x over previous
import numpy as np
import concourse.bass as bass
import concourse.mybir as mybir
from concourse.bass_utils import run_bass_kernel_spmd



F32 = mybir.dt.float32
BF16 = mybir.dt.bfloat16
AF = mybir.ActivationFunctionType
OP = mybir.AluOpType
AX = mybir.AxisListType

QUEUES = ("pe", "dve", "act", "pool", "sp")
DMA_BASE = {"sp": "dsp", "pool": "dpool", "act": "dact"}
R_SLOTS = 12


class _Op:
    __slots__ = ("fn", "queue", "domain", "waits", "ms", "seq", "snapshot",
                 "milestoned", "label")

    def __init__(self, fn, queue, domain, seq, label=""):
        self.fn = fn
        self.queue = queue
        self.domain = domain
        self.waits = []
        self.ms = None
        self.seq = seq
        self.snapshot = None
        self.milestoned = False
        self.label = label


class Buf:
    __slots__ = ("key", "writer", "readers")
    _n = 0

    def __init__(self, key=None):
        if key is None:
            Buf._n += 1
            key = f"_b{Buf._n}"
        self.key = key
        self.writer = None
        self.readers = {}


class Sched:
    def __init__(self, nc):
        self.nc = nc
        self._ctx = []
        self.sems = {}
        self.dma_domains = []
        names = ["pe", "dve", "act", "pool"]
        for q, base in DMA_BASE.items():
            for s in range(R_SLOTS):
                names.append(f"{base}{s}")
                self.dma_domains.append(f"{base}{s}")
        for d in names:
            cm = nc.semaphore("sem_" + d)
            self._ctx.append(cm)
            self.sems[d] = cm.__enter__()
        self.dma_seq = {q: 0 for q in DMA_BASE}
        self.last_dma = {}
        self.ops = {q: [] for q in QUEUES}
        self.count = {d: 0 for d in names}
        self.drained_until = {q: 0 for q in QUEUES}
        self._pending_drain = {q: False for q in QUEUES}
        self.n_drains = 0
        self.drain_sites = {}
        self._drain_blame = (None, None)
        self.know = {q: {} for q in QUEUES}
        self.snap = {}
        self.n_waits = 0
        self.is_dma_domain = {d: (d not in QUEUES) for d in names}
        for q in QUEUES:
            self.is_dma_domain.setdefault(q, False)

    def _need(self, target_q, dep_op, pending, raw=True):
        if dep_op.domain == target_q:
            if raw and target_q != "pe" and                     dep_op.seq >= self.drained_until[target_q]:
                self._pending_drain[target_q] = True
                self._drain_blame = (dep_op.label, target_q)
            return
        if self.know[target_q].get(dep_op.domain, -1) >= dep_op.seq:
            return
        dep_op.milestoned = True
        k = dict(self.know[target_q])
        k[dep_op.domain] = dep_op.seq
        sn = dep_op.snapshot
        if sn:
            for d, v in sn.items():
                if v > k.get(d, -1):
                    k[d] = v
        self.know[target_q] = k
        pending.append(dep_op)

    def op(self, queue, fn, reads=(), writes=(), label=""):
        return self._op(queue, queue, fn, reads, writes, label)

    def dma(self, queue, fn, reads=(), writes=(), label=""):
        i = self.dma_seq[queue]
        self.dma_seq[queue] = i + 1
        domain = f"{DMA_BASE[queue]}{i % R_SLOTS}"
        return self._op(queue, domain, fn, reads, writes, label)

    def _op(self, queue, domain, fn, reads, writes, label):
        self._pending_drain[queue] = False
        op = _Op(fn, queue, domain, len(self.ops[queue]), label)
        pending = []
        for b in reads:
            if b.writer is not None:
                self._need(queue, b.writer, pending, raw=True)
        for b in writes:
            if b.writer is not None:
                self._need(queue, b.writer, pending, raw=False)
            for r in b.readers.values():
                self._need(queue, r, pending, raw=False)
        if self._pending_drain[queue]:
            key = (queue, label, self._drain_blame[0])
            self.drain_sites[key] = self.drain_sites.get(key, 0) + 1
            dr = _Op(lambda e: e.drain(), queue, queue, len(self.ops[queue]),
                     "drain")
            dr.snapshot = self.know[queue]
            self.ops[queue].append(dr)
            self.drained_until[queue] = dr.seq
            self.n_drains += 1
            op.seq = len(self.ops[queue])
        if domain != queue:
            prev = self.last_dma.get(domain)
            if prev is not None:
                self._need(queue, prev, pending)
            self.last_dma[domain] = op
            op.milestoned = True
        wmax = {}
        for dep in pending:
            cur = wmax.get(dep.domain)
            if cur is None or dep.seq > cur.seq:
                wmax[dep.domain] = dep
        op.waits = sorted(wmax.values(), key=lambda o: o.domain)
        self.n_waits += len(op.waits)
        op.snapshot = self.know[queue]
        self.ops[queue].append(op)
        for b in writes:
            b.writer = op
            b.readers = {}
        for b in reads:
            b.readers[op.domain] = op
        return op

    def finalize(self):
        nc = self.nc
        for q in QUEUES:
            for op in self.ops[q]:
                if op.milestoned:
                    self.count[op.domain] += 1
                    op.ms = self.count[op.domain]

        def run_queue(qname):
            def body(eng):
                for op in self.ops[qname]:
                    for dep in op.waits:
                        amt = 16 if self.is_dma_domain[dep.domain] else 1
                        assert dep.ms is not None, dep.label
                        eng.wait_ge(self.sems[dep.domain], dep.ms * amt)
                    if op.fn is None:
                        continue
                    inst = op.fn(eng)
                    if op.ms is not None:
                        amt = 16 if self.is_dma_domain[op.domain] else 1
                        inst.then_inc(self.sems[op.domain], amt)
                if qname == "sp":
                    for d in self.dma_domains:
                        if self.count[d]:
                            eng.wait_ge(self.sems[d], 16 * self.count[d])
            return body

        with nc.Block() as block:
            block.tensor(run_queue("pe"))
            block.vector(run_queue("dve"))
            block.scalar(run_queue("act"))
            block.gpsimd(run_queue("pool"))
            block.sync(run_queue("sp"))

    def close(self):
        for cm in reversed(self._ctx):
            cm.__exit__(None, None, None)

    def stats(self):
        d = {q: len(self.ops[q]) for q in QUEUES}
        d["milestones"] = sum(self.count.values())
        d["waits"] = self.n_waits
        return d


class Tile:
    __slots__ = ("ap", "buf", "shape", "dtype")

    def __init__(self, ap, buf, shape, dtype):
        self.ap = ap
        self.buf = buf
        self.shape = shape
        self.dtype = dtype

    def __getitem__(self, idx):
        return self.ap[idx]

    @property
    def full(self):
        return self.ap[:]


class Arena:
    def __init__(self, nc, name, bytes_per_partition):
        self.nbytes = bytes_per_partition
        assert bytes_per_partition % 4 == 0
        self._cm = nc.sbuf_tensor(name, [128, bytes_per_partition // 4], F32)
        self.t = self._cm.__enter__()
        self.off = 0
        self.name = name
        self.carves = []
        self.reset_deps = []
        self.hiwater = 0

    def carve(self, shape, dtype=F32, name=""):
        assert shape[0] == 128
        esz = 2 if dtype == BF16 else 4
        n = 1
        for s in shape[1:]:
            n *= s
        nbytes = ((n * esz + 63) // 64) * 64
        assert self.off + nbytes <= self.nbytes, \
            f"arena {self.name} overflow: {self.off}+{nbytes} > {self.nbytes} ({name})"
        off_f32 = self.off // 4
        ap = self.t[:, off_f32: off_f32 + (n * esz + 3) // 4]
        if dtype != F32:
            ap = ap.bitcast(dtype)
            ap = ap[:, 0:n]
        if len(shape) > 2:
            dims = {f"d{i}": shape[i + 1] for i in range(len(shape) - 1)}
            pat = ("p (" + " ".join(dims) + ") -> p " + " ".join(dims))
            ap = ap.rearrange(pat, **dims)
        self.off += nbytes
        self.hiwater = max(self.hiwater, self.off)
        b = Buf(f"{self.name}:{self.off}:{name}")
        for i, dep in enumerate(self.reset_deps):
            b.readers[f"__reset{i}_{dep.domain}"] = dep
        self.carves.append(b)
        return Tile(ap, b, shape, dtype)

    def reset(self):
        last = {}
        for b in self.carves:
            cands = list(b.readers.values())
            if b.writer is not None:
                cands.append(b.writer)
            for o in cands:
                cur = last.get((o.queue, o.domain))
                if cur is None or o.seq > cur.seq:
                    last[(o.queue, o.domain)] = o
        self.reset_deps = list(last.values())
        self.carves = []
        self.off = 0

    def free(self):
        self._cm.__exit__(None, None, None)



D = 1024
DT = 8
T = 4096
SEQ = 2048


def set_geometry(t=4096, seq=2048):
    global T, SEQ
    T = t
    SEQ = seq


class Ctx:
    def __init__(self, nc, perm_kb=34, big_kb=172):
        self.nc = nc
        self.S = Sched(nc)
        self.perm = Arena(nc, "perm", perm_kb * 1024)
        self.big = Arena(nc, "big", big_kb * 1024)
        self.wa = self.big
        self.act = self.big
        self._psum_cm = []
        self.banks = []
        for i in range(8):
            cm = nc.psum_tensor(f"bank{i}", [128, 512], F32)
            self._psum_cm.append(cm)
            t = cm.__enter__()
            self.banks.append(Tile(t[:], Buf(f"psum{i}"), [128, 512], F32))
        self.dram = {}
        self.ones_bf = self.perm.carve([128, 1], BF16, "ones_bf")
        self.ones_f = self.perm.carve([128, 1], F32, "ones_f")
        self.ones_row_f = self.perm.carve([128, 128], F32, "ones_row_f")
        self.S.op("dve", lambda v: v.memset(self.ones_bf.full, 1.0),
                  writes=[self.ones_bf.buf], label="ones_bf")
        self.S.op("dve", lambda v: v.memset(self.ones_f.full, 1.0),
                  writes=[self.ones_f.buf], label="ones_f")
        self.S.op("dve", lambda v: v.memset(self.ones_row_f[0:1, :], 1.0),
                  writes=[self.ones_row_f.buf], label="ones_row")
        self._consts = {}
        self.rows = self.perm.carve([128, 6 * 512], F32, "rows")

    def constant(self, value):
        if value not in self._consts:
            t = self.perm.carve([128, 1], F32, f"c{value}")
            self.S.op("dve", lambda v, t=t, value=value: v.memset(t.full, value),
                      writes=[t.buf], label="const")
            self._consts[value] = t
        return self._consts[value]

    def dram_in(self, name, shape, dtype=F32):
        ap = self.nc.dram_tensor(name, list(shape), dtype, kind="ExternalInput").ap()
        self.dram[name] = (ap, Buf("dram_" + name))
        return self.dram[name]

    def dram_out(self, name, shape, dtype=F32):
        ap = self.nc.dram_tensor(name, list(shape), dtype, kind="ExternalOutput").ap()
        self.dram[name] = (ap, Buf("dram_" + name))
        return self.dram[name]

    def dram_scratch(self, name, shape, dtype=F32, external=False):
        kind = "ExternalOutput" if external else "Internal"
        ap = self.nc.dram_tensor(name, list(shape), dtype, kind=kind).ap()
        self.dram[name] = (ap, Buf("dram_" + name))
        return self.dram[name]

    def close(self):
        self.S.close()
        self.big.free()
        self.perm.free()
        for cm in reversed(self._psum_cm):
            cm.__exit__(None, None, None)



def load_pvec(ctx, name, n=D):
    if not hasattr(ctx, "_pvecs"):
        ctx._pvecs = {}
    if name in ctx._pvecs:
        return ctx._pvecs[name]
    nt = n // 128
    ap, buf = ctx.dram_in(name, [nt, 128])
    tile = ctx.perm.carve([128, nt], F32, name)
    def f(eng):
        return eng.dma_start(out=tile.full, in_=ap.rearrange("t p -> p t"),
                             allow_slow_non_contiguous=True)
    ctx.S.dma("sp", f, reads=[buf], writes=[tile.buf], label="ld_" + name)
    ctx._pvecs[name] = tile
    return tile


WEIGHT_MANIFEST = (
    [("WGT", D, D), ("WVT", D, D), ("WDT", D, D),
     ("WGATET", D, 3072), ("WUPT", D, 3072), ("WOUTT", 3072, D)]
    + [(f"B{p}{e}", D if p != "DOWN" else 512, 512 if p != "DOWN" else D)
       for e in range(16) for p in ("GATE", "UP", "DOWN")]
    + [("SHGT", D, D), ("SHUT", D, D), ("SHDT", D, D),
       ("WIXCT", D, 2048), ("WGATESFT", 2048, 2048), ("WGATESIT", 2048, 2048),
       ("WINGT", D, 2048), ("WOUTHT", 2048, D),
       ("WGROWT", D, 4096), ("WSHRT", 2048, D)]
    + [(f"DGUT{e}", D, 4096) for e in range(8)]
    + [(f"DDT{e}", 2048, D) for e in range(8)]
)


def emit_weight_prep(ctx):
    ctx.wbf16 = {}
    for name, kdim, mdim in WEIGHT_MANIFEST:
        kt = kdim // 128
        ap, buf = ctx.dram_in(name, [128, kt, mdim])
        bap, bbuf = ctx.dram_scratch(name + "_bf", [128, kt, mdim], BF16)
        ctx.S.dma("pool", lambda e, ap=ap, bap=bap: e.dma_start(out=bap[:],
                                                               in_=ap[:]),
                  reads=[buf], writes=[bbuf], label="prep_" + name)
        ctx.wbf16[name] = (bap, bbuf)


def load_weight(ctx, name, kdim, mdim, arena=None, dram_name=None):
    arena = arena or ctx.wa
    kt = kdim // 128
    dram_name = dram_name or name
    tile = arena.carve([128, kt, mdim], BF16, name)
    if hasattr(ctx, "wbf16") and dram_name in ctx.wbf16:
        ap, buf = ctx.wbf16[dram_name]
    elif dram_name in ctx.dram:
        ap, buf = ctx.dram[dram_name]
    else:
        ap, buf = ctx.dram_in(dram_name, [128, kt, mdim])
    def f(eng):
        return eng.dma_start(out=tile.full, in_=ap[:])
    q = "sp" if (hasattr(ctx, "wbf16") and dram_name in ctx.wbf16) else "pool"
    ctx.S.dma(q, f, reads=[buf], writes=[tile.buf], label="ldw_" + name)
    return tile


def load_weight_f32(ctx, name, kdim, mdim, arena=None):
    kt = kdim // 128
    arena = arena or ctx.wa
    if name in ctx.dram:
        ap, buf = ctx.dram[name]
    else:
        ap, buf = ctx.dram_in(name, [128, kt, mdim])
    tile = arena.carve([128, kt, mdim], F32, name)
    ctx.S.dma("sp", lambda e: e.dma_start(out=tile.full, in_=ap[:]),
              reads=[buf], writes=[tile.buf], label="ldwf_" + name)
    return tile


def bcast_row(ctx, row_ap, row_buf, bank, width=512, label="bcast"):
    S = ctx.S
    onr = ctx.ones_row_f
    def f(eng):
        return eng.matmul(bank.ap[:, 0:width], onr[0:1, :], row_ap,
                          start=True, stop=True)
    S.op("pe", f, reads=[row_buf, onr.buf], writes=[bank.buf], label=label)


def mm_group(ctx, bank, w_tile, m_slice, rhs_tiles, n=512, label="mm",
             accumulate=False, final=True):
    S = ctx.S
    kt = w_tile.shape[1]
    m0, m1 = m_slice
    def f(eng):
        mm = None
        for k in range(kt):
            mm = eng.matmul(bank.ap[0:m1 - m0, 0:n], w_tile[:, k, m0:m1],
                            rhs_tiles[k][1], start=(k == 0 and not accumulate),
                            stop=(k == kt - 1 and final))
        return mm
    S.op("pe", f, reads=[w_tile.buf] + [rb for rb, _ in rhs_tiles],
         writes=[bank.buf], label=label)



def emit_stats(ctx, xt_par, cs, xbf_rot, sq_rot, ntiles=DT, with_s1=True):
    S = ctx.S
    b6, b7 = ctx.banks[6], ctx.banks[7]
    for m in range(ntiles):
        xb = xbf_rot[m % len(xbf_rot)]
        sq = sq_rot[m % len(sq_rot)]
        xbuf, xap = xt_par[m]
        S.op("act", lambda e, xap=xap, xb=xb, cs=cs:
             e.activation(out=xb.ap[:, 0:cs], in_=xap, func=AF.Copy),
             reads=[xbuf], writes=[xb.buf], label="cast_x")
        S.op("dve", lambda e, xb=xb, sq=sq, cs=cs:
             e.tensor_tensor(out=sq.ap[:, 0:cs], in0=xb.ap[:, 0:cs],
                             in1=xb.ap[:, 0:cs], op=OP.mult),
             reads=[xb.buf], writes=[sq.buf], label="sq_x")
        if with_s1:
            S.op("pe", lambda e, xb=xb, m=m, cs=cs:
                 e.matmul(b6.ap[0:1, 0:cs], ctx.ones_bf.full, xb.ap[:, 0:cs],
                          start=(m == 0), stop=(m == ntiles - 1)),
                 reads=[xb.buf, ctx.ones_bf.buf], writes=[b6.buf], label="S1mm")
        S.op("pe", lambda e, sq=sq, m=m, cs=cs:
             e.matmul(b7.ap[0:1, 0:cs], ctx.ones_bf.full, sq.ap[:, 0:cs],
                      start=(m == 0), stop=(m == ntiles - 1)),
             reads=[sq.buf, ctx.ones_bf.buf], writes=[b7.buf], label="S2mm")
    return b6, b7


def row(rows_tile, i, cs):
    return rows_tile.ap[0:1, i * 512: i * 512 + cs]


def emit_norm(ctx, xt_par, cs, xbf_rot, sq_rot, rows, kind, par, btiles,
              eps1=1e-5, eps2=1e-6):
    S = ctx.S
    with_s1 = kind != "rms"
    b6, b7 = emit_stats(ctx, xt_par, cs, xbf_rot, sq_rot, ntiles=len(xt_par),
                        with_s1=with_s1)
    r_mu, r_var, r_rstd, r_q, r_x2, r_pr = (row(rows, i, cs) for i in range(6))
    if kind == "rms":
        epsc = ctx.constant(eps2)
        S.op("act", lambda e: e.activation(out=r_pr, in_=b7.ap[0:1, 0:cs],
                                           func=AF.Ln, scale=1.0 / D,
                                           bias=epsc[0:1, 0:1]),
             reads=[b7.buf, epsc.buf], writes=[rows.buf], label="ln_s2")
        S.op("act", lambda e: e.activation(out=r_pr, in_=r_pr, func=AF.Exp,
                                           scale=-0.5),
             reads=[rows.buf], writes=[rows.buf], label="rms")
        blist = [(r_pr, btiles["rms_b"][par], b7)]
    else:
        eps1c = ctx.constant(eps1)
        S.op("act", lambda e: e.activation(out=r_mu, in_=b6.ap[0:1, 0:cs],
                                           func=AF.Copy, scale=1.0 / D),
             reads=[b6.buf], writes=[rows.buf], label="mu")
        S.op("act", lambda e: e.activation(out=r_q, in_=b6.ap[0:1, 0:cs],
                                           func=AF.Square, scale=1.0 / D),
             reads=[b6.buf], writes=[rows.buf], label="musq")
        S.op("dve", lambda e: e.scalar_tensor_tensor(
                 out=r_var, in0=b7.ap[0:1, 0:cs], scalar=1.0 / D, in1=r_q,
                 op0=OP.mult, op1=OP.subtract),
             reads=[b7.buf, rows.buf], writes=[rows.buf], label="var")
        S.op("act", lambda e: e.activation(out=r_rstd, in_=r_var, func=AF.Ln,
                                           bias=eps1c[0:1, 0:1]),
             reads=[rows.buf, eps1c.buf], writes=[rows.buf], label="lnvar")
        S.op("act", lambda e: e.activation(out=r_rstd, in_=r_rstd, func=AF.Exp,
                                           scale=-0.5),
             reads=[rows.buf], writes=[rows.buf], label="rstd")
        blist = [(r_mu, btiles["mu_b"][par], b6)]
        if "rstd_b" in btiles:
            blist.append((r_rstd, btiles["rstd_b"][par], b7))
        if kind in ("ln_rms", "ln_affine"):
            e2 = eps2 if kind == "ln_rms" else eps1
            e2c = ctx.constant(e2) if e2 != 0.0 else None
            S.op("dve", lambda e: e.tensor_tensor(out=r_q, in0=r_rstd,
                                                  in1=r_rstd, op=OP.mult),
                 reads=[rows.buf], writes=[rows.buf], label="rstd2")
            S.op("dve", lambda e: e.tensor_tensor(out=r_q, in0=r_var, in1=r_q,
                                                   op=OP.mult),
                 reads=[rows.buf], writes=[rows.buf], label="q")
            if e2c is None:
                S.op("act", lambda e: e.activation(out=r_x2, in_=r_q, func=AF.Ln),
                     reads=[rows.buf], writes=[rows.buf], label="lnq")
            else:
                S.op("act", lambda e: e.activation(out=r_x2, in_=r_q, func=AF.Ln,
                                                   bias=e2c[0:1, 0:1]),
                     reads=[rows.buf, e2c.buf], writes=[rows.buf], label="lnq")
            S.op("act", lambda e: e.activation(out=r_x2, in_=r_x2, func=AF.Exp,
                                               scale=-0.5),
                 reads=[rows.buf], writes=[rows.buf], label="rmsu")
            S.op("dve", lambda e: e.tensor_tensor(out=r_pr, in0=r_rstd,
                                                  in1=r_x2, op=OP.mult),
                 reads=[rows.buf], writes=[rows.buf], label="PR")
            key = "pr_b" if kind == "ln_rms" else "rr_b"
            blist.append((r_pr, btiles[key][par], b6))
    for rap, dst, bank in blist:
        bcast_row(ctx, rap, rows.buf, bank, cs)
        S.op("act", lambda e, dst=dst, bank=bank: e.activation(
                 out=dst.ap[:, 0:cs], in_=bank.ap[:, 0:cs], func=AF.Copy),
             reads=[bank.buf], writes=[dst.buf], label="evac_bcast")



def emit_phase_a1(ctx, cs=512):
    S = ctx.S
    nch = T // cs
    xin_ap, xin_buf = ctx.dram["XIN"]
    u2_ap, u2_buf = ctx.dram["U2"]

    wg = load_weight(ctx, "WGT", D, D)
    wv = load_weight(ctx, "WVT", D, D)
    wd = load_weight(ctx, "WDT", D, D)
    bg = load_pvec(ctx, "BG")
    bv = load_pvec(ctx, "BV")
    bd = load_pvec(ctx, "BD")
    rmsw = load_pvec(ctx, "RMSMIXW")

    A = ctx.act
    xt = [A.carve([128, DT, cs], F32, "xt") for _ in range(2)]
    t1 = [A.carve([128, DT, cs], BF16, "t1") for _ in range(2)]
    xbf_rot = [A.carve([128, cs], BF16, "xbf") for _ in range(2)]
    sq_rot = [A.carve([128, cs], BF16, "sqx") for _ in range(2)]
    rows = ctx.rows
    pr_b = [A.carve([128, cs], F32, "pr_b") for _ in range(2)]
    rstd_b = [A.carve([128, cs], F32, "rstd_b") for _ in range(2)]
    mu_b = [A.carve([128, cs], F32, "mu_b") for _ in range(2)]
    sg_rot = [A.carve([128, cs], F32, "sg") for _ in range(2)]
    tv_rot = [A.carve([128, cs], F32, "tv") for _ in range(2)]
    sa_rot = [A.carve([128, cs], F32, "sa") for _ in range(2)]
    xsc_rot = [A.carve([128, cs], F32, "xsc") for _ in range(2)]
    a_rot = [A.carve([128, cs], F32, "a") for _ in range(2)]
    h_rot = [A.carve([128, cs], F32, "h") for _ in range(2)]
    tmp_rot = [A.carve([128, cs], F32, "tmp") for _ in range(2)]
    u_rot = [A.carve([128, cs], F32, "u") for _ in range(2)]
    u2_rot = [A.carve([128, cs], F32, "u2") for _ in range(4)]
    carry = ctx.perm.carve([128, DT], F32, "carryA")
    eps5 = ctx.constant(1e-5)
    eps6 = ctx.constant(1e-6)

    b6, b7 = ctx.banks[6], ctx.banks[7]

    def emit_chunk(c):
        par = c % 2
        x_t = xt[par]
        S.dma("sp", lambda e: e.dma_start(
                  out=x_t.full,
                  in_=xin_ap[:, :, c * cs:(c + 1) * cs].rearrange("t p n -> p t n")),
              reads=[xin_buf], writes=[x_t.buf], label=f"ldx{c}")
        xt_par = [(x_t.buf, x_t[:, m, :]) for m in range(DT)]
        emit_stats(ctx, xt_par, cs, xbf_rot, sq_rot)
        r_mu, r_var, r_rstd, r_q, r_rmsu, r_pr = (
            row(rows, i, cs) for i in range(6))
        S.op("act", lambda e: e.activation(out=r_mu, in_=b6.ap[0:1, 0:cs],
                                           func=AF.Copy, scale=1.0 / D),
             reads=[b6.buf], writes=[rows.buf], label="mu")
        S.op("act", lambda e: e.activation(out=r_q, in_=b6.ap[0:1, 0:cs],
                                           func=AF.Square, scale=1.0 / D),
             reads=[b6.buf], writes=[rows.buf], label="musq")
        S.op("dve", lambda e: e.scalar_tensor_tensor(
                 out=r_var, in0=b7.ap[0:1, 0:cs], scalar=1.0 / D, in1=r_q,
                 op0=OP.mult, op1=OP.subtract),
             reads=[b7.buf, rows.buf], writes=[rows.buf], label="var")
        S.op("act", lambda e: e.activation(out=r_rstd, in_=r_var, func=AF.Ln,
                                           bias=eps5[0:1, 0:1]),
             reads=[rows.buf, eps5.buf], writes=[rows.buf], label="lnvar")
        S.op("act", lambda e: e.activation(out=r_rstd, in_=r_rstd, func=AF.Exp,
                                           scale=-0.5),
             reads=[rows.buf], writes=[rows.buf], label="rstd")
        S.op("dve", lambda e: e.tensor_tensor(out=r_q, in0=r_rstd, in1=r_rstd,
                                              op=OP.mult),
             reads=[rows.buf], writes=[rows.buf], label="rstd2")
        S.op("dve", lambda e: e.tensor_tensor(out=r_q, in0=r_var, in1=r_q,
                                               op=OP.mult),
             reads=[rows.buf], writes=[rows.buf], label="q")
        S.op("act", lambda e: e.activation(out=r_rmsu, in_=r_q, func=AF.Ln,
                                           bias=eps6[0:1, 0:1]),
             reads=[rows.buf, eps6.buf], writes=[rows.buf], label="lnq")
        S.op("act", lambda e: e.activation(out=r_rmsu, in_=r_rmsu, func=AF.Exp,
                                           scale=-0.5),
             reads=[rows.buf], writes=[rows.buf], label="rmsu")
        S.op("dve", lambda e: e.tensor_tensor(out=r_pr, in0=r_rstd, in1=r_rmsu,
                                              op=OP.mult),
             reads=[rows.buf], writes=[rows.buf], label="PR")
        for rap, dst, scl, bank in ((r_pr, pr_b[par], 1.0, b6),
                                    (r_rstd, rstd_b[par], 1.0, b7),
                                    (r_mu, mu_b[par], 1.0, b6)):
            bcast_row(ctx, rap, rows.buf, bank, cs)
            S.op("act", lambda e, dst=dst, scl=scl, bank=bank: e.activation(
                     out=dst.ap[:, 0:cs], in_=bank.ap[:, 0:cs], func=AF.Copy,
                     scale=scl),
                 reads=[bank.buf], writes=[dst.buf], label="evac_bcast")

        def emit_t1(m):
            tmp = tmp_rot[m % 2]
            S.op("dve", lambda e: e.tensor_tensor(
                     out=tmp.ap[:, 0:cs], in0=x_t[:, m, :],
                     in1=mu_b[par].ap[:, 0:cs], op=OP.subtract),
                 reads=[x_t.buf, mu_b[par].buf], writes=[tmp.buf], label="tmp")
            S.op("dve", lambda e: e.scalar_tensor_tensor(
                     out=t1[par][:, m, :], in0=tmp.ap[:, 0:cs],
                     scalar=rmsw[:, m:m + 1], in1=pr_b[par].ap[:, 0:cs],
                     op0=OP.mult, op1=OP.mult),
                 reads=[tmp.buf, rmsw.buf, pr_b[par].buf], writes=[t1[par].buf],
                 label="t1")
        for m in range(DT):
            emit_t1(m)
        t1_tiles = [(t1[par].buf, t1[par][:, k, :]) for k in range(DT)]

        def emit_m(m):
            bG = ctx.banks[(m % 2) * 3 + 0]
            bV = ctx.banks[(m % 2) * 3 + 1]
            bD = ctx.banks[(m % 2) * 3 + 2]
            mm_group(ctx, bG, wg, (m * 128, (m + 1) * 128), t1_tiles, cs, f"G{m}")
            mm_group(ctx, bV, wv, (m * 128, (m + 1) * 128), t1_tiles, cs, f"V{m}")
            mm_group(ctx, bD, wd, (m * 128, (m + 1) * 128), t1_tiles, cs, f"D{m}")
            sg, tv, sa = sg_rot[m % 2], tv_rot[m % 2], sa_rot[m % 2]
            xsc, a_t, h = xsc_rot[m % 2], a_rot[m % 2], h_rot[m % 2]
            S.op("act", lambda e: e.activation(
                     out=sg.ap[:, 0:cs], in_=bG.ap[:, 0:cs], func=AF.Sigmoid,
                     bias=bg[:, m:m + 1]),
                 reads=[bG.buf, bg.buf], writes=[sg.buf], label="sig_g")
            S.op("act", lambda e: e.activation(
                     out=tv.ap[:, 0:cs], in_=bV.ap[:, 0:cs], func=AF.Tanh,
                     bias=bv[:, m:m + 1]),
                 reads=[bV.buf, bv.buf], writes=[tv.buf], label="tanh_v")
            S.op("dve", lambda e: e.tensor_tensor(
                     out=xsc.ap[:, 0:cs], in0=sg.ap[:, 0:cs], in1=tv.ap[:, 0:cs],
                     op=OP.mult),
                 reads=[sg.buf, tv.buf], writes=[xsc.buf], label="xsc")
            S.op("act", lambda e: e.activation(
                     out=sa.ap[:, 0:cs], in_=bD.ap[:, 0:cs], func=AF.Sigmoid,
                     bias=bd[:, m:m + 1]),
                 reads=[bD.buf, bd.buf], writes=[sa.buf], label="sig_d")
            S.op("dve", lambda e: e.tensor_scalar(
                     out=a_t.ap[:, 0:cs], in0=sa.ap[:, 0:cs], scalar1=0.998,
                     scalar2=0.001, op0=OP.mult, op1=OP.add),
                 reads=[sa.buf], writes=[a_t.buf], label="a_affine")
            if c % (SEQ // cs) == 0:
                init = 0.0
                init_reads = []
            else:
                init = carry[:, m:m + 1]
                init_reads = [carry.buf]
            S.op("dve", lambda e: e.tensor_tensor_scan(
                     out=h.ap[:, 0:cs], data0=a_t.ap[:, 0:cs],
                     data1=xsc.ap[:, 0:cs], initial=init,
                     op0=OP.mult, op1=OP.add),
                 reads=[a_t.buf, xsc.buf] + init_reads, writes=[h.buf],
                 label="scan")
            S.op("act", lambda e: e.activation(
                     out=carry[:, m:m + 1], in_=h.ap[:, cs - 1:cs], func=AF.Copy),
                 reads=[h.buf], writes=[carry.buf], label="carry")
            tmp2, u, u2t = tmp_rot[m % 2], u_rot[m % 2], u2_rot[m % 4]
            S.op("dve", lambda e: e.tensor_tensor(
                     out=tmp2.ap[:, 0:cs], in0=x_t[:, m, :],
                     in1=mu_b[par].ap[:, 0:cs], op=OP.subtract),
                 reads=[x_t.buf, mu_b[par].buf], writes=[tmp2.buf], label="tmp2")
            S.op("dve", lambda e: e.tensor_tensor(
                     out=u.ap[:, 0:cs], in0=tmp2.ap[:, 0:cs],
                     in1=rstd_b[par].ap[:, 0:cs], op=OP.mult),
                 reads=[tmp2.buf, rstd_b[par].buf], writes=[u.buf], label="u")
            S.op("dve", lambda e: e.tensor_tensor(
                     out=u2t.ap[:, 0:cs], in0=u.ap[:, 0:cs], in1=h.ap[:, 0:cs],
                     op=OP.add),
                 reads=[u.buf, h.buf], writes=[u2t.buf], label="u2")
            S.dma("sp", lambda e: e.dma_start(
                      out=u2_ap[m, :, c * cs:(c + 1) * cs], in_=u2t.ap[:, 0:cs]),
                  reads=[u2t.buf], writes=[u2_buf], label=f"st_u2_{c}_{m}")

        for m in range(DT):
            emit_m(m)

    for c in range(nch):
        emit_chunk(c)



def emit_phase_a2a(ctx, cs=512):
    S = ctx.S
    nch = T // cs
    u2_ap, u2_buf = ctx.dram["U2"]
    gu_ap, gu_buf = ctx.dram["GU"]
    FT = 24

    wgate = load_weight(ctx, "WGATET", D, 3072)
    wup = load_weight(ctx, "WUPT", D, 3072)
    rmsfw = load_pvec(ctx, "RMSFFNW")

    A = ctx.act
    ut = [A.carve([128, DT, cs], F32, "ut") for _ in range(2)]
    t2 = [A.carve([128, DT, cs], BF16, "t2") for _ in range(2)]
    xbf_rot = [A.carve([128, cs], BF16, "xbf") for _ in range(2)]
    sq_rot = [A.carve([128, cs], BF16, "sqx") for _ in range(2)]
    rows = ctx.rows
    rms_b = [A.carve([128, cs], F32, "rms_b") for _ in range(2)]
    sil_rot = [A.carve([128, cs], BF16, "sil") for _ in range(2)]
    gu_rot = [A.carve([128, cs], BF16, "gu") for _ in range(4)]
    btiles = {"rms_b": rms_b}

    def emit_chunk(c):
        par = c % 2
        u_t = ut[par]
        S.dma("sp", lambda e: e.dma_start(
                  out=u_t.full,
                  in_=u2_ap[:, :, c * cs:(c + 1) * cs].rearrange("t p n -> p t n")),
              reads=[u2_buf], writes=[u_t.buf], label=f"ldu{c}")
        ut_par = [(u_t.buf, u_t[:, m, :]) for m in range(DT)]
        emit_norm(ctx, ut_par, cs, xbf_rot, sq_rot, rows, "rms", par, btiles,
                  eps2=1e-6)
        def emit_t2(m):
            S.op("dve", lambda e: e.scalar_tensor_tensor(
                     out=t2[par][:, m, :], in0=u_t[:, m, :],
                     scalar=rmsfw[:, m:m + 1], in1=rms_b[par].ap[:, 0:cs],
                     op0=OP.mult, op1=OP.mult),
                 reads=[u_t.buf, rmsfw.buf, rms_b[par].buf], writes=[t2[par].buf],
                 label="t2")
        for m in range(DT):
            emit_t2(m)
        t2_tiles = [(t2[par].buf, t2[par][:, k, :]) for k in range(DT)]

        def emit_f(f):
            bG = ctx.banks[(f % 3)]
            bU = ctx.banks[3 + (f % 3)]
            mm_group(ctx, bG, wgate, (f * 128, (f + 1) * 128), t2_tiles, cs, f"Ga{f}")
            mm_group(ctx, bU, wup, (f * 128, (f + 1) * 128), t2_tiles, cs, f"Ua{f}")
            sil = sil_rot[f % 2]
            gu_t = gu_rot[f % 4]
            S.op("act", lambda e: e.activation(
                     out=sil.ap[:, 0:cs], in_=bG.ap[:, 0:cs], func=AF.Silu),
                 reads=[bG.buf], writes=[sil.buf], label="silu")
            S.op("dve", lambda e: e.tensor_tensor(
                     out=gu_t.ap[:, 0:cs], in0=sil.ap[:, 0:cs],
                     in1=bU.ap[:, 0:cs], op=OP.mult),
                 reads=[sil.buf, bU.buf], writes=[gu_t.buf], label="gu")
            S.dma("sp", lambda e: e.dma_start(
                      out=gu_ap[f, :, c * cs:(c + 1) * cs], in_=gu_t.ap[:, 0:cs]),
                  reads=[gu_t.buf], writes=[gu_buf], label=f"st_gu{c}_{f}")
        for f in range(FT):
            emit_f(f)

    for c in range(nch):
        emit_chunk(c)



def emit_phase_a2b(ctx, cs=512):
    S = ctx.S
    nch = T // cs
    u2_ap, u2_buf = ctx.dram["U2"]
    gu_ap, gu_buf = ctx.dram["GU"]
    xin_ap, xin_buf = ctx.dram["XIN"]
    xr_ap, xr_buf = ctx.dram["XR"]
    FT = 24

    wout = load_weight(ctx, "WOUTT", 3072, D)

    A = ctx.act
    ut = [A.carve([128, DT, cs], F32, "ut")]
    xt = [A.carve([128, DT, cs], F32, "xt") for _ in range(2)]
    gut = [A.carve([128, FT, cs], BF16, "gut")]
    yt = [A.carve([128, DT, cs], F32, "yt")]
    xbf_rot = [A.carve([128, cs], BF16, "xbf") for _ in range(2)]
    sq_rot = [A.carve([128, cs], BF16, "sqx") for _ in range(2)]
    rows = ctx.rows
    mu_b = [A.carve([128, cs], F32, "mu_b") for _ in range(2)]
    rstd_b = [A.carve([128, cs], F32, "rstd_b") for _ in range(2)]
    tmp_rot = [A.carve([128, cs], F32, "tmp") for _ in range(2)]
    t_rot = [A.carve([128, cs], F32, "t") for _ in range(2)]
    xo_rot = [A.carve([128, cs], F32, "xo") for _ in range(3)]
    btiles = {"mu_b": mu_b, "rstd_b": rstd_b}

    def emit_chunk(c):
        par = c % 2
        u_t, x_t, gu_t, y_t = ut[0], xt[par], gut[0], yt[0]
        S.dma("sp", lambda e: e.dma_start(
                  out=u_t.full,
                  in_=u2_ap[:, :, c * cs:(c + 1) * cs].rearrange("t p n -> p t n")),
              reads=[u2_buf], writes=[u_t.buf], label=f"ldu{c}")
        S.dma("sp", lambda e: e.dma_start(
                  out=x_t.full,
                  in_=xin_ap[:, :, c * cs:(c + 1) * cs].rearrange("t p n -> p t n")),
              reads=[xin_buf], writes=[x_t.buf], label=f"ldx{c}")
        S.dma("sp", lambda e: e.dma_start(
                  out=gu_t.full,
                  in_=gu_ap[:, :, c * cs:(c + 1) * cs].rearrange("t p n -> p t n")),
              reads=[gu_buf], writes=[gu_t.buf], label=f"ldgu{c}")
        gu_tiles = [(gu_t.buf, gu_t[:, k, :]) for k in range(FT)]

        def emit_y(m):
            bank = ctx.banks[m % 4]
            mm_group(ctx, bank, wout, (m * 128, (m + 1) * 128), gu_tiles, cs,
                     f"Y{m}")
            S.op("dve", lambda e: e.tensor_tensor(
                     out=y_t[:, m, :], in0=u_t[:, m, :], in1=bank.ap[:, 0:cs],
                     op=OP.add),
                 reads=[u_t.buf, bank.buf], writes=[y_t.buf], label="yadd")
        for m in range(DT):
            emit_y(m)
        yt_par = [(y_t.buf, y_t[:, m, :]) for m in range(DT)]
        emit_norm(ctx, yt_par, cs, xbf_rot, sq_rot, rows, "ln", par, btiles)

        def emit_xo(m):
            tmp, t_, xo = tmp_rot[m % 2], t_rot[m % 2], xo_rot[m % 3]
            S.op("dve", lambda e: e.tensor_tensor(
                     out=tmp.ap[:, 0:cs], in0=y_t[:, m, :],
                     in1=mu_b[par].ap[:, 0:cs], op=OP.subtract),
                 reads=[y_t.buf, mu_b[par].buf], writes=[tmp.buf], label="ytmp")
            S.op("dve", lambda e: e.tensor_tensor(
                     out=t_.ap[:, 0:cs], in0=tmp.ap[:, 0:cs],
                     in1=rstd_b[par].ap[:, 0:cs], op=OP.mult),
                 reads=[tmp.buf, rstd_b[par].buf], writes=[t_.buf], label="yln")
            S.op("dve", lambda e: e.tensor_tensor(
                     out=xo.ap[:, 0:cs], in0=t_.ap[:, 0:cs], in1=x_t[:, m, :],
                     op=OP.add),
                 reads=[t_.buf, x_t.buf], writes=[xo.buf], label="xo")
            S.dma("sp", lambda e: e.dma_start(
                      out=xr_ap[m, :, c * cs:(c + 1) * cs], in_=xo.ap[:, 0:cs]),
                  reads=[xo.buf], writes=[xr_buf], label=f"st_xr{c}_{m}")
        for m in range(DT):
            emit_xo(m)

    for c in range(nch):
        emit_chunk(c)


def get_iden(ctx):
    if not hasattr(ctx, "iden"):
        ap, buf = ctx.dram_in("IDEN", [128, 128])
        tile = ctx.perm.carve([128, 128], F32, "iden")
        ctx.S.dma("sp", lambda e: e.dma_start(out=tile.full, in_=ap[:]),
                  reads=[buf], writes=[tile.buf], label="ld_iden")
        ctx.iden = tile
    return ctx.iden


def load_pcol(ctx, name, p):
    ap, buf = ctx.dram_in(name, [1, p])
    tile = ctx.perm.carve([128, 1], F32, name)
    ctx.S.dma("sp", lambda e: e.dma_start(out=tile.ap[0:p, 0:1],
                                          in_=ap.rearrange("o p -> p o"),
                                          allow_slow_non_contiguous=True),
              reads=[buf], writes=[tile.buf], label="ld_" + name)
    return tile



def emit_phase_b0(ctx, cs=512):
    S = ctx.S
    nch = T // cs
    xr_ap, xr_buf = ctx.dram["XR"]
    ub_ap, ub_buf = ctx.dram["UB"]
    rw = load_weight_f32(ctx, "RWTB", D, 16)
    rb = load_pcol(ctx, "RBB", 16)
    iden = get_iden(ctx)
    comb_ap, comb_buf = ctx.dram["COMB"]

    A = ctx.act
    xt = [A.carve([128, DT, cs], F32, "xt") for _ in range(2)]
    ub = [A.carve([128, DT, cs], BF16, "ub") for _ in range(2)]
    uf = [A.carve([128, DT, cs], F32, "uf")]
    xbf_rot = [A.carve([128, cs], BF16, "xbf") for _ in range(2)]
    sq_rot = [A.carve([128, cs], BF16, "sqx") for _ in range(2)]
    rows = ctx.rows
    mu_b = [A.carve([128, cs], F32, "mu_b") for _ in range(2)]
    rstd_b = [A.carve([128, cs], F32, "rstd_b") for _ in range(2)]
    tmp_rot = [A.carve([128, cs], F32, "tmp") for _ in range(2)]
    lbt = [A.carve([128, cs], F32, "lbt") for _ in range(2)]
    ltk_rot = [A.carve([128, 16], F32, "ltk") for _ in range(2)]
    ctile_rot = [A.carve([128, 128], F32, "ctile") for _ in range(2)]
    col_rot = [A.carve([128, 1], F32, f"col{i}") for i in range(8)]
    s16_rot = [A.carve([128, 16], F32, f"s16{i}") for i in range(6)]
    btiles = {"mu_b": mu_b, "rstd_b": rstd_b}

    def emit_chunk(c):
        par = c % 2
        x_t, ub_t, lb = xt[par], ub[par], lbt[par]
        S.dma("sp", lambda e: e.dma_start(
                  out=x_t.full,
                  in_=xr_ap[:, :, c * cs:(c + 1) * cs].rearrange("t p n -> p t n")),
              reads=[xr_buf], writes=[x_t.buf], label=f"ldx{c}")
        xt_par = [(x_t.buf, x_t[:, m, :]) for m in range(DT)]
        emit_norm(ctx, xt_par, cs, xbf_rot, sq_rot, rows, "ln", par, btiles)

        u_f = uf[0]
        def emit_u(m):
            tmp = tmp_rot[m % 2]
            S.op("dve", lambda e: e.tensor_tensor(
                     out=tmp.ap[:, 0:cs], in0=x_t[:, m, :],
                     in1=mu_b[par].ap[:, 0:cs], op=OP.subtract),
                 reads=[x_t.buf, mu_b[par].buf], writes=[tmp.buf], label="tmp")
            S.op("dve", lambda e: e.tensor_tensor(
                     out=u_f[:, m, :], in0=tmp.ap[:, 0:cs],
                     in1=rstd_b[par].ap[:, 0:cs], op=OP.mult),
                 reads=[tmp.buf, rstd_b[par].buf], writes=[u_f.buf], label="uf")
            S.op("act", lambda e: e.activation(
                     out=ub_t[:, m, :], in_=u_f[:, m, :], func=AF.Copy),
                 reads=[u_f.buf], writes=[ub_t.buf], label="ub")
        for m in range(DT):
            emit_u(m)
        S.dma("sp", lambda e: e.dma_start(
                  out=ub_ap[:, :, c * cs:(c + 1) * cs].rearrange("t p n -> p t n"),
                  in_=ub_t.full),
              reads=[ub_t.buf], writes=[ub_buf], label=f"st_ub{c}")
        uf_tiles = [(u_f.buf, u_f[:, k, :]) for k in range(DT)]
        b0 = ctx.banks[0]
        mm_group(ctx, b0, rw, (0, 16), uf_tiles, cs, "router")
        S.op("dve", lambda e: e.tensor_scalar(
                 out=lb[0:16, 0:cs], in0=b0.ap[0:16, 0:cs],
                 scalar1=rb[0:16, 0:1], scalar2=None, op0=OP.add),
             reads=[b0.buf, rb.buf], writes=[lb.buf], label="lb")

        def emit_j(j):
            bT = ctx.banks[1 + (j % 2)]
            S.op("pe", lambda e: e.matmul(bT.ap[:, 0:16],
                                          lb[0:16, j * 128:(j + 1) * 128],
                                          iden[0:16, 0:16], is_transpose=True,
                                          start=True, stop=True),
                 reads=[lb.buf, iden.buf], writes=[bT.buf], label="trin")
            ltk = ltk_rot[j % 2]
            S.op("act", lambda e: e.activation(out=ltk.full, in_=bT.ap[:, 0:16],
                                               func=AF.Copy),
                 reads=[bT.buf], writes=[ltk.buf], label="ltk")
            m1, m2, e2c, p1, p2 = (col_rot[(j % 2) * 4 + i] for i in range(4)) \
                if False else (col_rot[0], col_rot[1], col_rot[2], col_rot[3],
                               col_rot[4])
            d1, msk, ge1, ge2, ctk = (s16_rot[i] for i in range(5))
            S.op("dve", lambda e: e.reduce_max(out=m1.full, in_=ltk.full,
                                               axis=AX.X),
                 reads=[ltk.buf], writes=[m1.buf], label="m1")
            S.op("dve", lambda e: e.tensor_scalar(
                     out=d1.full, in0=ltk.full, scalar1=m1.full, scalar2=None,
                     op0=OP.subtract),
                 reads=[ltk.buf, m1.buf], writes=[d1.buf], label="d1")
            S.op("dve", lambda e: e.tensor_scalar(
                     out=ge1.full, in0=d1.full, scalar1=0.0, scalar2=None,
                     op0=OP.is_ge),
                 reads=[d1.buf], writes=[ge1.buf], label="ge1")
            S.op("dve", lambda e: e.scalar_tensor_tensor(
                     out=msk.full, in0=ge1.full, scalar=-1e30, in1=ltk.full,
                     op0=OP.mult, op1=OP.add),
                 reads=[ge1.buf, ltk.buf], writes=[msk.buf], label="msk")
            S.op("dve", lambda e: e.reduce_max(out=m2.full, in_=msk.full,
                                               axis=AX.X),
                 reads=[msk.buf], writes=[m2.buf], label="m2")
            S.op("dve", lambda e: e.tensor_tensor(
                     out=e2c.full, in0=m2.full, in1=m1.full, op=OP.subtract),
                 reads=[m2.buf, m1.buf], writes=[e2c.buf], label="d21")
            S.op("act", lambda e: e.activation(out=e2c.full, in_=e2c.full,
                                               func=AF.Exp),
                 reads=[e2c.buf], writes=[e2c.buf], label="e2")
            S.op("dve", lambda e: e.tensor_scalar(
                     out=p1.full, in0=e2c.full, scalar1=1.0, scalar2=1.0,
                     op0=OP.mult, op1=OP.add),
                 reads=[e2c.buf], writes=[p1.buf], label="den")
            S.op("dve", lambda e: e.reciprocal(out=p1.full, in_=p1.full),
                 reads=[p1.buf], writes=[p1.buf], label="p1")
            S.op("dve", lambda e: e.tensor_tensor(
                     out=p2.full, in0=e2c.full, in1=p1.full, op=OP.mult),
                 reads=[e2c.buf, p1.buf], writes=[p2.buf], label="p2")
            S.op("dve", lambda e: e.tensor_scalar(
                     out=ge2.full, in0=msk.full, scalar1=m2.full, scalar2=None,
                     op0=OP.subtract),
                 reads=[msk.buf, m2.buf], writes=[ge2.buf], label="d2")
            S.op("dve", lambda e: e.tensor_scalar(
                     out=ge2.full, in0=ge2.full, scalar1=0.0, scalar2=None,
                     op0=OP.is_ge),
                 reads=[ge2.buf], writes=[ge2.buf], label="ge2")
            S.op("dve", lambda e: e.tensor_scalar(
                     out=d1.full, in0=ge1.full, scalar1=p1.full, scalar2=None,
                     op0=OP.mult),
                 reads=[ge1.buf, p1.buf], writes=[d1.buf], label="c1")
            S.op("dve", lambda e: e.scalar_tensor_tensor(
                     out=ctk.full, in0=ge2.full, scalar=p2.full, in1=d1.full,
                     op0=OP.mult, op1=OP.add),
                 reads=[ge2.buf, p2.buf, d1.buf], writes=[ctk.buf], label="ctk")
            bT2 = ctx.banks[3 + (j % 2)]
            S.op("pe", lambda e: e.matmul(bT2.ap[0:16, 0:128], ctk.full,
                                          iden.full, is_transpose=True,
                                          start=True, stop=True),
                 reads=[ctk.buf, iden.buf], writes=[bT2.buf], label="trout")
            ctile = ctile_rot[j % 2]
            S.op("act", lambda e: e.activation(
                     out=ctile.ap[0:16, 0:128],
                     in_=bT2.ap[0:16, 0:128], func=AF.Copy),
                 reads=[bT2.buf], writes=[ctile.buf], label="comb")
            S.dma("pool", lambda e: e.dma_start(
                      out=comb_ap[0:16, c * cs + j * 128: c * cs + (j + 1) * 128],
                      in_=ctile.ap[0:16, 0:128]),
                  reads=[ctile.buf], writes=[comb_buf], label=f"stcomb{c}_{j}")
        for j in range(cs // 128):
            emit_j(j)

    for c in range(nch):
        emit_chunk(c)



def emit_phase_bexp(ctx, experts, first, cs=512):
    S = ctx.S
    nch = T // cs
    ub_ap, ub_buf = ctx.dram["UB"]
    acc_ap, acc_buf = ctx.dram["ACC"]
    comb_ap, comb_buf = ctx.dram["COMB"]
    NE = len(experts)

    wg = [load_weight(ctx, f"BGATE{e}", D, 512) for e in experts]
    wu = [load_weight(ctx, f"BUP{e}", D, 512) for e in experts]
    wd = [load_weight(ctx, f"BDOWN{e}", 512, D) for e in experts]

    A = ctx.act
    ub = [A.carve([128, DT, cs], BF16, "ub") for _ in range(2)]
    hb = A.carve([128, NE * 4, cs], BF16, "hb")
    cb = [A.carve([128, cs], BF16, f"cb{i}") for i in range(NE)]
    sil_rot = [A.carve([128, cs], BF16, "sil") for _ in range(2)]
    upe_rot = [A.carve([128, cs], BF16, "upe") for _ in range(2)]
    h1_rot = [A.carve([128, cs], BF16, "h1") for _ in range(2)]
    res_rot = [A.carve([128, cs], F32, "res") for _ in range(4)]

    def emit_chunk(c):
        par = c % 2
        ub_t = ub[par]
        S.dma("sp", lambda e: e.dma_start(
                  out=ub_t.full,
                  in_=ub_ap[:, :, c * cs:(c + 1) * cs].rearrange("t p n -> p t n")),
              reads=[ub_buf], writes=[ub_t.buf], label=f"ldub{c}")
        ub_tiles = [(ub_t.buf, ub_t[:, k, :]) for k in range(DT)]
        for i, e_ in enumerate(experts):
            S.dma("sp", lambda e, i=i, e_=e_: e.dma_start(
                      out=cb[i].ap[:, 0:cs],
                      in_=comb_ap[e_, c * cs:(c + 1) * cs].partition_broadcast(128)),
                  reads=[comb_buf], writes=[cb[i].buf], label=f"ldcb{i}")

        def emit_eh(i, fm):
            bG = ctx.banks[fm % 2]
            bU = ctx.banks[2 + fm % 2]
            mm_group(ctx, bG, wg[i], (fm * 128, (fm + 1) * 128), ub_tiles, cs,
                     f"eG{i}_{fm}")
            mm_group(ctx, bU, wu[i], (fm * 128, (fm + 1) * 128), ub_tiles, cs,
                     f"eU{i}_{fm}")
            sil, upe, h1 = sil_rot[fm % 2], upe_rot[fm % 2], h1_rot[fm % 2]
            S.op("act", lambda e: e.activation(
                     out=sil.ap[:, 0:cs], in_=bG.ap[:, 0:cs], func=AF.Silu),
                 reads=[bG.buf], writes=[sil.buf], label="silu")
            S.op("dve", lambda e: e.activation(out=upe.ap[:, 0:cs],
                                               in_=bU.ap[:, 0:cs], func=AF.Copy)
                 if False else e.tensor_copy(out=upe.ap[:, 0:cs],
                                             in_=bU.ap[:, 0:cs]),
                 reads=[bU.buf], writes=[upe.buf], label="upe")
            S.op("dve", lambda e: e.tensor_tensor(
                     out=h1.ap[:, 0:cs], in0=sil.ap[:, 0:cs],
                     in1=upe.ap[:, 0:cs], op=OP.mult),
                 reads=[sil.buf, upe.buf], writes=[h1.buf], label="h1")
            S.op("dve", lambda e: e.tensor_tensor(
                     out=hb[:, i * 4 + fm, :], in0=h1.ap[:, 0:cs],
                     in1=cb[i].ap[:, 0:cs], op=OP.mult),
                 reads=[h1.buf, cb[i].buf], writes=[hb.buf], label="hb")
        for i in range(NE):
            for fm in range(4):
                emit_eh(i, fm)

        def emit_down(m):
            bank = ctx.banks[4 + (m % 2)]
            for i in range(NE):
                tiles = [(hb.buf, hb[:, i * 4 + k, :]) for k in range(4)]
                mm_group(ctx, bank, wd[i], (m * 128, (m + 1) * 128), tiles, cs,
                         f"dn{i}_{m}", accumulate=(i > 0), final=(i == NE - 1))
            res = res_rot[m % 4]
            S.op("act", lambda e: e.activation(out=res.ap[:, 0:cs],
                                               in_=bank.ap[:, 0:cs], func=AF.Copy),
                 reads=[bank.buf], writes=[res.buf], label="res")
            if first:
                S.dma("pool", lambda e: e.dma_start(
                          out=acc_ap[m, :, c * cs:(c + 1) * cs],
                          in_=res.ap[:, 0:cs]),
                      reads=[res.buf], writes=[acc_buf], label=f"stacc{c}_{m}")
            else:
                S.dma("pool", lambda e: e.dma_start(
                          out=acc_ap[m, :, c * cs:(c + 1) * cs],
                          in_=res.ap[:, 0:cs], accum_op=OP.add),
                      reads=[res.buf], writes=[acc_buf], label=f"acacc{c}_{m}")
        for m in range(DT):
            emit_down(m)

    for c in range(nch):
        emit_chunk(c)



def emit_phase_b5(ctx, cs=512):
    S = ctx.S
    nch = T // cs
    ub_ap, ub_buf = ctx.dram["UB"]
    acc_ap, acc_buf = ctx.dram["ACC"]
    xr_ap, xr_buf = ctx.dram["XR"]

    shg = load_weight(ctx, "SHGT", D, D)
    shu = load_weight(ctx, "SHUT", D, D)
    shd = load_weight(ctx, "SHDT", D, D)

    A = ctx.act
    ub = [A.carve([128, DT, cs], BF16, "ub") for _ in range(2)]
    acct = [A.carve([128, DT, cs], F32, "acct")]
    xt = [A.carve([128, DT, cs], F32, "xt") for _ in range(2)]
    hh = A.carve([128, DT, cs], BF16, "hh")
    yt = [A.carve([128, DT, cs], F32, "yt")]
    sil_rot = [A.carve([128, cs], BF16, "sil") for _ in range(2)]
    upe_rot = [A.carve([128, cs], BF16, "upe") for _ in range(2)]
    xbf_rot = [A.carve([128, cs], BF16, "xbf") for _ in range(2)]
    sq_rot = [A.carve([128, cs], BF16, "sqx") for _ in range(2)]
    rows = ctx.rows
    mu_b = [A.carve([128, cs], F32, "mu_b") for _ in range(2)]
    rstd_b = [A.carve([128, cs], F32, "rstd_b") for _ in range(2)]
    tmp_rot = [A.carve([128, cs], F32, "tmp") for _ in range(2)]
    t_rot = [A.carve([128, cs], F32, "t") for _ in range(2)]
    xo_rot = [A.carve([128, cs], F32, "xo") for _ in range(3)]
    btiles = {"mu_b": mu_b, "rstd_b": rstd_b}

    def emit_chunk(c):
        par = c % 2
        ub_t, acc_t, x_t, y_t = ub[par], acct[0], xt[par], yt[0]
        S.dma("sp", lambda e: e.dma_start(
                  out=ub_t.full,
                  in_=ub_ap[:, :, c * cs:(c + 1) * cs].rearrange("t p n -> p t n")),
              reads=[ub_buf], writes=[ub_t.buf], label=f"ldub{c}")
        S.dma("sp", lambda e: e.dma_start(
                  out=acc_t.full,
                  in_=acc_ap[:, :, c * cs:(c + 1) * cs].rearrange("t p n -> p t n")),
              reads=[acc_buf], writes=[acc_t.buf], label=f"ldacc{c}")
        S.dma("sp", lambda e: e.dma_start(
                  out=x_t.full,
                  in_=xr_ap[:, :, c * cs:(c + 1) * cs].rearrange("t p n -> p t n")),
              reads=[xr_buf], writes=[x_t.buf], label=f"ldx{c}")
        ub_tiles = [(ub_t.buf, ub_t[:, k, :]) for k in range(DT)]

        def emit_sh(fm):
            bG = ctx.banks[fm % 2]
            bU = ctx.banks[2 + fm % 2]
            mm_group(ctx, bG, shg, (fm * 128, (fm + 1) * 128), ub_tiles, cs,
                     f"shG{fm}")
            mm_group(ctx, bU, shu, (fm * 128, (fm + 1) * 128), ub_tiles, cs,
                     f"shU{fm}")
            sil, upe = sil_rot[fm % 2], upe_rot[fm % 2]
            S.op("act", lambda e: e.activation(
                     out=sil.ap[:, 0:cs], in_=bG.ap[:, 0:cs], func=AF.Silu),
                 reads=[bG.buf], writes=[sil.buf], label="silu")
            S.op("dve", lambda e: e.tensor_copy(out=upe.ap[:, 0:cs],
                                                in_=bU.ap[:, 0:cs]),
                 reads=[bU.buf], writes=[upe.buf], label="upe")
            S.op("dve", lambda e: e.tensor_tensor(
                     out=hh[:, fm, :], in0=sil.ap[:, 0:cs], in1=upe.ap[:, 0:cs],
                     op=OP.mult),
                 reads=[sil.buf, upe.buf], writes=[hh.buf], label="hh")
        for fm in range(DT):
            emit_sh(fm)
        hh_tiles = [(hh.buf, hh[:, k, :]) for k in range(DT)]

        def emit_y(m):
            bank = ctx.banks[4 + (m % 2)]
            mm_group(ctx, bank, shd, (m * 128, (m + 1) * 128), hh_tiles, cs,
                     f"shD{m}")
            S.op("dve", lambda e: e.tensor_tensor(
                     out=y_t[:, m, :], in0=acc_t[:, m, :], in1=bank.ap[:, 0:cs],
                     op=OP.add),
                 reads=[acc_t.buf, bank.buf], writes=[y_t.buf], label="yadd")
        for m in range(DT):
            emit_y(m)
        yt_par = [(y_t.buf, y_t[:, m, :]) for m in range(DT)]
        emit_norm(ctx, yt_par, cs, xbf_rot, sq_rot, rows, "ln", par, btiles)

        def emit_xo(m):
            tmp, t_, xo = tmp_rot[m % 2], t_rot[m % 2], xo_rot[m % 3]
            S.op("dve", lambda e: e.tensor_tensor(
                     out=tmp.ap[:, 0:cs], in0=y_t[:, m, :],
                     in1=mu_b[par].ap[:, 0:cs], op=OP.subtract),
                 reads=[y_t.buf, mu_b[par].buf], writes=[tmp.buf], label="ytmp")
            S.op("dve", lambda e: e.tensor_tensor(
                     out=t_.ap[:, 0:cs], in0=tmp.ap[:, 0:cs],
                     in1=rstd_b[par].ap[:, 0:cs], op=OP.mult),
                 reads=[tmp.buf, rstd_b[par].buf], writes=[t_.buf], label="yln")
            S.op("dve", lambda e: e.tensor_tensor(
                     out=xo.ap[:, 0:cs], in0=t_.ap[:, 0:cs], in1=x_t[:, m, :],
                     op=OP.add),
                 reads=[t_.buf, x_t.buf], writes=[xo.buf], label="xo")
            S.dma("sp", lambda e: e.dma_start(
                      out=xr_ap[m, :, c * cs:(c + 1) * cs], in_=xo.ap[:, 0:cs]),
                  reads=[xo.buf], writes=[xr_buf], label=f"st_xr{c}_{m}")
        for m in range(DT):
            emit_xo(m)

    for c in range(nch):
        emit_chunk(c)



def emit_phase_c0(ctx, cs=512):
    S = ctx.S
    nch = T // cs
    HT = 16
    xr_ap, xr_buf = ctx.dram["XR"]
    xc_ap, xc_buf = ctx.dram["XC"]

    wx = load_weight(ctx, "WIXCT", D, 2048)
    gam1 = load_pvec(ctx, "GAM1")
    cw = [load_pvec(ctx, f"CONVW{k}", 2048) for k in range(4)]
    cb = load_pvec(ctx, "CONVB", 2048)

    A = ctx.act
    xt = [A.carve([128, DT, cs], F32, "xt")]
    xn = [A.carve([128, DT, cs], BF16, "xn")]
    xcbuf = [A.carve([128, HT, cs + 4], F32, "xcbuf") for _ in range(2)]
    accb = A.carve([128, HT, cs], F32, "accb")
    xbf_rot = [A.carve([128, cs], BF16, "xbf") for _ in range(2)]
    sq_rot = [A.carve([128, cs], BF16, "sqx") for _ in range(2)]
    rows = ctx.rows
    mu_b = [A.carve([128, cs], F32, "mu_b") for _ in range(2)]
    pr_b = [A.carve([128, cs], F32, "pr_b") for _ in range(2)]
    tmp_rot = [A.carve([128, cs], F32, "tmp") for _ in range(2)]
    btiles = {"mu_b": mu_b, "pr_b": pr_b}

    def emit_chunk(c):
        par = c % 2
        x_t, xn_t, xcb = xt[0], xn[0], xcbuf[par]
        S.dma("sp", lambda e: e.dma_start(
                  out=x_t.full,
                  in_=xr_ap[:, :, c * cs:(c + 1) * cs].rearrange("t p n -> p t n")),
              reads=[xr_buf], writes=[x_t.buf], label=f"ldx{c}")
        xt_par = [(x_t.buf, x_t[:, m, :]) for m in range(DT)]
        emit_norm(ctx, xt_par, cs, xbf_rot, sq_rot, rows, "ln_rms", par, btiles,
                  eps2=0.0)
        def emit_xn(m):
            tmp = tmp_rot[m % 2]
            S.op("dve", lambda e: e.tensor_tensor(
                     out=tmp.ap[:, 0:cs], in0=x_t[:, m, :],
                     in1=mu_b[par].ap[:, 0:cs], op=OP.subtract),
                 reads=[x_t.buf, mu_b[par].buf], writes=[tmp.buf], label="tmp")
            S.op("dve", lambda e: e.scalar_tensor_tensor(
                     out=xn_t[:, m, :], in0=tmp.ap[:, 0:cs],
                     scalar=gam1[:, m:m + 1], in1=pr_b[par].ap[:, 0:cs],
                     op0=OP.mult, op1=OP.mult),
                 reads=[tmp.buf, gam1.buf, pr_b[par].buf], writes=[xn_t.buf],
                 label="xn")
        for m in range(DT):
            emit_xn(m)
        xn_tiles = [(xn_t.buf, xn_t[:, k, :]) for k in range(DT)]
        if c % (SEQ // cs) == 0:
            S.op("dve", lambda e: e.memset(xcb[:, :, 0:3], 0.0),
                 writes=[xcb.buf], label="halo0")
        else:
            prev = xcbuf[1 - par]
            S.op("act", lambda e: e.activation(
                     out=xcb[:, :, 0:3], in_=prev[:, :, cs:cs + 3], func=AF.Copy),
                 reads=[prev.buf], writes=[xcb.buf], label="halo")
        for fm in range(HT):
            bank = ctx.banks[fm % 4]
            mm_group(ctx, bank, wx, (fm * 128, (fm + 1) * 128), xn_tiles, cs,
                     f"xc{fm}")
            S.op("act", lambda e, fm=fm, bank=bank: e.activation(
                     out=xcb[:, fm, 3:3 + cs], in_=bank.ap[:, 0:cs], func=AF.Copy),
                 reads=[bank.buf], writes=[xcb.buf], label="evxc")
        def st0(fm):
            S.op("dve", lambda e: e.tensor_scalar(
                     out=accb[:, fm, :], in0=xcb[:, fm, 0:cs],
                     scalar1=cw[0][:, fm:fm + 1], scalar2=cb[:, fm:fm + 1],
                     op0=OP.mult, op1=OP.add),
                 reads=[xcb.buf, cw[0].buf, cb.buf], writes=[accb.buf],
                 label="cv0")
        for fm in range(HT):
            st0(fm)
        for k in range(1, 4):
            def stk(fm, k=k):
                S.op("dve", lambda e: e.scalar_tensor_tensor(
                         out=accb[:, fm, :], in0=xcb[:, fm, k:k + cs],
                         scalar=cw[k][:, fm:fm + 1], in1=accb[:, fm, :],
                         op0=OP.mult, op1=OP.add),
                     reads=[xcb.buf, cw[k].buf, accb.buf], writes=[accb.buf],
                     label=f"cv{k}")
            for fm in range(HT):
                stk(fm)
        S.dma("pool", lambda e: e.dma_start(
                  out=xc_ap[:, :, c * cs:(c + 1) * cs].rearrange("t p n -> p t n"),
                  in_=accb.full),
              reads=[accb.buf], writes=[xc_buf], label=f"st_xc{c}")
    for c in range(nch):
        emit_chunk(c)



def emit_phase_c1f(ctx, cs=256):
    S = ctx.S
    nch = T // cs
    HT = 16
    xc_ap, xc_buf = ctx.dram["XC"]
    at_ap, at_buf = ctx.dram["AT"]

    wf = load_weight(ctx, "WGATESFT", 2048, 2048)
    bgf = load_pvec(ctx, "BGF", 2048)
    fb = load_pvec(ctx, "FB", 2048)
    one_c = ctx.constant(1.0)
    c2 = ctx.perm.carve([128, HT], F32, "c2col")
    t1 = ctx.perm.carve([128, HT], F32, "spt1")
    S.op("act", lambda e: e.activation(out=t1.full, in_=fb.full, func=AF.Abs),
         reads=[fb.buf], writes=[t1.buf], label="spabs")
    S.op("act", lambda e: e.activation(out=t1.full, in_=t1.full, func=AF.Exp,
                                       scale=-1.0),
         reads=[t1.buf], writes=[t1.buf], label="spexp")
    S.op("act", lambda e: e.activation(out=t1.full, in_=t1.full, func=AF.Ln,
                                       bias=one_c.full),
         reads=[t1.buf, one_c.buf], writes=[t1.buf], label="spln")
    S.op("dve", lambda e: e.tensor_scalar(out=c2.full, in0=fb.full, scalar1=0.0,
                                           scalar2=None, op0=OP.max),
         reads=[fb.buf], writes=[c2.buf], label="spmax")
    S.op("dve", lambda e: e.tensor_tensor(out=c2.full, in0=c2.full, in1=t1.full,
                                          op=OP.add),
         reads=[c2.buf, t1.buf], writes=[c2.buf], label="spadd")
    S.op("dve", lambda e: e.tensor_scalar(out=c2.full, in0=c2.full, scalar1=-4.0,
                                           scalar2=None, op0=OP.mult),
         reads=[c2.buf], writes=[c2.buf], label="spscale")
    ctx.c2col = c2

    A = ctx.act
    xcb = [A.carve([128, HT, cs], BF16, "xcb") for _ in range(2)]
    fbuf = A.carve([128, HT, cs], F32, "fbuf")
    atb = [A.carve([128, HT, cs], F32, "atb") for _ in range(2)]

    def emit_chunk(c):
        par = c % 2
        xc_t, at_t = xcb[par], atb[par]
        S.dma("sp", lambda e: e.dma_start(
                  out=xc_t.full,
                  in_=xc_ap[:, :, c * cs:(c + 1) * cs].rearrange("t p n -> p t n")),
              reads=[xc_buf], writes=[xc_t.buf], label=f"ldxc{c}")
        xc_tiles = [(xc_t.buf, xc_t[:, k, :]) for k in range(HT)]
        for fm in range(HT):
            bank = ctx.banks[fm % 4]
            mm_group(ctx, bank, wf, (fm * 128, (fm + 1) * 128), xc_tiles, cs,
                     f"fg{fm}")
            S.op("dve", lambda e, fm=fm, bank=bank: e.tensor_scalar(
                     out=fbuf[:, fm, :], in0=bank.ap[:, 0:cs],
                     scalar1=bgf[:, fm:fm + 1], scalar2=None, op0=OP.add),
                 reads=[bank.buf, bgf.buf], writes=[fbuf.buf], label="evf")
        S.op("act", lambda e: e.activation(out=fbuf.full, in_=fbuf.full,
                                           func=AF.Tanh, scale=0.5),
             reads=[fbuf.buf], writes=[fbuf.buf], label="tanhF")
        def emit_at(fm):
            S.op("dve", lambda e: e.tensor_scalar(
                     out=at_t[:, fm, :], in0=fbuf[:, fm, :],
                     scalar1=c2[:, fm:fm + 1], scalar2=c2[:, fm:fm + 1],
                     op0=OP.mult, op1=OP.add),
                 reads=[fbuf.buf, c2.buf], writes=[at_t.buf], label="at")
        for fm in range(HT):
            emit_at(fm)
        S.dma("sp", lambda e: e.dma_start(
                  out=at_ap[:, :, c * cs:(c + 1) * cs].rearrange("t p n -> p t n"),
                  in_=at_t.full),
              reads=[at_t.buf], writes=[at_buf], label=f"st_at{c}")
    for c in range(nch):
        emit_chunk(c)



def emit_phase_c1i(ctx, cs=256):
    S = ctx.S
    nch = T // cs
    HT = 16
    xc_ap, xc_buf = ctx.dram["XC"]
    at_ap, at_buf = ctx.dram["AT"]
    h_ap, h_buf = ctx.dram["H"]

    wi = load_weight(ctx, "WGATESIT", 2048, 2048)
    bgi = load_pvec(ctx, "BGI", 2048)

    A = ctx.act
    xcb = [A.carve([128, HT, cs], BF16, "xcb")]
    ibuf = A.carve([128, HT, cs], F32, "ibuf")
    atb = [A.carve([128, HT, cs], F32, "atb") for _ in range(2)]
    albuf = A.carve([128, HT, cs], F32, "albuf")
    bebuf = A.carve([128, HT, cs], F32, "bebuf")
    hb = [A.carve([128, HT, cs], BF16, "hb") for _ in range(2)]

    def emit_chunk(c):
        par = c % 2
        xc_t, at_t, h_t = xcb[0], atb[par], hb[par]
        S.dma("sp", lambda e: e.dma_start(
                  out=xc_t.full,
                  in_=xc_ap[:, :, c * cs:(c + 1) * cs].rearrange("t p n -> p t n")),
              reads=[xc_buf], writes=[xc_t.buf], label=f"ldxc{c}")
        S.dma("sp", lambda e: e.dma_start(
                  out=at_t.full,
                  in_=at_ap[:, :, c * cs:(c + 1) * cs].rearrange("t p n -> p t n")),
              reads=[at_buf], writes=[at_t.buf], label=f"ldat{c}")
        xc_tiles = [(xc_t.buf, xc_t[:, k, :]) for k in range(HT)]
        for fm in range(HT):
            bank = ctx.banks[fm % 4]
            mm_group(ctx, bank, wi, (fm * 128, (fm + 1) * 128), xc_tiles, cs,
                     f"ig{fm}")
            S.op("dve", lambda e, fm=fm, bank=bank: e.tensor_scalar(
                     out=ibuf[:, fm, :], in0=bank.ap[:, 0:cs],
                     scalar1=bgi[:, fm:fm + 1], scalar2=None, op0=OP.add),
                 reads=[bank.buf, bgi.buf], writes=[ibuf.buf], label="evi")
        S.op("act", lambda e: e.activation(out=ibuf.full, in_=ibuf.full,
                                           func=AF.Tanh, scale=0.5),
             reads=[ibuf.buf], writes=[ibuf.buf], label="tanhI")
        S.op("act", lambda e: e.activation(out=albuf.full, in_=at_t.full,
                                           func=AF.Exp),
             reads=[at_t.buf], writes=[albuf.buf], label="alpha")
        S.op("act", lambda e: e.activation(out=bebuf.full, in_=at_t.full,
                                           func=AF.Exp, scale=2.0),
             reads=[at_t.buf], writes=[bebuf.buf], label="alpha2")
        S.op("dve", lambda e: e.tensor_scalar(
                 out=bebuf.full, in0=bebuf.full, scalar1=-1.0,
                 scalar2=1.0 + 1e-6, op0=OP.mult, op1=OP.add),
             reads=[bebuf.buf], writes=[bebuf.buf], label="bsq")
        S.op("act", lambda e: e.activation(out=bebuf.full, in_=bebuf.full,
                                           func=AF.Sqrt),
             reads=[bebuf.buf], writes=[bebuf.buf], label="beta")
        S.op("dve", lambda e: e.tensor_tensor(out=ibuf.full, in0=bebuf.full,
                                               in1=ibuf.full, op=OP.mult),
             reads=[bebuf.buf, ibuf.buf], writes=[ibuf.buf], label="btI")
        S.op("dve", lambda e: e.tensor_tensor(out=at_t.full, in0=bebuf.full,
                                              in1=ibuf.full, op=OP.add),
             reads=[bebuf.buf, ibuf.buf], writes=[at_t.buf], label="xs2")
        def emit_xs(fm):
            S.op("dve", lambda e: e.scalar_tensor_tensor(
                     out=at_t[:, fm, :], in0=at_t[:, fm, :], scalar=0.5,
                     in1=xc_t[:, fm, :], op0=OP.mult, op1=OP.mult),
                 reads=[at_t.buf, xc_t.buf], writes=[at_t.buf], label="xs")
        for fm in range(HT):
            emit_xs(fm)
        def emit_scan(fm):
            if c % (SEQ // cs) == 0:
                init = 0.0
                init_reads = []
            else:
                init = hb[1 - par][:, fm, cs - 1:cs]
                init_reads = [hb[1 - par].buf]
            S.op("dve", lambda e, init=init: e.tensor_tensor_scan(
                     out=h_t[:, fm, :], data0=albuf[:, fm, :],
                     data1=at_t[:, fm, :], initial=init, op0=OP.mult,
                     op1=OP.add),
                 reads=[albuf.buf, at_t.buf] + init_reads, writes=[h_t.buf],
                 label="scan")
        for fm in range(HT):
            emit_scan(fm)
        S.dma("sp", lambda e: e.dma_start(
                  out=h_ap[:, :, c * cs:(c + 1) * cs].rearrange("t p n -> p t n"),
                  in_=h_t.full),
              reads=[h_t.buf], writes=[h_buf], label=f"st_h{c}")
    for c in range(nch):
        emit_chunk(c)



def emit_phase_c2a(ctx, cs=512):
    S = ctx.S
    nch = T // cs
    HT = 16
    xr_ap, xr_buf = ctx.dram["XR"]
    h_ap, h_buf = ctx.dram["H"]
    u2_ap, u2_buf = ctx.dram["U2"]

    wg = load_weight(ctx, "WINGT", D, 2048)
    wo = load_weight(ctx, "WOUTHT", 2048, D)
    gam1 = load_pvec(ctx, "GAM1")
    A = ctx.act
    xt = [A.carve([128, DT, cs], F32, "xt")]
    xn = [A.carve([128, DT, cs], BF16, "xn")]
    ub = A.carve([128, DT, cs], F32, "ub")
    ht = [A.carve([128, HT, cs], BF16, "ht")]
    pb = A.carve([128, HT, cs], BF16, "pb")
    xbf_rot = [A.carve([128, cs], BF16, "xbf") for _ in range(2)]
    sq_rot = [A.carve([128, cs], BF16, "sqx") for _ in range(2)]
    rows = ctx.rows
    mu_b = [A.carve([128, cs], F32, "mu_b") for _ in range(2)]
    rstd_b = [A.carve([128, cs], F32, "rstd_b") for _ in range(2)]
    pr_b = [A.carve([128, cs], F32, "pr_b") for _ in range(2)]
    tmp_rot = [A.carve([128, cs], F32, "tmp") for _ in range(2)]
    gl_rot = [A.carve([128, cs], BF16, "gl") for _ in range(2)]
    u2_rot = [A.carve([128, cs], F32, "u2o") for _ in range(4)]
    btiles = {"mu_b": mu_b, "rstd_b": rstd_b, "pr_b": pr_b}

    def emit_chunk(c):
        par = c % 2
        x_t, xn_t, h_t = xt[0], xn[0], ht[0]
        S.dma("sp", lambda e: e.dma_start(
                  out=x_t.full,
                  in_=xr_ap[:, :, c * cs:(c + 1) * cs].rearrange("t p n -> p t n")),
              reads=[xr_buf], writes=[x_t.buf], label=f"ldx{c}")
        S.dma("sp", lambda e: e.dma_start(
                  out=h_t.full,
                  in_=h_ap[:, :, c * cs:(c + 1) * cs].rearrange("t p n -> p t n")),
              reads=[h_buf], writes=[h_t.buf], label=f"ldh{c}")
        xt_par = [(x_t.buf, x_t[:, m, :]) for m in range(DT)]
        emit_norm(ctx, xt_par, cs, xbf_rot, sq_rot, rows, "ln_rms", par, btiles,
                  eps2=0.0)
        def emit_xn(m):
            tmp = tmp_rot[m % 2]
            S.op("dve", lambda e: e.tensor_tensor(
                     out=tmp.ap[:, 0:cs], in0=x_t[:, m, :],
                     in1=mu_b[par].ap[:, 0:cs], op=OP.subtract),
                 reads=[x_t.buf, mu_b[par].buf], writes=[tmp.buf], label="tmp")
            S.op("dve", lambda e: e.scalar_tensor_tensor(
                     out=xn_t[:, m, :], in0=tmp.ap[:, 0:cs],
                     scalar=gam1[:, m:m + 1], in1=pr_b[par].ap[:, 0:cs],
                     op0=OP.mult, op1=OP.mult),
                 reads=[tmp.buf, gam1.buf, pr_b[par].buf], writes=[xn_t.buf],
                 label="xn")
            S.op("dve", lambda e: e.tensor_tensor(
                     out=ub[:, m, :], in0=tmp.ap[:, 0:cs],
                     in1=rstd_b[par].ap[:, 0:cs], op=OP.mult),
                 reads=[tmp.buf, rstd_b[par].buf], writes=[ub.buf], label="u")
        for m in range(DT):
            emit_xn(m)
        xn_tiles = [(xn_t.buf, xn_t[:, k, :]) for k in range(DT)]
        for fm in range(HT):
            bank = ctx.banks[fm % 4]
            mm_group(ctx, bank, wg, (fm * 128, (fm + 1) * 128), xn_tiles, cs,
                     f"gt{fm}")
            gl = gl_rot[fm % 2]
            S.op("act", lambda e, bank=bank, gl=gl: e.activation(
                     out=gl.ap[:, 0:cs], in_=bank.ap[:, 0:cs], func=AF.Gelu),
                 reads=[bank.buf], writes=[gl.buf], label="gelu")
            S.op("dve", lambda e, fm=fm, gl=gl: e.tensor_tensor(
                     out=pb[:, fm, :], in0=gl.ap[:, 0:cs], in1=h_t[:, fm, :],
                     op=OP.mult),
                 reads=[gl.buf, h_t.buf], writes=[pb.buf], label="p")
        pb_tiles = [(pb.buf, pb[:, k, :]) for k in range(HT)]
        def emit_r(m):
            bank = ctx.banks[4 + (m % 2)]
            mm_group(ctx, bank, wo, (m * 128, (m + 1) * 128), pb_tiles, cs,
                     f"r{m}")
            u2o = u2_rot[m % 4]
            S.op("dve", lambda e: e.tensor_tensor(
                     out=u2o.ap[:, 0:cs], in0=ub[:, m, :], in1=bank.ap[:, 0:cs],
                     op=OP.add),
                 reads=[ub.buf, bank.buf], writes=[u2o.buf], label="u2")
            S.dma("sp", lambda e: e.dma_start(
                      out=u2_ap[m, :, c * cs:(c + 1) * cs], in_=u2o.ap[:, 0:cs]),
                  reads=[u2o.buf], writes=[u2_buf], label=f"st_u2{c}_{m}")
        for m in range(DT):
            emit_r(m)
    for c in range(nch):
        emit_chunk(c)



def emit_phase_c2b(ctx, cs=256):
    S = ctx.S
    nch = T // cs
    HT = 16
    xr_ap, xr_buf = ctx.dram["XR"]
    u2_ap, u2_buf = ctx.dram["U2"]

    wgr = load_weight(ctx, "WGROWT", D, 4096)
    wsh = load_weight(ctx, "WSHRT", 2048, D)
    gam2 = load_pvec(ctx, "GAM2")

    A = ctx.act
    ut = [A.carve([128, DT, cs], F32, "ut")]
    xt = [A.carve([128, DT, cs], F32, "xt")]
    xn = [A.carve([128, DT, cs], BF16, "xn")]
    tb = A.carve([128, HT, cs], BF16, "tb")
    yt = [A.carve([128, DT, cs], F32, "yt")]
    xbf_rot = [A.carve([128, cs], BF16, "xbf") for _ in range(2)]
    sq_rot = [A.carve([128, cs], BF16, "sqx") for _ in range(2)]
    rows = ctx.rows
    rms_b = [A.carve([128, cs], F32, "rms_b") for _ in range(2)]
    mu_b = [A.carve([128, cs], F32, "mu_b") for _ in range(2)]
    rstd_b = [A.carve([128, cs], F32, "rstd_b") for _ in range(2)]
    gl_rot = [A.carve([128, cs], BF16, "gl") for _ in range(2)]
    mx_rot = [A.carve([128, cs], BF16, "mx") for _ in range(2)]
    tmp_rot = [A.carve([128, cs], F32, "tmp") for _ in range(2)]
    t_rot = [A.carve([128, cs], F32, "t") for _ in range(2)]
    xo_rot = [A.carve([128, cs], F32, "xo") for _ in range(3)]

    def emit_chunk(c):
        par = c % 2
        u_t, x_t, xn_t, y_t = ut[0], xt[0], xn[0], yt[0]
        S.dma("sp", lambda e: e.dma_start(
                  out=u_t.full,
                  in_=u2_ap[:, :, c * cs:(c + 1) * cs].rearrange("t p n -> p t n")),
              reads=[u2_buf], writes=[u_t.buf], label=f"ldu{c}")
        S.dma("sp", lambda e: e.dma_start(
                  out=x_t.full,
                  in_=xr_ap[:, :, c * cs:(c + 1) * cs].rearrange("t p n -> p t n")),
              reads=[xr_buf], writes=[x_t.buf], label=f"ldx{c}")
        ut_par = [(u_t.buf, u_t[:, m, :]) for m in range(DT)]
        emit_norm(ctx, ut_par, cs, xbf_rot, sq_rot, rows, "rms", par,
                  {"rms_b": rms_b}, eps2=0.0)
        def emit_xn(m):
            S.op("dve", lambda e: e.scalar_tensor_tensor(
                     out=xn_t[:, m, :], in0=u_t[:, m, :],
                     scalar=gam2[:, m:m + 1], in1=rms_b[par].ap[:, 0:cs],
                     op0=OP.mult, op1=OP.mult),
                 reads=[u_t.buf, gam2.buf, rms_b[par].buf], writes=[xn_t.buf],
                 label="xn2")
        for m in range(DT):
            emit_xn(m)
        xn_tiles = [(xn_t.buf, xn_t[:, k, :]) for k in range(DT)]
        def emit_pair(fm):
            bM = ctx.banks[fm % 2]
            bX = ctx.banks[2 + fm % 2]
            mm_group(ctx, bM, wgr, (fm * 128, (fm + 1) * 128), xn_tiles, cs,
                     f"mg{fm}")
            mm_group(ctx, bX, wgr, (2048 + fm * 128, 2048 + (fm + 1) * 128),
                     xn_tiles, cs, f"mx{fm}")
            gl, mx = gl_rot[fm % 2], mx_rot[fm % 2]
            S.op("act", lambda e: e.activation(
                     out=gl.ap[:, 0:cs], in_=bM.ap[:, 0:cs], func=AF.Gelu),
                 reads=[bM.buf], writes=[gl.buf], label="gelu")
            S.op("dve", lambda e: e.tensor_copy(out=mx.ap[:, 0:cs],
                                                in_=bX.ap[:, 0:cs]),
                 reads=[bX.buf], writes=[mx.buf], label="mx")
            S.op("dve", lambda e: e.tensor_tensor(
                     out=tb[:, fm, :], in0=gl.ap[:, 0:cs], in1=mx.ap[:, 0:cs],
                     op=OP.mult),
                 reads=[gl.buf, mx.buf], writes=[tb.buf], label="t")
        for fm in range(HT):
            emit_pair(fm)
        tb_tiles = [(tb.buf, tb[:, k, :]) for k in range(HT)]
        def emit_y(m):
            bank = ctx.banks[4 + (m % 2)]
            mm_group(ctx, bank, wsh, (m * 128, (m + 1) * 128), tb_tiles, cs,
                     f"ml{m}")
            S.op("dve", lambda e: e.tensor_tensor(
                     out=y_t[:, m, :], in0=u_t[:, m, :], in1=bank.ap[:, 0:cs],
                     op=OP.add),
                 reads=[u_t.buf, bank.buf], writes=[y_t.buf], label="yadd")
        for m in range(DT):
            emit_y(m)
        yt_par = [(y_t.buf, y_t[:, m, :]) for m in range(DT)]
        emit_norm(ctx, yt_par, cs, xbf_rot, sq_rot, rows, "ln", par,
                  {"mu_b": mu_b, "rstd_b": rstd_b})
        def emit_xo(m):
            tmp, t_, xo = tmp_rot[m % 2], t_rot[m % 2], xo_rot[m % 3]
            S.op("dve", lambda e: e.tensor_tensor(
                     out=tmp.ap[:, 0:cs], in0=y_t[:, m, :],
                     in1=mu_b[par].ap[:, 0:cs], op=OP.subtract),
                 reads=[y_t.buf, mu_b[par].buf], writes=[tmp.buf], label="ytmp")
            S.op("dve", lambda e: e.tensor_tensor(
                     out=t_.ap[:, 0:cs], in0=tmp.ap[:, 0:cs],
                     in1=rstd_b[par].ap[:, 0:cs], op=OP.mult),
                 reads=[tmp.buf, rstd_b[par].buf], writes=[t_.buf], label="yln")
            S.op("dve", lambda e: e.tensor_tensor(
                     out=xo.ap[:, 0:cs], in0=t_.ap[:, 0:cs], in1=x_t[:, m, :],
                     op=OP.add),
                 reads=[t_.buf, x_t.buf], writes=[xo.buf], label="xo")
            S.dma("sp", lambda e: e.dma_start(
                      out=xr_ap[m, :, c * cs:(c + 1) * cs], in_=xo.ap[:, 0:cs]),
                  reads=[xo.buf], writes=[xr_buf], label=f"st_xr{c}_{m}")
        for m in range(DT):
            emit_xo(m)
    for c in range(nch):
        emit_chunk(c)



def emit_phase_d0(ctx, cs=512):
    S = ctx.S
    nch = T // cs
    xr_ap, xr_buf = ctx.dram["XR"]
    xn_ap, xn_buf = ctx.dram["XN"]
    dg_ap, dg_buf = ctx.dram["DGC"]

    rw = load_weight_f32(ctx, "RWTD", D, 8)
    lng = load_pvec(ctx, "LNG")
    lnb = load_pvec(ctx, "LNB")
    iden = get_iden(ctx)

    A = ctx.act
    xt = [A.carve([128, DT, cs], F32, "xt") for _ in range(2)]
    xnf = [A.carve([128, DT, cs], F32, "xnf")]
    xnb = [A.carve([128, DT, cs], BF16, "xnb")]
    xbf_rot = [A.carve([128, cs], BF16, "xbf") for _ in range(2)]
    sq_rot = [A.carve([128, cs], BF16, "sqx") for _ in range(2)]
    rows = ctx.rows
    mu_b = [A.carve([128, cs], F32, "mu_b") for _ in range(2)]
    rr_b = [A.carve([128, cs], F32, "rr_b") for _ in range(2)]
    tmp_rot = [A.carve([128, cs], F32, "tmp") for _ in range(2)]
    xp_rot = [A.carve([128, cs], F32, "xp") for _ in range(2)]
    lbt = [A.carve([128, cs], F32, "lbt") for _ in range(2)]
    ltk_rot = [A.carve([128, 8], F32, "ltk") for _ in range(2)]
    ebuf = A.carve([128, 8], F32, "ebuf")
    col_rot = [A.carve([128, 1], F32, f"dcol{i}") for i in range(6)]
    s8_rot = [A.carve([128, 8], F32, f"s8{i}") for i in range(5)]
    ctile_rot = [A.carve([128, 128], F32, "ctile") for _ in range(2)]
    btiles = {"mu_b": mu_b, "rr_b": rr_b}

    def emit_chunk(c):
        par = c % 2
        x_t, xn_f, xn_t, lb = xt[par], xnf[0], xnb[0], lbt[par]
        S.dma("sp", lambda e: e.dma_start(
                  out=x_t.full,
                  in_=xr_ap[:, :, c * cs:(c + 1) * cs].rearrange("t p n -> p t n")),
              reads=[xr_buf], writes=[x_t.buf], label=f"ldx{c}")
        xt_par = [(x_t.buf, x_t[:, m, :]) for m in range(DT)]
        emit_norm(ctx, xt_par, cs, xbf_rot, sq_rot, rows, "ln_affine", par,
                  btiles)
        def emit_xn(m):
            tmp, xp = tmp_rot[m % 2], xp_rot[m % 2]
            S.op("dve", lambda e: e.tensor_tensor(
                     out=tmp.ap[:, 0:cs], in0=x_t[:, m, :],
                     in1=mu_b[par].ap[:, 0:cs], op=OP.subtract),
                 reads=[x_t.buf, mu_b[par].buf], writes=[tmp.buf], label="tmp")
            S.op("dve", lambda e: e.scalar_tensor_tensor(
                     out=xp.ap[:, 0:cs], in0=tmp.ap[:, 0:cs],
                     scalar=lng[:, m:m + 1], in1=rr_b[par].ap[:, 0:cs],
                     op0=OP.mult, op1=OP.mult),
                 reads=[tmp.buf, lng.buf, rr_b[par].buf], writes=[xp.buf],
                 label="xp")
            S.op("dve", lambda e: e.tensor_scalar(
                     out=xn_f[:, m, :], in0=xp.ap[:, 0:cs],
                     scalar1=lnb[:, m:m + 1], scalar2=None, op0=OP.add),
                 reads=[xp.buf, lnb.buf], writes=[xn_f.buf], label="xnf")
            S.op("act", lambda e: e.activation(
                     out=xn_t[:, m, :], in_=xn_f[:, m, :], func=AF.Copy),
                 reads=[xn_f.buf], writes=[xn_t.buf], label="xnb")
        for m in range(DT):
            emit_xn(m)
        S.dma("sp", lambda e: e.dma_start(
                  out=xn_ap[:, :, c * cs:(c + 1) * cs].rearrange("t p n -> p t n"),
                  in_=xn_t.full),
              reads=[xn_t.buf], writes=[xn_buf], label=f"st_xn{c}")
        xnf_tiles = [(xn_f.buf, xn_f[:, k, :]) for k in range(DT)]
        b0 = ctx.banks[0]
        mm_group(ctx, b0, rw, (0, 8), xnf_tiles, cs, "routerD")
        S.op("dve", lambda e: e.tensor_scalar(
                 out=lb[0:8, 0:cs], in0=b0.ap[0:8, 0:cs], scalar1=-1e4,
                 scalar2=1e4, op0=OP.max, op1=OP.min),
             reads=[b0.buf], writes=[lb.buf], label="clip")

        def emit_j(j):
            bT = ctx.banks[1 + (j % 2)]
            S.op("pe", lambda e: e.matmul(bT.ap[:, 0:8],
                                          lb[0:8, j * 128:(j + 1) * 128],
                                          iden[0:8, 0:8], is_transpose=True,
                                          start=True, stop=True),
                 reads=[lb.buf, iden.buf], writes=[bT.buf], label="trin")
            ltk = ltk_rot[j % 2]
            S.op("act", lambda e: e.activation(out=ltk.full, in_=bT.ap[:, 0:8],
                                               func=AF.Copy),
                 reads=[bT.buf], writes=[ltk.buf], label="ltk")
            m1, s_, m1p, m2 = col_rot[0], col_rot[1], col_rot[2], col_rot[3]
            d_, prc, msk, ge2, dg = (s8_rot[i] for i in range(5))
            S.op("dve", lambda e: e.reduce_max(out=m1.full, in_=ltk.full,
                                               axis=AX.X),
                 reads=[ltk.buf], writes=[m1.buf], label="m1")
            S.op("dve", lambda e: e.tensor_scalar(
                     out=d_.full, in0=ltk.full, scalar1=m1.full, scalar2=None,
                     op0=OP.subtract),
                 reads=[ltk.buf, m1.buf], writes=[d_.buf], label="dsub")
            S.op("act", lambda e: e.activation(out=ebuf.full, in_=d_.full,
                                               func=AF.Exp),
                 reads=[d_.buf], writes=[ebuf.buf], label="eexp")
            S.op("dve", lambda e: e.reduce_sum(out=s_.full, in_=ebuf.full,
                                               axis=AX.X),
                 reads=[ebuf.buf], writes=[s_.buf], label="esum")
            S.op("dve", lambda e: e.tensor_scalar(
                     out=s_.full, in0=s_.full, scalar1=1.0, scalar2=1e-8,
                     op0=OP.mult, op1=OP.add),
                 reads=[s_.buf], writes=[s_.buf], label="den")
            S.op("dve", lambda e: e.reciprocal(out=s_.full, in_=s_.full),
                 reads=[s_.buf], writes=[s_.buf], label="rec")
            S.op("dve", lambda e: e.tensor_scalar(
                     out=prc.full, in0=ebuf.full, scalar1=s_.full, scalar2=None,
                     op0=OP.mult),
                 reads=[ebuf.buf, s_.buf], writes=[prc.buf], label="probs")
            S.op("dve", lambda e: e.tensor_scalar(
                     out=prc.full, in0=prc.full, scalar1=1e-8, scalar2=1.0,
                     op0=OP.max, op1=OP.min),
                 reads=[prc.buf], writes=[prc.buf], label="clipp")
            S.op("dve", lambda e: e.reduce_max(out=m1p.full, in_=prc.full,
                                               axis=AX.X),
                 reads=[prc.buf], writes=[m1p.buf], label="m1p")
            S.op("dve", lambda e: e.tensor_scalar(
                     out=msk.full, in0=prc.full, scalar1=m1p.full, scalar2=None,
                     op0=OP.subtract),
                 reads=[prc.buf, m1p.buf], writes=[msk.buf], label="d1")
            S.op("dve", lambda e: e.tensor_scalar(
                     out=msk.full, in0=msk.full, scalar1=0.0, scalar2=None,
                     op0=OP.is_ge),
                 reads=[msk.buf], writes=[msk.buf], label="ge1")
            S.op("dve", lambda e: e.scalar_tensor_tensor(
                     out=msk.full, in0=msk.full, scalar=-1e30, in1=prc.full,
                     op0=OP.mult, op1=OP.add),
                 reads=[msk.buf, prc.buf], writes=[msk.buf], label="msk")
            S.op("dve", lambda e: e.reduce_max(out=m2.full, in_=msk.full,
                                               axis=AX.X),
                 reads=[msk.buf], writes=[m2.buf], label="m2")
            S.op("dve", lambda e: e.tensor_scalar(
                     out=ge2.full, in0=prc.full, scalar1=m2.full, scalar2=None,
                     op0=OP.subtract),
                 reads=[prc.buf, m2.buf], writes=[ge2.buf], label="d2")
            S.op("dve", lambda e: e.tensor_scalar(
                     out=ge2.full, in0=ge2.full, scalar1=0.0, scalar2=None,
                     op0=OP.is_ge),
                 reads=[ge2.buf], writes=[ge2.buf], label="ge2")
            S.op("dve", lambda e: e.tensor_tensor(
                     out=dg.full, in0=prc.full, in1=ge2.full, op=OP.mult),
                 reads=[prc.buf, ge2.buf], writes=[dg.buf], label="dg")
            bT2 = ctx.banks[3 + (j % 2)]
            S.op("pe", lambda e: e.matmul(bT2.ap[0:8, 0:128], dg.full,
                                          iden.full, is_transpose=True,
                                          start=True, stop=True),
                 reads=[dg.buf, iden.buf], writes=[bT2.buf], label="trout")
            ctile = ctile_rot[j % 2]
            S.op("act", lambda e: e.activation(
                     out=ctile.ap[0:8, 0:128], in_=bT2.ap[0:8, 0:128],
                     func=AF.Copy),
                 reads=[bT2.buf], writes=[ctile.buf], label="dgc")
            S.dma("pool", lambda e: e.dma_start(
                      out=dg_ap[0:8, c * cs + j * 128: c * cs + (j + 1) * 128],
                      in_=ctile.ap[0:8, 0:128]),
                  reads=[ctile.buf], writes=[dg_buf], label=f"stdg{c}_{j}")
        for j in range(cs // 128):
            emit_j(j)
    for c in range(nch):
        emit_chunk(c)



def emit_phase_dexp(ctx, e_, first, cs=512):
    S = ctx.S
    nch = T // cs
    HT = 16
    xn_ap, xn_buf = ctx.dram["XN"]
    dg_ap, dg_buf = ctx.dram["DGC"]
    acc_ap, acc_buf = ctx.dram["ACC"]

    wgu = load_weight(ctx, f"DGUT{e_}", D, 4096)
    wdn = load_weight(ctx, f"DDT{e_}", 2048, D)

    A = ctx.act
    xnb = [A.carve([128, DT, cs], BF16, "xnb") for _ in range(2)]
    xeb = A.carve([128, DT, cs], BF16, "xeb")
    dgb = [A.carve([128, cs], BF16, "dgb") for _ in range(2)]
    hbuf = A.carve([128, HT, cs], BF16, "hbuf")
    sil_rot = [A.carve([128, cs], BF16, "sil") for _ in range(2)]
    upe_rot = [A.carve([128, cs], BF16, "upe") for _ in range(2)]
    res_rot = [A.carve([128, cs], F32, "res") for _ in range(4)]

    def emit_chunk(c):
        par = c % 2
        xn_t, dg_t = xnb[par], dgb[par]
        S.dma("sp", lambda e: e.dma_start(
                  out=xn_t.full,
                  in_=xn_ap[:, :, c * cs:(c + 1) * cs].rearrange("t p n -> p t n")),
              reads=[xn_buf], writes=[xn_t.buf], label=f"ldxn{c}")
        S.dma("sp", lambda e: e.dma_start(
                  out=dg_t.ap[:, 0:cs],
                  in_=dg_ap[e_, c * cs:(c + 1) * cs].partition_broadcast(128)),
              reads=[dg_buf], writes=[dg_t.buf], label=f"lddg{c}")
        def emit_xe(m):
            S.op("dve", lambda e: e.tensor_tensor(
                     out=xeb[:, m, :], in0=xn_t[:, m, :], in1=dg_t.ap[:, 0:cs],
                     op=OP.mult),
                 reads=[xn_t.buf, dg_t.buf], writes=[xeb.buf], label="xe")
        for m in range(DT):
            emit_xe(m)
        xe_tiles = [(xeb.buf, xeb[:, k, :]) for k in range(DT)]
        def emit_h(fm):
            bG = ctx.banks[fm % 2]
            bU = ctx.banks[2 + fm % 2]
            mm_group(ctx, bG, wgu, (fm * 128, (fm + 1) * 128), xe_tiles, cs,
                     f"dG{fm}")
            mm_group(ctx, bU, wgu, (2048 + fm * 128, 2048 + (fm + 1) * 128),
                     xe_tiles, cs, f"dU{fm}")
            sil, upe = sil_rot[fm % 2], upe_rot[fm % 2]
            S.op("act", lambda e: e.activation(
                     out=sil.ap[:, 0:cs], in_=bG.ap[:, 0:cs], func=AF.Silu),
                 reads=[bG.buf], writes=[sil.buf], label="silu")
            S.op("dve", lambda e: e.tensor_copy(out=upe.ap[:, 0:cs],
                                                in_=bU.ap[:, 0:cs]),
                 reads=[bU.buf], writes=[upe.buf], label="upe")
            S.op("dve", lambda e: e.tensor_tensor(
                     out=hbuf[:, fm, :], in0=sil.ap[:, 0:cs],
                     in1=upe.ap[:, 0:cs], op=OP.mult),
                 reads=[sil.buf, upe.buf], writes=[hbuf.buf], label="hb")
        for fm in range(HT):
            emit_h(fm)
        hb_tiles = [(hbuf.buf, hbuf[:, k, :]) for k in range(HT)]
        def emit_down(m):
            bank = ctx.banks[4 + (m % 2)]
            mm_group(ctx, bank, wdn, (m * 128, (m + 1) * 128), hb_tiles, cs,
                     f"dD{m}")
            res = res_rot[m % 4]
            S.op("act", lambda e: e.activation(out=res.ap[:, 0:cs],
                                               in_=bank.ap[:, 0:cs],
                                               func=AF.Copy),
                 reads=[bank.buf], writes=[res.buf], label="res")
            if first:
                S.dma("pool", lambda e: e.dma_start(
                          out=acc_ap[m, :, c * cs:(c + 1) * cs],
                          in_=res.ap[:, 0:cs]),
                      reads=[res.buf], writes=[acc_buf], label=f"stacc{c}_{m}")
            else:
                S.dma("pool", lambda e: e.dma_start(
                          out=acc_ap[m, :, c * cs:(c + 1) * cs],
                          in_=res.ap[:, 0:cs], accum_op=OP.add),
                      reads=[res.buf], writes=[acc_buf], label=f"acacc{c}_{m}")
        for m in range(DT):
            emit_down(m)
    for c in range(nch):
        emit_chunk(c)



def emit_phase_df(ctx, cs=512):
    S = ctx.S
    nch = T // cs
    xr_ap, xr_buf = ctx.dram["XR"]
    acc_ap, acc_buf = ctx.dram["ACC"]
    y_ap, y_buf = ctx.dram["Y"]

    A = ctx.act
    xt = [A.carve([128, DT, cs], F32, "xt") for _ in range(2)]
    acct = [A.carve([128, DT, cs], F32, "acct")]
    yt = [A.carve([128, DT, cs], F32, "yt")]
    xbf_rot = [A.carve([128, cs], BF16, "xbf") for _ in range(2)]
    sq_rot = [A.carve([128, cs], BF16, "sqx") for _ in range(2)]
    rows = ctx.rows
    mu_b = [A.carve([128, cs], F32, "mu_b") for _ in range(2)]
    rstd_b = [A.carve([128, cs], F32, "rstd_b") for _ in range(2)]
    mu2_b = [A.carve([128, cs], F32, "mu2_b") for _ in range(2)]
    rstd2_b = [A.carve([128, cs], F32, "rstd2_b") for _ in range(2)]
    tmp_rot = [A.carve([128, cs], F32, "tmp") for _ in range(2)]
    u_rot = [A.carve([128, cs], F32, "u") for _ in range(2)]
    t_rot = [A.carve([128, cs], F32, "t") for _ in range(2)]
    xo_rot = [A.carve([128, cs], F32, "xo") for _ in range(3)]

    def emit_chunk(c):
        par = c % 2
        x_t, acc_t, y_t = xt[par], acct[0], yt[0]
        S.dma("sp", lambda e: e.dma_start(
                  out=x_t.full,
                  in_=xr_ap[:, :, c * cs:(c + 1) * cs].rearrange("t p n -> p t n")),
              reads=[xr_buf], writes=[x_t.buf], label=f"ldx{c}")
        S.dma("sp", lambda e: e.dma_start(
                  out=acc_t.full,
                  in_=acc_ap[:, :, c * cs:(c + 1) * cs].rearrange("t p n -> p t n")),
              reads=[acc_buf], writes=[acc_t.buf], label=f"ldacc{c}")
        xt_par = [(x_t.buf, x_t[:, m, :]) for m in range(DT)]
        emit_norm(ctx, xt_par, cs, xbf_rot, sq_rot, rows, "ln", par,
                  {"mu_b": mu_b, "rstd_b": rstd_b})
        def emit_y(m):
            tmp, u = tmp_rot[m % 2], u_rot[m % 2]
            S.op("dve", lambda e: e.tensor_tensor(
                     out=tmp.ap[:, 0:cs], in0=x_t[:, m, :],
                     in1=mu_b[par].ap[:, 0:cs], op=OP.subtract),
                 reads=[x_t.buf, mu_b[par].buf], writes=[tmp.buf], label="tmp")
            S.op("dve", lambda e: e.tensor_tensor(
                     out=u.ap[:, 0:cs], in0=tmp.ap[:, 0:cs],
                     in1=rstd_b[par].ap[:, 0:cs], op=OP.mult),
                 reads=[tmp.buf, rstd_b[par].buf], writes=[u.buf], label="u")
            S.op("dve", lambda e: e.tensor_tensor(
                     out=y_t[:, m, :], in0=u.ap[:, 0:cs], in1=acc_t[:, m, :],
                     op=OP.add),
                 reads=[u.buf, acc_t.buf], writes=[y_t.buf], label="y")
        for m in range(DT):
            emit_y(m)
        yt_par = [(y_t.buf, y_t[:, m, :]) for m in range(DT)]
        emit_norm(ctx, yt_par, cs, xbf_rot, sq_rot, rows, "ln", par,
                  {"mu_b": mu2_b, "rstd_b": rstd2_b})
        def emit_xo(m):
            tmp, t_, xo = tmp_rot[m % 2], t_rot[m % 2], xo_rot[m % 3]
            S.op("dve", lambda e: e.tensor_tensor(
                     out=tmp.ap[:, 0:cs], in0=y_t[:, m, :],
                     in1=mu2_b[par].ap[:, 0:cs], op=OP.subtract),
                 reads=[y_t.buf, mu2_b[par].buf], writes=[tmp.buf], label="ytmp")
            S.op("dve", lambda e: e.tensor_tensor(
                     out=t_.ap[:, 0:cs], in0=tmp.ap[:, 0:cs],
                     in1=rstd2_b[par].ap[:, 0:cs], op=OP.mult),
                 reads=[tmp.buf, rstd2_b[par].buf], writes=[t_.buf], label="yln")
            S.op("dve", lambda e: e.tensor_tensor(
                     out=xo.ap[:, 0:cs], in0=t_.ap[:, 0:cs], in1=x_t[:, m, :],
                     op=OP.add),
                 reads=[t_.buf, x_t.buf], writes=[xo.buf], label="xo")
            S.dma("sp", lambda e: e.dma_start(
                      out=y_ap[m, :, c * cs:(c + 1) * cs], in_=xo.ap[:, 0:cs]),
                  reads=[xo.buf], writes=[y_buf], label=f"st_y{c}_{m}")
        for m in range(DT):
            emit_xo(m)
    for c in range(nch):
        emit_chunk(c)




def fm(x2, t):
    return np.ascontiguousarray(x2.T.reshape(8, 128, t))


def wpack(wt):
    kdim, mdim = wt.shape
    return np.ascontiguousarray(wt.reshape(kdim // 128, 128, mdim)
                                .transpose(1, 0, 2))


def build_inmap(p, xcore, t):
    A = lambda k: np.asarray(p[k])
    o = {
        "XIN": fm(xcore, t),
        "IDEN": np.eye(128, dtype=np.float32),
        "WGT": wpack(np.ascontiguousarray(A("L0_attn_Wg").T)),
        "WVT": wpack(np.ascontiguousarray(A("L0_attn_Wv").T)),
        "WDT": wpack(np.ascontiguousarray(A("L0_attn_Wd").T)),
        "BG": A("L0_attn_bg").reshape(8, 128),
        "BV": A("L0_attn_bv").reshape(8, 128),
        "BD": A("L0_attn_bd").reshape(8, 128),
        "RMSMIXW": A("L0_attn_rms_mix_w").reshape(8, 128),
        "RMSFFNW": A("L0_attn_rms_ffn_w").reshape(8, 128),
        "WGATET": wpack(np.ascontiguousarray(A("L0_attn_W_gate").T)),
        "WUPT": wpack(np.ascontiguousarray(A("L0_attn_W_up").T)),
        "WOUTT": wpack(np.ascontiguousarray(A("L0_attn_W_out").T)),
        "RWTB": wpack(np.ascontiguousarray(A("L0_ffn_router_w").T)),
        "RBB": A("L0_ffn_router_bias").reshape(1, 16),
        "SHGT": wpack(np.ascontiguousarray(A("L0_ffn_sh_gate").T)),
        "SHUT": wpack(np.ascontiguousarray(A("L0_ffn_sh_up").T)),
        "SHDT": wpack(np.ascontiguousarray(A("L0_ffn_sh_down").T)),
        "WIXCT": wpack(np.ascontiguousarray(A("L1_attn_W_in")[2048:].T)),
        "WINGT": wpack(np.ascontiguousarray(A("L1_attn_W_in")[:2048].T)),
        "CONVB": A("L1_attn_conv_b").reshape(16, 128),
        "WGATESFT": wpack(np.ascontiguousarray(A("L1_attn_W_gates")[:2048].T)),
        "WGATESIT": wpack(np.ascontiguousarray(A("L1_attn_W_gates")[2048:].T)),
        "BGF": A("L1_attn_b_gates")[:2048].reshape(16, 128),
        "BGI": A("L1_attn_b_gates")[2048:].reshape(16, 128),
        "FB": A("L1_attn_forget_base").reshape(16, 128),
        "WOUTHT": wpack(np.ascontiguousarray(A("L1_attn_W_out").T)),
        "GAM1": A("L1_attn_gamma1").reshape(8, 128),
        "GAM2": A("L1_attn_gamma2").reshape(8, 128),
        "WGROWT": wpack(np.ascontiguousarray(A("L1_attn_W_grow").T)),
        "WSHRT": wpack(np.ascontiguousarray(A("L1_attn_W_shrink").T)),
        "LNG": A("L1_ffn_ln_gamma").reshape(8, 128),
        "LNB": A("L1_ffn_ln_beta").reshape(8, 128),
        "RWTD": wpack(np.ascontiguousarray(A("L1_ffn_router_w").T)),
    }
    up = A("L0_ffn_up_proj")
    gp = A("L0_ffn_gate_proj")
    dp = A("L0_ffn_down_proj")
    for e in range(16):
        o[f"BGATE{e}"] = wpack(gp[e])
        o[f"BUP{e}"] = wpack(up[e])
        o[f"BDOWN{e}"] = wpack(dp[e])
    cw = A("L1_attn_conv_w")[:, 0, :]
    for k in range(4):
        o[f"CONVW{k}"] = np.ascontiguousarray(cw[:, k].reshape(16, 128))
    gu = A("L1_ffn_gate_up_w")
    dw = A("L1_ffn_down_w")
    for e in range(8):
        o[f"DGUT{e}"] = wpack(np.ascontiguousarray(gu[e].T))
        o[f"DDT{e}"] = wpack(np.ascontiguousarray(dw[e].T))
    return o


_PROG_CACHE = {}


def _build_program():
    if "nc" in _PROG_CACHE:
        return _PROG_CACHE["nc"]
    set_geometry(4096, 2048)
    nc = bass.Bass("TRN2", target_bir_lowering=False)
    ctx = Ctx(nc)
    ctx.dram_in("XIN", [8, 128, T])
    ctx.dram_out("Y", [8, 128, T])
    for name, shape, dt in (
            ("U2", [8, 128, T], F32), ("GU", [24, 128, T], BF16),
            ("XR", [8, 128, T], F32), ("UB", [8, 128, T], BF16),
            ("ACC", [8, 128, T], F32), ("COMB", [16, T], BF16),
            ("XC", [16, 128, T], BF16), ("AT", [16, 128, T], F32),
            ("H", [16, 128, T], BF16), ("XN", [8, 128, T], BF16),
            ("DGC", [8, T], BF16)):
        ctx.dram_scratch(name, shape, dt)
    phases = [
        emit_phase_a1, emit_phase_a2a, emit_phase_a2b,
        emit_phase_b0,
        lambda c: emit_phase_bexp(c, [0, 1, 2, 3], True),
        lambda c: emit_phase_bexp(c, [4, 5, 6, 7], False),
        lambda c: emit_phase_bexp(c, [8, 9, 10, 11], False),
        lambda c: emit_phase_bexp(c, [12, 13, 14, 15], False),
        emit_phase_b5,
        emit_phase_c0, emit_phase_c1f, emit_phase_c1i,
        emit_phase_c2a, emit_phase_c2b,
        emit_phase_d0,
    ] + [(lambda e: lambda c: emit_phase_dexp(c, e, e == 0))(e)
         for e in range(8)] + [emit_phase_df]
    emit_weight_prep(ctx)
    for fn in phases:
        fn(ctx)
        ctx.big.reset()
    ctx.S.finalize()
    _PROG_CACHE["nc"] = nc
    return nc


def kernel(**inputs):
    nc = _build_program()
    x = np.asarray(inputs["L0_attn_x"])
    in_maps = []
    for core in range(8):
        xc = np.ascontiguousarray(x[2 * core: 2 * core + 2]).reshape(T, D)
        in_maps.append(build_inmap(inputs, xc, T))
    res = run_bass_kernel_spmd(nc, in_maps, core_ids=list(range(8)))
    outs = []
    for core in range(8):
        yf = np.asarray(res.results[core]["Y"])
        outs.append(yf.reshape(D, T).T.reshape(2, 2048, D))
    return np.concatenate(outs, axis=0)
